# version 30
# baseline (speedup 1.0000x reference)
import contextlib
import numpy as np
import concourse.bass as bass
import concourse.mybir as mybir
from concourse.bass_utils import run_bass_kernel_spmd
from concourse.aot_env import allow_ici_all_to_all

F32 = mybir.dt.float32
BF16 = mybir.dt.bfloat16
U8 = mybir.dt.uint8
AF = mybir.ActivationFunctionType
ALU = mybir.AluOpType
AX = mybir.AxisListType

NCORES = 8
B, S, D = 2, 4096, 2048
TPC = 1024
TT = 512
CH = 128
NTT = S // TT
DK = DV = 128
MLP = 16384
NSL = MLP // NCORES
LN_EPS = 1e-5
RMS_EPS = 1e-6
L2_EPS = 1e-6
MASK_NEG = -60000.0

ENGS = ("sync", "tensor", "vector", "scalar", "gpsimd")


class Sem:
    __slots__ = ("name", "n", "h")

    def __init__(self, name):
        self.name, self.n, self.h = name, 0, None


class Plan:
    def __init__(self):
        self.steps = {e: [] for e in ENGS}
        self.sems = []

    def sem(self, name):
        s = Sem(name)
        self.sems.append(s)
        return s

    def op(self, eng, fn, waits=(), inc=None, k=1):
        for w_ in waits:
            if w_ is not None and (not isinstance(w_, tuple) or len(w_) != 2):
                raise ValueError(f"bad wait item: {w_!r}")
        ws = tuple((s, v) for w_ in waits if w_ is not None for (s, v) in [w_])
        if inc is not None:
            inc.n += (1 if k is None else k)
        self.steps[eng].append((fn, ws, (inc, k) if inc is not None else None))
        return (inc, inc.n) if inc is not None else None


class Bank:
    __slots__ = ("t", "free")

    def __init__(self, t):
        self.t = t
        self.free = []


def emit(nc, plan, stack):
    for s in plan.sems:
        s.h = stack.enter_context(nc.semaphore(s.name))
    block = stack.enter_context(nc.Block())
    for eng in ENGS:
        steps = plan.steps[eng]

        def body(e, steps=steps):
            for fn, ws, inc in steps:
                for (sm, val) in ws:
                    e.wait_ge(sm.h, val)
                inst = fn(e)
                if inc is not None:
                    if inc[1] is None:
                        inst.then_inc(inc[0].h)
                    else:
                        inst.then_inc(inc[0].h, inc[1])
        getattr(block, eng)(body)


class Arena:

    def __init__(self, t, nbytes):
        self.t = t
        self.nbytes = nbytes
        self.off = 0

    def carve(self, shape, dtype, name=""):
        p, f = shape
        esz = {F32: 4, BF16: 2, U8: 1, mybir.dt.int32: 4, mybir.dt.uint16: 2}[dtype]
        nb = f * esz
        nb_al = (nb + 31) // 32 * 32
        assert self.off + nb_al <= self.nbytes, \
            f"arena overflow at {name}: {self.off}+{nb_al} > {self.nbytes}"
        ap = self.t[:, self.off:self.off + nb].bitcast(dtype)
        self.off += nb_al
        if p < 128:
            ap = ap[0:p, :]
        return ap

    def mark(self):
        return self.off

    def reset(self, mark):
        self.off = mark


def build_program(debug=()):
    import os
    dbgflag = os.environ.get("KDBG", "0") == "1"
    nc = bass.Bass("TRN2", target_bir_lowering=False, debug=dbgflag)
    dram = {}

    def din(name, shape, dtype):
        dram[name] = nc.dram_tensor(name, shape, dtype, kind="ExternalInput").ap()
        return dram[name]

    def dout(name, shape, dtype):
        dram[name] = nc.dram_tensor(name, shape, dtype, kind="ExternalOutput").ap()
        return dram[name]

    def dint(name, shape, dtype):
        dram[name] = nc.dram_tensor(name, shape, dtype).ap()
        return dram[name]

    x_sl = din("x_sl", [TPC, D], F32)
    wq = din("wq", [D, 1536], F32)
    wba = din("wba", [D, 8], F32)
    convw = din("convw", [1024, 4], F32)
    dtba = din("dtba", [2, 4], F32)
    normw = din("normw", [DV], F32)
    wout = din("wout", [4096, D], F32)
    pkg = din("pkg", [1024, NSL], U8)
    pku = din("pku", [1024, NSL], U8)
    pkd = din("pkd", [1024, D], U8)
    scg = din("scg", [64, NSL], F32)
    scu = din("scu", [64, NSL], F32)
    scd = din("scd", [64, D], F32)
    identb = din("identb", [128, 128], BF16)
    maskS = din("maskS", [128, 128], F32)
    maskI = din("maskI", [128, 128], F32)
    out = dout("out", [TPC, D], F32)

    ag0_in = dint("ag0_in", [D, TPC], BF16)
    ag0_out = dint("ag0_out", [NCORES * D, TPC], BF16)
    agw_in = {m: dint(f"agw_{m}_in", [NSL, D], BF16) for m in "gud"}
    agw_out = {m: dint(f"agw_{m}_out", [MLP, D], BF16) for m in "gud"}
    a2a_in = dint("a2a_in", [4096, TPC], BF16)
    a2a_out = dint("a2a_out", [4096, TPC], BF16)
    x2_sp = dint("x2_sp", [TPC, D], F32)
    dbg = {}
    for name in debug:
        dbg[name] = dout("dbg_" + name, DEBUG_SHAPES[name][0], DEBUG_SHAPES[name][1])

    p = Plan()
    stack = contextlib.ExitStack()
    with stack, allow_ici_all_to_all():
        build_body(nc, p, stack, dram, dbg)
        emit(nc, p, stack)
    return nc


DEBUG_SHAPES = {
    "lnx": ([NCORES * D, TPC], BF16),
    "lnxin": ([D, TPC], BF16),
    "convq": ([128, S], F32),
    "convk": ([128, S], F32),
    "convv": ([128, S], F32),
    "gbeta": ([8, S], F32),
    "yloc": ([512, TPC], F32),
    "x2": ([TPC, D], F32),
    "wdeq": ([NSL, D], F32),
    "inter": ([128, 512], F32),
}


def build_body(nc, p, stack, dram, dbg):
    import os
    STAGE = int(os.environ.get("KSTAGE", "3"))
    DEQ_ON = os.environ.get("KDEQ", "1") == "1"
    SCAN_ON = os.environ.get("KSCAN", "1") == "1"
    GATE_ON = os.environ.get("KGATE", "1") == "1"
    DEQN = int(os.environ.get("KDEQN", "192"))
    SCPART = int(os.environ.get("KSCPART", "3"))
    SC2 = int(os.environ.get("KSC2", "9"))
    DEQIL = os.environ.get("KDEQIL", "1") == "1"
    SC3 = os.environ.get("KSC3", "abc")
    NTTL = int(os.environ.get("KNTT", "99"))
    sb = stack.enter_context(nc.sbuf_tensor("arena", [128, 210944], U8))
    A = Arena(sb[:], 210944)
    banks = [Bank(stack.enter_context(nc.psum_tensor(f"pb{i}", [128, 512], F32)))
             for i in range(8)]

    pT, pV, pA = p.sem("pT"), p.sem("pV"), p.sem("pA")
    d_x = p.sem("d_x")
    d_misc = p.sem("d_misc")
    d_wq = p.sem("d_wq")
    d_lnx = [p.sem("d_lnx0"), p.sem("d_lnx1")]
    d_pk = p.sem("d_pk")
    d_sc = [p.sem("d_sc0"), p.sem("d_sc1")]
    d_ag0st = p.sem("d_ag0st")
    d_wst = [p.sem("d_wst0"), p.sem("d_wst1")]
    d_yst = [p.sem("d_yst0"), p.sem("d_yst1")]
    d_wo = p.sem("d_wo")
    d_yt = p.sem("d_yt")
    d_x2st = p.sem("d_x2st")
    d_wgl = [p.sem(f"d_wgl{i}") for i in range(4)]
    d_wul = [p.sem(f"d_wul{i}") for i in range(4)]
    d_wd = [p.sem(f"d_wd{i}") for i in range(4)]
    d_x2l = p.sem("d_x2l")
    d_outst = p.sem("d_outst")
    d_dbg = p.sem("d_dbg")
    c_ag0 = p.sem("c_ag0")
    c_w = {m: p.sem(f"c_{m}") for m in "gud"}
    c_a2a = p.sem("c_a2a")

    def bank_acquire(i):
        return banks[i].free

    def bank_release(i, milestone):
        banks[i].free = milestone if isinstance(milestone, list) else [milestone]

    ev_tog = [0]

    def evac(dst, src, waits, eng=None):
        if eng is None:
            ev_tog[0] ^= 1
            eng = "vector" if ev_tog[0] else "scalar"
        if eng == "vector":
            return p.op("vector", lambda e, d=dst, s=src: e.tensor_copy(d, s),
                        waits=waits, inc=pV)
        return p.op("scalar", lambda e, d=dst, s=src: e.copy(d, s),
                    waits=waits, inc=pA)

    CONSTS = [-2.0, 1.0, -6.0, 0.5, -0.5, -1.0, LN_EPS, RMS_EPS, L2_EPS,
              1.0 / 128.0, 1.0 / 2048.0, DK ** -0.5, 2048.0]
    cst = A.carve([128, len(CONSTS)], F32, "cst")
    for i, v in enumerate(CONSTS):
        nc.const_aps.register(F32, float(v), cst[:, i:i + 1])
    onesb = A.carve([128, 128], BF16, "onesb")
    onesf = A.carve([128, 128], F32, "onesf")
    identb_sb = A.carve([128, 128], BF16, "identb")
    identf = A.carve([8, 8], F32, "identf")
    maskS_sb = A.carve([128, 128], F32, "maskS")
    maskI_sb = A.carve([128, 128], F32, "maskI")
    normw_sb = A.carve([128, 1], F32, "normw")
    dtcol = A.carve([8, 2], F32, "dtcol")
    convw_sb = A.carve([128, 32], F32, "convw")

    for i, v in enumerate(CONSTS):
        p.op("vector", lambda e, i=i, v=v: e.memset(cst[:, i:i + 1], float(v)))
    p.op("vector", lambda e: e.memset(onesb[:], 1.0))
    cini = p.op("vector", lambda e: e.memset(onesf[:], 1.0), inc=pV)
    p.op("sync", lambda e: e.dma_start(out=identb_sb, in_=dram["identb"]), inc=d_misc, k=16)
    IDF_MS = None
    p.op("sync", lambda e: e.dma_start(out=maskS_sb, in_=dram["maskS"]), inc=d_misc, k=16)
    p.op("sync", lambda e: e.dma_start(out=maskI_sb, in_=dram["maskI"]), inc=d_misc, k=16)
    p.op("sync", lambda e: e.dma_start(out=normw_sb, in_=dram["normw"].unsqueeze(-1)),
         inc=d_misc, k=16)
    p.op("sync", lambda e: e.dma_start(
         out=convw_sb[:].rearrange("p (g t) -> p g t", t=4),
         in_=dram["convw"].rearrange("(g p) t -> p g t", p=128)), inc=d_misc, k=16)
    mz = p.op("vector", lambda e: e.memset(dtcol[0:4, :], 0.0), inc=pV)
    p.op("sync", lambda e: e.dma_start(out=dtcol[4:8, 0:1],
         in_=dram["dtba"][0:1, :].rearrange("a b -> b a")), inc=d_misc, k=16)
    p.op("sync", lambda e: e.dma_start(out=dtcol[4:8, 1:2],
         in_=dram["dtba"][1:2, :].rearrange("a b -> b a")), inc=d_misc, k=16)
    MISC_ALL = (d_misc, d_misc.n)
    IDF_MS = p.op("vector", lambda e: e.tensor_copy(identf, identb_sb[0:8, 0:8]),
                  waits=[MISC_ALL], inc=pV)
    m = p.op("scalar", lambda e: e.activation(dtcol[:, 1:2], dtcol[:, 1:2], AF.Exp),
             waits=[MISC_ALL, mz], inc=pA)
    m = p.op("vector", lambda e: e.tensor_scalar(dtcol[:, 1:2], dtcol[:, 1:2],
             -1.0, None, ALU.mult), waits=[m], inc=pV)
    NEGA_READY = m

    resmark = A.mark()
    wq_sb = A.carve([128, 16 * 1536], BF16, "wq")
    wba_sb = A.carve([128, 16 * 8], BF16, "wba")
    for fc in range(16):
        p.op("gpsimd", lambda e, fc=fc: e.dma_start(
            out=wq_sb[:, 1536 * fc:1536 * (fc + 1)],
            in_=dram["wq"][128 * fc:128 * (fc + 1), :]), inc=d_wq, k=16)
        p.op("gpsimd", lambda e, fc=fc: e.dma_start(
            out=wba_sb[:, 8 * fc:8 * (fc + 1)],
            in_=dram["wba"][128 * fc:128 * (fc + 1), :]), inc=d_wq, k=16)
    WQ_ALL = (d_wq, d_wq.n)

    ph0_mark = A.mark()
    xt = A.carve([128, D], F32, "xt")
    sq0 = A.carve([128, D], F32, "sq0")
    st0 = A.carve([128, 8], F32, "st0")
    lnb = A.carve([128, D], BF16, "lnb")
    tstage = A.carve([128, 16 * 128], BF16, "tstage")
    xt_free = None
    lnb_free = None
    tstage_free = None
    p0_m3 = None
    p0_m5 = None
    for i in range(8):
        ld = p.op("sync", lambda e, i=i: e.dma_start(
            out=xt, in_=dram["x_sl"][128 * i:128 * (i + 1), :]),
            waits=[xt_free] if xt_free else [], inc=d_x, k=16)
        w = [(d_x, d_x.n)]
        m1 = p.op("vector", lambda e: e.tensor_reduce(st0[:, 0:1], xt, AX.X, ALU.add),
                  waits=w + [p0_m5], inc=pV)
        m2_ = p.op("scalar", lambda e: e.activation(sq0, xt, AF.Square),
                   waits=w + [p0_m3], inc=pA)
        m3 = p.op("vector", lambda e: e.tensor_reduce(st0[:, 1:2], sq0, AX.X, ALU.add),
                  waits=[m2_], inc=pV)
        p0_m3 = m3
        mm1 = p.op("vector", lambda e: e.tensor_scalar(st0[:, 2:3], st0[:, 0:1], 1.0 / 2048.0, None, ALU.mult),
                   waits=[m1], inc=pV)
        mm2 = p.op("vector", lambda e: e.tensor_scalar(st0[:, 3:4], st0[:, 1:2], 1.0 / 2048.0, None, ALU.mult),
                   waits=[m3], inc=pV)
        mm3 = p.op("vector", lambda e: e.tensor_tensor(st0[:, 4:5], st0[:, 2:3], st0[:, 2:3], ALU.mult),
                   waits=[mm1], inc=pV)
        m4 = p.op("vector", lambda e: e.tensor_tensor(st0[:, 5:6], st0[:, 3:4], st0[:, 4:5], ALU.subtract),
                  waits=[mm2, mm3], inc=pV)
        m5 = p.op("scalar", lambda e: e.activation(st0[:, 6:7], st0[:, 5:6], AF.Ln, bias=LN_EPS),
                  waits=[m4], inc=pA)
        p0_m5 = m5
        m6 = p.op("scalar", lambda e: e.activation(st0[:, 7:8], st0[:, 6:7], AF.Exp, scale=-0.5),
                  waits=[m5], inc=pA)
        xt_free = p.op("vector", lambda e: e.tensor_scalar(
            lnb, xt, st0[:, 2:3], st0[:, 7:8], ALU.subtract, ALU.mult),
            waits=[m6, mm1] + ([lnb_free] if lnb_free else []), inc=pV)
        evms = []
        for c in range(16):
            bk = 2 + (c & 1)
            pw = [xt_free, MISC_ALL]
            pw.extend(banks[bk].free)
            if tstage_free:
                pw.append(tstage_free)
            mt = p.op("tensor", lambda e, c=c, bk=bk: e.transpose(
                banks[bk].t[:].bitcast(BF16)[:, 0:128], lnb[:, 128 * c:128 * (c + 1)], identb_sb),
                waits=pw, inc=pT)
            me = evac(tstage[:, 128 * c:128 * (c + 1)], banks[bk].t[:].bitcast(BF16)[:, 0:128], [mt])
            bank_release(bk, me)
            evms.append(me)
        lnb_free = (pT, pT.n)
        stm = p.op("gpsimd", lambda e, i=i: e.dma_start(
            out=dram["ag0_in"].rearrange("(c p) t -> p c t", p=128)[:, :, 128 * i:128 * (i + 1)],
            in_=tstage[:].rearrange("p (c q) -> p c q", q=128)),
            waits=evms, inc=d_ag0st, k=16)
        tstage_free = (d_ag0st, d_ag0st.n)
    if "lnxin" in dbg:
        p.op("gpsimd", lambda e: e.dma_start(out=dbg["lnxin"], in_=dram["ag0_in"]),
             waits=[(d_ag0st, d_ag0st.n)], inc=d_dbg, k=16)
    AG0STORES = (d_ag0st, d_ag0st.n)
    p.op("gpsimd", lambda e: e.collective_compute(
        "AllGather", ALU.bypass, replica_groups=[list(range(NCORES))],
        ins=[dram["ag0_in"].opt()], outs=[dram["ag0_out"].opt()]),
        waits=[AG0STORES], inc=c_ag0, k=None)
    AG0_DONE = (c_ag0, 1)
    if "lnx" in dbg:
        p.op("gpsimd", lambda e: e.dma_start(out=dbg["lnx"], in_=dram["ag0_out"]),
             waits=[AG0_DONE], inc=d_dbg, k=16)
    if STAGE < 1:
        p.op("gpsimd", lambda e: e.nop(), waits=[AG0_DONE, (d_dbg, d_dbg.n)])
        return

    A.reset(ph0_mark)
    KH = 512
    dq_pk = A.carve([128, 2048], U8, "dq_pk")
    dq_sc = [A.carve([128, KH], F32, f"dq_sc{i}") for i in range(2)]
    dq_m8 = A.carve([128, KH], U8, "dq_m8")
    dq_s8 = dq_m8
    dq_mf = [A.carve([128, KH], BF16, f"dq_mf{i}") for i in range(2)]
    dq_sg = [A.carve([128, KH], BF16, f"dq_sg{i}") for i in range(2)]
    dq_r1 = [A.carve([128, KH], BF16, f"dq_r1{i}") for i in range(2)]
    dq_r2 = [A.carve([128, KH], BF16, f"dq_r2{i}") for i in range(2)]
    dq_wf = [A.carve([128, KH], BF16, "dq_wf") for _ in range(2)]
    dq_state = {"pk_free": None, "sc_free": [None, None],
                "sc_ready": [None, None], "mf_free": [None, None],
                "sg_free": [None, None], "r1_free": [None, None],
                "r2_free": [None, None], "wf_free": [None, None],
                "wtog": 0, "utog": 0, "tile_loaded": None,
                "pk_last_read": None, "pend": None}

    PKMAP = {"g": "pkg", "u": "pku", "d": "pkd"}
    SCMAP = {"g": "scg", "u": "scu", "d": "scd"}

    def deq_A(m, ti, half, colh):
        ds = dq_state
        tog = ds["utog"]
        if half == 0 and colh == 0:
            w = [AG0STORES] + ([ds["pk_last_read"]] if ds["pk_last_read"] else [])
            ds["tile_loaded"] = p.op("sync", lambda e, m=m, ti=ti: e.dma_start(
                out=dq_pk, in_=dram[PKMAP[m]][128 * ti:128 * (ti + 1), :]),
                waits=w, inc=d_pk, k=16)
        if half == 0:
            ctog = colh % 2
            w = [ds["sc_free"][ctog]] if ds["sc_free"][ctog] else []
            for rr_ in range(8):
                p.op("sync", lambda e, rr_=rr_, m=m, ti=ti, colh=colh, ctog=ctog: e.dma_start(
                    out=dq_sc[ctog][16 * rr_:16 * (rr_ + 1), :],
                    in_=dram[SCMAP[m]][8 * ti + rr_:8 * ti + rr_ + 1,
                                       KH * colh:KH * (colh + 1)].broadcast_to([16, KH])),
                    waits=w if rr_ == 0 else [], inc=d_sc[ctog], k=16)
            ds["sc_ready"][ctog] = (d_sc[ctog], d_sc[ctog].n)
        pk = dq_pk[:, KH * colh:KH * (colh + 1)]
        w0 = [ds["tile_loaded"]]
        if ds["mf_free"][tog]:
            w0.append(ds["mf_free"][tog])
        if half == 0:
            mu8 = p.op("vector", lambda e, pk=pk: e.tensor_single_scalar(
                dq_m8, pk, 7, ALU.bitwise_and), waits=w0, inc=pV)
        else:
            mu8 = p.op("vector", lambda e, pk=pk: e.tensor_scalar(
                dq_m8, pk, 4, 7, ALU.logical_shift_right, ALU.bitwise_and),
                waits=w0, inc=pV)
        mfm = p.op("vector", lambda e, tog=tog: e.tensor_copy(dq_mf[tog], dq_m8),
                   waits=[mu8], inc=pV)
        if half == 0:
            ms8 = p.op("vector", lambda e, pk=pk: e.tensor_scalar(
                dq_s8, pk, 3, 1, ALU.logical_shift_right, ALU.bitwise_and),
                waits=[mfm], inc=pV)
        else:
            ms8 = p.op("vector", lambda e, pk=pk: e.tensor_scalar(
                dq_s8, pk, 7, 1, ALU.logical_shift_right, ALU.bitwise_and),
                waits=[mfm], inc=pV)
        ds["pk_last_read"] = ms8
        w = [ms8] + ([ds["sg_free"][tog]] if ds["sg_free"][tog] else [])
        mcs8 = p.op("vector", lambda e, tog=tog: e.tensor_copy(dq_sg[tog], dq_s8),
                    waits=w, inc=pV)
        msg2 = p.op("vector", lambda e, tog=tog: e.tensor_scalar(
            dq_sg[tog], dq_sg[tog], -2.0, 1.0, ALU.mult, ALU.add),
            waits=[mcs8], inc=pV)
        w = [mfm] + ([ds["r1_free"][tog]] if ds["r1_free"][tog] else [])
        r1m = p.op("scalar", lambda e, tog=tog: e.activation(
            dq_r1[tog], dq_mf[tog], AF.Relu, scale=0.5, bias=-2.0), waits=w, inc=pA)
        w = [ds["r2_free"][tog]] if ds["r2_free"][tog] else []
        r2m = p.op("scalar", lambda e, tog=tog: e.activation(
            dq_r2[tog], dq_mf[tog], AF.Relu, bias=-6.0), waits=w, inc=pA)
        ds["pend"] = (m, ti, half, colh, tog, r2m, msg2)
        ds["utog"] ^= 1

    def deq_B():
        ds = dq_state
        if ds["pend"] is None:
            return
        m, ti, half, colh, tog, r2m, msg2 = ds["pend"]
        ds["pend"] = None
        ctog = colh % 2
        mq1 = p.op("vector", lambda e, tog=tog: e.tensor_tensor(
            dq_r1[tog], dq_r1[tog], dq_r2[tog], ALU.add), waits=[r2m], inc=pV)
        ds["r2_free"][tog] = mq1
        mq2 = p.op("vector", lambda e, tog=tog: e.scalar_tensor_tensor(
            dq_r1[tog], dq_mf[tog], 0.5, dq_r1[tog], ALU.mult, ALU.add),
            waits=[mq1], inc=pV)
        ds["mf_free"][tog] = mq2
        mq3 = p.op("vector", lambda e, tog=tog: e.tensor_tensor(
            dq_r1[tog], dq_r1[tog], dq_sg[tog], ALU.mult),
            waits=[mq2, msg2], inc=pV)
        ds["sg_free"][tog] = mq3
        wt_ = ds["wtog"]
        wf = dq_wf[wt_]
        w = [mq3, ds["sc_ready"][ctog]]
        if ds["wf_free"][wt_]:
            w.append(ds["wf_free"][wt_])
        wfm = p.op("vector", lambda e, wf=wf, tog=tog, ctog=ctog: e.tensor_tensor(
            wf, dq_r1[tog], dq_sc[ctog], ALU.mult), waits=w, inc=pV)
        ds["r1_free"][tog] = wfm
        if half == 1:
            ds["sc_free"][ctog] = wfm
        row0 = 128 * (8 * half + ti) if m in "gu" else 256 * ti + 128 * half
        stm = p.op("gpsimd", lambda e, m=m, row0=row0, colh=colh, wf=wf: e.dma_start(
            out=dram[f"agw_{m}_in"][row0:row0 + 128, KH * colh:KH * (colh + 1)],
            in_=wf), waits=[wfm], inc=d_wst[wt_], k=16)
        ds["wf_free"][wt_] = (d_wst[wt_], d_wst[wt_].n)
        ds["wtog"] ^= 1

    def emit_deq_half(m, ti, half, colh):
        prev = dq_state["pend"]
        dq_state["pend"] = None
        deq_A(m, ti, half, colh)
        new = dq_state["pend"]
        dq_state["pend"] = prev
        deq_B()
        dq_state["pend"] = new

    DEQ_UNITS = [(m, ti, half, colh) for m in "gud" for ti in range(8)
                 for colh in range(4) for half in range(2)]
    AGW_AFTER = {}

    lnxb = [A.carve([128, 16 * 512], BF16, "lnx0"), A.carve([128, 16 * 512], BF16, "lnx1")]
    raw = [[A.carve([128, 515], BF16, f"raw{ct}_{bu}") for bu in range(1 if ct % 6 >= 4 else 2)]
           for ct in range(12)]
    ba_f = [A.carve([8, 512], F32, f"baf{bu}") for bu in range(2)]
    convo = [A.carve([128, 512], BF16, f"convo{cg}") for cg in range(8)]
    qn = [A.carve([128, 512], BF16, f"qn{kh}") for kh in range(2)]
    kn = [A.carve([128, 512], BF16, f"kn{kh}") for kh in range(2)]
    sqg2 = [A.carve([128, 512], BF16, f"sqg{i}") for i in range(2)]
    sqg = sqg2[0]
    rrow2 = [A.carve([1, 512], F32, f"rrow{i}") for i in range(2)]
    rrow = rrow2[0]
    bsig = A.carve([8, 512], F32, "bsig")
    spt = A.carve([8, 512], F32, "spt")
    g8 = A.carve([8, 512], F32, "g8")
    gcum = A.carve([8, 512], F32, "gcum")
    kkb = [[A.carve([128, 128], BF16, f"kkb{s}{kh}") for kh in range(2)] for s in range(2)]
    kqb = [[A.carve([128, 128], BF16, f"kqb{s}{kh}") for kh in range(2)] for s in range(2)]
    cb = [A.carve([128, 8], F32, f"cb{s}") for s in range(2)]
    cgc = [A.carve([128, 8], F32, f"cgc{s}") for s in range(2)]
    cex = [A.carve([128, 4], F32, f"cex{s}") for s in range(2)]
    cnb = [A.carve([128, 4], F32, f"cnb{s}") for s in range(2)]
    kcol4 = A.carve([128, 16], F32, "kcol4")
    dsc_sh = A.carve([128, 512], F32, "dsc_sh")
    d4_sh = A.carve([128, 512], F32, "d4_sh")
    dl4_sh = A.carve([128, 512], BF16, "dl4_sh")
    dsc_free = [None]
    vtm4_free = [None, None]
    d4_free = [None]
    dl4_free = [None]
    ktm4_free = [None, None]
    CS = []
    for s in range(2):
        d = dict(free=None)
        d["grow4"] = A.carve([1, 512], F32, f"grow4{s}")
        d["nbrow4"] = A.carve([1, 512], BF16, f"nbrow4{s}")
        d["erow4"] = A.carve([1, 512], BF16, f"erow4{s}")
        d["D4"] = d4_sh
        d["dsc"] = dsc_sh
        d["dL4"] = dl4_sh
        d["ktm4"] = A.carve([128, 512], BF16, f"ktm4_{s}")
        d["vtm4"] = A.carve([128, 512], BF16, f"vtm4_{s}")
        for nm in ("dS4", "dI4", "N4", "NL4", "Mk4", "A24", "B24",
                   "rhs4", "U4"):
            d[nm] = A.carve([128, 512], BF16, f"{nm}{s}")
        d["wt4"] = d["B24"]
        d["qe4"] = d["A24"]
        CS.append(d)
    ost = [A.carve([128, 512], BF16, f"ost{j}") for j in range(4)]
    xgt2 = [A.carve([128, 512], BF16, f"xgt{i}") for i in range(2)]
    zs = [A.carve([128, 512], BF16, f"zs{j}") for j in range(4)]
    yt = [A.carve([128, 512], BF16, f"yt{i}") for i in range(2)]
    Sf = A.carve([128, 1024], F32, "Sf")
    Sb = A.carve([128, 1024], BF16, "Sb")
    ph1_high = A.off

    mzs = p.op("vector", lambda e: e.memset(Sf, 0.0), inc=pV)
    mzb = p.op("vector", lambda e: e.memset(Sb, 0.0), inc=pV)
    st_ms = [mzb] * 8
    stf_ms = [mzs] * 8

    CONV_MAP = [(0, 0), (1, 2), (2, 4), (3, 5), (6, 1), (7, 3), (8, 6), (9, 7)]
    lnx_free = [None, None]
    lnx_ready = [None, None]
    raw_ready = [[None, None] for _ in range(12)]
    raw_free = [[None, None] for _ in range(12)]
    baf_free = [None, None]
    convo_ready = [None] * 8
    convo_free = [None] * 8
    qn_ms = [None, None]
    kn_ms = [None, None]
    qn_free = [None, None]
    kn_free = [None, None]
    yt_free = [None, None]
    ba_ms = None
    bsig_free = [None]
    gcum_free = [None]
    l2_sq_free = [None, None]
    l2_rr_free = [None, None]
    CSLOT = [{"free": None}, {"free": None}]
    kcol_free = [None, None]
    kkq_free = [None, None]
    ba_act_ms = None
    mg8_w = [None]
    ost_ms = [[None] * 4 for _ in range(4)]
    ost_free = [None] * 4
    zs_free = [None] * 4
    xgt_free = [None, None]
    chd = [0]
    deq_idx = [0]

    def tiny_bank():
        chd[0] += 1
        return 2 + (chd[0] & 1)

    deq_goal = [0]
    pend_scan = [None]

    def deq_tick(n):
        for _ in range(max(n, 0)):
            if deq_idx[0] < min(DEQN, deq_goal[0]):
                m_, ti_, half_, colh_ = DEQ_UNITS[deq_idx[0]]
                emit_deq_half(m_, ti_, half_, colh_)
                deq_idx[0] += 1

    for b in range(B):
        for tt in range(NTT):
            gti = NTT * b + tt
            if gti >= NTTL:
                continue
            bu = gti % 2
            r = gti // 2
            cb_ = (gti % 2) * 512
            def emit_lnx(g2):
                bu2 = g2 % 2
                r2 = g2 // 2
                cb2 = (g2 % 2) * 512
                w = [AG0_DONE] + ([lnx_free[bu2]] if lnx_free[bu2] else [])
                lnx_ready[bu2] = p.op("sync", lambda e, r2=r2, cb2=cb2, bu2=bu2: e.dma_start(
                    out=lnxb[bu2][:].rearrange("p (c q) -> p c q", q=512),
                    in_=dram["ag0_out"].rearrange("(r c p) t -> r p c t", r=8, p=128)[r2][:, :, cb2:cb2 + 512]),
                    waits=w, inc=d_lnx[bu2], k=16)
            if gti == 0:
                emit_lnx(0)
            for ct in range(13):
                bk = ct % 2
                if ct < 12:
                    kh, sub = ct // 6, ct % 6
                    coff = 768 * kh + 128 * sub
                    cw = 128
                    pv = banks[bk].t[:]
                else:
                    cw = 8
                    pv = banks[bk].t[0:8, :]
                for fc in range(16):
                    w = []
                    if fc == 0:
                        w = [lnx_ready[bu], WQ_ALL]
                        w.extend(banks[bk].free)
                    if ct < 12:
                        lh = wq_sb[:, 1536 * fc + coff:1536 * fc + coff + cw]
                    else:
                        lh = wba_sb[:, 8 * fc:8 * fc + 8]
                    inc = pT if fc == 15 else None
                    mm = p.op("tensor", lambda e, lh=lh, pv=pv, fc=fc, bu=bu: e.matmul(
                        pv, lh, lnxb[bu][:, 512 * fc:512 * (fc + 1)],
                        start=(fc == 0), stop=(fc == 15)), waits=w, inc=inc)
                if ct == 12:
                    lnx_free[bu] = mm
                if ct < 12:
                    rbu = 0 if sub >= 4 else bu
                    w = [mm]
                    if raw_free[ct][rbu]:
                        w.append(raw_free[ct][rbu])
                    if sub < 4:
                        if tt == 0:
                            p.op("vector", lambda e, ct=ct, bu=bu: e.memset(raw[ct][bu][:, 0:3], 0.0),
                                 waits=[raw_free[ct][bu]] if raw_free[ct][bu] else [])
                        else:
                            mh = p.op("vector", lambda e, ct=ct, bu=bu: e.tensor_copy(
                                raw[ct][bu][:, 0:3], raw[ct][1 - bu][:, 512:515]),
                                waits=[raw_ready[ct][1 - bu]] + ([raw_free[ct][bu]] if raw_free[ct][bu] else []),
                                inc=pV)
                            raw_free[ct][1 - bu] = mh
                    raw_ready[ct][rbu] = evac(raw[ct][rbu][:, 3:515], pv, w)
                    bank_release(bk, raw_ready[ct][rbu])
                else:
                    w = [mm]
                    if baf_free[bu]:
                        w.append(baf_free[bu])
                    bam = evac(ba_f[bu], pv, w)
                    bank_release(bk, bam)
            if gti + 1 < min(NTTL, 2 * NTT):
                emit_lnx(gti + 1)
            if pend_scan[0]:
                for _ in pend_scan[0]:
                    pass
                pend_scan[0] = None
            for (ct, cg) in CONV_MAP:
                src = raw[ct][bu]
                tmp = convo[cg]
                w = [raw_ready[ct][bu], MISC_ALL]
                if convo_free[cg]:
                    w.append(convo_free[cg])
                mt = p.op("vector", lambda e, src=src, tmp=tmp, cg=cg: e.tensor_scalar(
                    tmp, src[:, 0:512], convw_sb[:, 4 * cg:4 * cg + 1], None, ALU.mult),
                    waits=w, inc=pV)
                for tap in (1, 2, 3):
                    mt = p.op("vector", lambda e, src=src, tmp=tmp, cg=cg, tap=tap: e.scalar_tensor_tensor(
                        tmp, src[:, tap:tap + 512], convw_sb[:, 4 * cg + tap:4 * cg + tap + 1],
                        tmp, ALU.mult, ALU.add), waits=[mt], inc=pV)
                raw_free[ct][bu] = mt
                convo_ready[cg] = p.op("scalar", lambda e, tmp=tmp, cg=cg: e.activation(
                    convo[cg], tmp, AF.Silu), waits=[mt], inc=pA)
            zs_ms = [None] * 4
            for j in range(4):
                kh = j // 2
                ct_z = 6 * kh + 4 + (j % 2)
                w = [raw_ready[ct_z][0]] + ([zs_free[j]] if zs_free[j] else [])
                zs_ms[j] = p.op("scalar", lambda e, ct_z=ct_z, j=j: e.activation(
                    zs[j], raw[ct_z][0][:, 3:515], AF.Silu), waits=w, inc=pA)
                raw_free[ct_z][0] = zs_ms[j]
            L2C = [(0, qn, qn_ms, qn_free, 0), (2, kn, kn_ms, kn_free, 0),
                   (1, qn, qn_ms, qn_free, 1), (3, kn, kn_ms, kn_free, 1)]
            for pr in range(2):
                stl = [{} for _ in range(2)]
                for u in range(2):
                    cg, dstl, msl, freel, kh = L2C[2 * pr + u]
                    tg = u
                    stl[u]["msq"] = p.op("vector", lambda e, cg=cg, tg=tg: e.tensor_tensor(
                        sqg2[tg], convo[cg], convo[cg], ALU.mult),
                        waits=[convo_ready[cg], l2_sq_free[tg]], inc=pV)
                for u in range(2):
                    tg = u
                    bq = 6 + tg
                    stl[u]["mcs"] = p.op("tensor", lambda e, tg=tg, bq=bq: e.matmul(
                        banks[bq].t[0:1, :], onesb[:, 0:1], sqg2[tg], start=True, stop=True),
                        waits=[stl[u]["msq"], cini] + list(banks[bq].free), inc=pT)
                    l2_sq_free[tg] = stl[u]["mcs"]
                for u in range(2):
                    tg = u
                    bq = 6 + tg
                    mln = p.op("scalar", lambda e, tg=tg, bq=bq: e.activation(
                        rrow2[tg], banks[bq].t[0:1, :], AF.Ln, bias=L2_EPS),
                        waits=[stl[u]["mcs"], l2_rr_free[tg]], inc=pA)
                    bank_release(bq, mln)
                    stl[u]["mln"] = mln
                for u in range(2):
                    cg = L2C[2 * pr + u][0]
                    tg = u
                    mex = p.op("scalar", lambda e, tg=tg: e.activation(
                        rrow2[tg], rrow2[tg], AF.Exp, scale=-0.5),
                        waits=[stl[u]["mln"]], inc=pA)
                    if cg < 2:
                        mex = p.op("vector", lambda e, tg=tg: e.tensor_scalar(
                            rrow2[tg], rrow2[tg], DK ** -0.5, None, ALU.mult),
                            waits=[mex], inc=pV)
                    stl[u]["mex"] = mex
                for u in range(2):
                    tg = u
                    bq = 6 + tg
                    stl[u]["mbc"] = p.op("tensor", lambda e, tg=tg, bq=bq: e.matmul(
                        banks[bq].t[:], onesf[0:1, :], rrow2[tg], start=True, stop=True),
                        waits=[stl[u]["mex"]] + list(banks[bq].free), inc=pT)
                    l2_rr_free[tg] = stl[u]["mbc"]
                for u in range(2):
                    cg, dstl, msl, freel, kh = L2C[2 * pr + u]
                    tg = u
                    bq = 6 + tg
                    w2 = [stl[u]["mbc"]]
                    if freel[kh]:
                        w2.append(freel[kh])
                    msl[kh] = p.op("vector", lambda e, cg=cg, dstl=dstl, kh=kh, bq=bq: e.tensor_tensor(
                        dstl[kh], convo[cg], banks[bq].t[:], ALU.mult), waits=w2, inc=pV)
                    bank_release(bq, msl[kh])
                    convo_free[cg] = msl[kh]
            mbs = p.op("scalar", lambda e, bu=bu: e.activation(bsig, ba_f[bu], AF.Sigmoid),
                       waits=[bam] + ([bsig_free[0]] if bsig_free[0] else []), inc=pA)
            msp = p.op("vector", lambda e, bu=bu: e.tensor_scalar(
                spt, ba_f[bu], dtcol[:, 0:1], None, ALU.add),
                waits=[bam, MISC_ALL, ba_act_ms], inc=pV)
            baf_free[bu] = msp
            mb1 = p.op("scalar", lambda e: e.activation(g8, spt, AF.Abs),
                       waits=[msp, mg8_w[0]], inc=pA)
            mrt = p.op("scalar", lambda e: e.activation(spt, spt, AF.Relu), waits=[mb1], inc=pA)
            mb2 = p.op("scalar", lambda e: e.activation(g8, g8, AF.Exp, scale=-1.0),
                       waits=[mb1], inc=pA)
            mab = p.op("scalar", lambda e: e.activation(g8, g8, AF.Ln, bias=1.0),
                       waits=[mb2], inc=pA)
            ba_act_ms = mab
            msp2 = p.op("vector", lambda e: e.tensor_tensor(spt, spt, g8, ALU.add),
                        waits=[mab, mrt], inc=pV)
            mg8 = p.op("vector", lambda e: e.tensor_scalar(
                g8, spt, dtcol[:, 1:2], None, ALU.mult),
                waits=[NEGA_READY, mab, msp2], inc=pV)
            for c in range(4):
                ba_ms = p.op("vector", lambda e, c=c: e.tensor_tensor_scan(
                    gcum[:, 128 * c:128 * (c + 1)], onesf[0:8, 0:128],
                    g8[:, 128 * c:128 * (c + 1)], 0.0, ALU.mult, ALU.add),
                    waits=([gcum_free[0], mg8] if c == 0 else [mg8]), inc=pV)
            mg8_w[0] = ba_ms

            def do_scan(gti=gti, b=b, tt=tt, bu=bu, mbs=mbs,
                        ba_ms=ba_ms, zs_ms=zs_ms):
                if gti < 12 and DEQ_ON:
                    deq_goal[0] = 16 * (gti + 1)
                def sc_new(c):
                    st = {"c": c, "par": c & 1}
                    st["slc"] = c % 2
                    st["ssl"] = (gti * 4 + c) % 2
                    st["csl"] = CSLOT[st["slc"]]
                    st["cs"] = CS[c % 2]
                    st["csw"] = [st["cs"]["free"]] if st["cs"]["free"] else []
                    st["bkc"] = 2 + (c & 1)
                    st["bkb"] = 2 + ((c + 1) & 1)
                    st["kcol"] = kcol4[:, 8 * st["par"]:8 * st["par"] + 8]
                    return st

                def sc_early(st):
                    c, par, slc, ssl = st["c"], st["par"], st["slc"], st["ssl"]
                    cs, csl, csw, bkc, bkb = st["cs"], st["csl"], st["csw"], st["bkc"], st["bkb"]
                    bkt = par
                    w = [mbs, ba_ms] + ([csl["free"]] if csl["free"] else []) \
                        + list(banks[bkt].free)
                    mt1 = p.op("tensor", lambda e, c=c, bkt=bkt: e.transpose(
                        banks[bkt].t[:, 0:8], bsig[:, 128 * c:128 * (c + 1)], identf),
                        waits=w + [IDF_MS], inc=pT)
                    bsig_free[0] = mt1
                    mt2 = p.op("tensor", lambda e, c=c, bkt=bkt: e.transpose(
                        banks[bkt].t[:, 8:16], gcum[:, 128 * c:128 * (c + 1)], identf),
                        waits=[ba_ms], inc=pT)
                    gcum_free[0] = mt2
                    me1 = evac(cb[slc], banks[bkt].t[:, 0:8], [mt2], eng="scalar")
                    me2 = evac(cgc[slc], banks[bkt].t[:, 8:16], [mt2], eng="scalar")
                    bank_release(bkt, [me1, me2])
                    st["me1"], st["me2"] = me1, me2
                    st["mex_"] = p.op("scalar", lambda e, slc=slc: e.activation(
                        cex[slc], cgc[slc][:, 4:8], AF.Exp), waits=[me2], inc=pA)
                    mnb = p.op("vector", lambda e, slc=slc: e.tensor_scalar(
                        cnb[slc], cb[slc][:, 0:4], -1.0, None, ALU.mult), waits=[me1], inc=pV)
                    st["mnb"] = mnb
                    mge = None
                    for j in range(4):
                        wj = [ba_ms] + csw + (list(banks[bkc].free) if j == 0 else [])
                        mge = p.op("tensor", lambda e, j=j, c=c, bkc=bkc: e.matmul(
                            banks[bkc].t[0:1, 128 * j:128 * (j + 1)],
                            identf[:, 4 + j:5 + j], gcum[0:8, 128 * c:128 * (c + 1)],
                            start=True, stop=True), waits=wj, inc=pT)
                        gcum_free[0] = mge
                    mgre = evac(cs["grow4"], banks[bkc].t[0:1, 0:512], [mge] + csw)
                    bank_release(bkc, mgre)
                    st["mgre"] = mgre
                    mbe = None
                    for j in range(4):
                        wj = [mbs] + (list(banks[bkb].free) if j == 0 else [])
                        mbe = p.op("tensor", lambda e, j=j, c=c, bkb=bkb: e.matmul(
                            banks[bkb].t[0:1, 128 * j:128 * (j + 1)],
                            identf[:, j:j + 1], bsig[0:8, 128 * c:128 * (c + 1)],
                            start=True, stop=True), waits=wj, inc=pT)
                        bsig_free[0] = mbe
                    mnbr = p.op("vector", lambda e, cs=cs, bkb=bkb: e.tensor_scalar(
                        cs["nbrow4"], banks[bkb].t[0:1, 0:512], -1.0, None, ALU.mult),
                        waits=[mbe] + csw, inc=pV)
                    bank_release(bkb, mnbr)
                    st["mnbr"] = mnbr
                    st["mer"] = p.op("scalar", lambda e, cs=cs: e.activation(
                        cs["erow4"], cs["grow4"], AF.Exp), waits=[mgre] + csw, inc=pA)
                    kkms, kqms = [None, None], [None, None]
                    for kh in range(2):
                        bkk = 4 + 2 * par + kh
                        w = [kn_ms[kh], kkq_free[ssl]] + list(banks[bkk].free)
                        mk1 = p.op("tensor", lambda e, kh=kh, c=c, bkk=bkk: e.matmul(
                            banks[bkk].t[:, 0:128], kn[kh][:, 128 * c:128 * (c + 1)],
                            kn[kh][:, 128 * c:128 * (c + 1)], start=True, stop=True),
                            waits=w, inc=pT)
                        mk2 = p.op("tensor", lambda e, kh=kh, c=c, bkk=bkk: e.matmul(
                            banks[bkk].t[:, 128:256], kn[kh][:, 128 * c:128 * (c + 1)],
                            qn[kh][:, 128 * c:128 * (c + 1)], start=True, stop=True),
                            waits=[qn_ms[kh]], inc=pT)
                        engk = "vector" if kh == 0 else "scalar"
                        kkms[kh] = evac(kkb[ssl][kh], banks[bkk].t[:, 0:128], [mk2], eng=engk)
                        kqms[kh] = evac(kqb[ssl][kh], banks[bkk].t[:, 128:256], [mk2], eng=engk)
                        bank_release(bkk, [kkms[kh], kqms[kh]])
                    st["kkms"], st["kqms"] = kkms, kqms
                    mbc1 = p.op("tensor", lambda e, cs=cs, bkc=bkc: e.matmul(
                        banks[bkc].t[:], onesf[0:1, :], cs["grow4"], start=True, stop=True),
                        waits=[mgre] + list(banks[bkc].free), inc=pT)
                    gview = cgc[slc][:, 4:8].unsqueeze(-1).broadcast_to([128, 4, 128])
                    mD = p.op("vector", lambda e, cs=cs, gview=gview, bkc=bkc: e.tensor_tensor(
                        cs["D4"].rearrange("p (a b) -> p a b", a=4),
                        banks[bkc].t[:].rearrange("p (a b) -> p a b", a=4), gview, ALU.subtract),
                        waits=[mbc1, me2, d4_free[0]] + csw, inc=pV)
                    bank_release(bkc, mD)
                    mskSv = maskS_sb.unsqueeze(1).broadcast_to([128, 4, 128])
                    mskIv = maskI_sb.unsqueeze(1).broadcast_to([128, 4, 128])
                    mds_ = p.op("vector", lambda e, cs=cs, mskSv=mskSv: e.tensor_tensor(
                        cs["dsc"].rearrange("p (a b) -> p a b", a=4),
                        cs["D4"].rearrange("p (a b) -> p a b", a=4), mskSv, ALU.add),
                        waits=[MISC_ALL, mD, dsc_free[0]], inc=pV)
                    mdS = p.op("scalar", lambda e, cs=cs: e.activation(
                        cs["dS4"], cs["dsc"], AF.Exp), waits=[mds_], inc=pA)
                    mdi_ = p.op("vector", lambda e, cs=cs, mskIv=mskIv: e.tensor_tensor(
                        cs["dsc"].rearrange("p (a b) -> p a b", a=4),
                        cs["D4"].rearrange("p (a b) -> p a b", a=4), mskIv, ALU.add),
                        waits=[mdS], inc=pV)
                    mdI = p.op("scalar", lambda e, cs=cs: e.activation(
                        cs["dI4"], cs["dsc"], AF.Exp), waits=[mdi_], inc=pA)
                    dsc_free[0] = mdI
                    d4_free[0] = mdi_
                    st["mdS"], st["mdI"] = mdS, mdI
                    mtl = None
                    for j in range(4):
                        wj = [mdS] + (list(banks[bkb].free) if j == 0 else [])
                        mtl = p.op("tensor", lambda e, cs=cs, j=j, bkb=bkb: e.transpose(
                            banks[bkb].t[:].bitcast(BF16)[:, 128 * j:128 * (j + 1)],
                            cs["dS4"][:, 128 * j:128 * (j + 1)], identb_sb), waits=wj, inc=pT)
                    mdL = evac(cs["dL4"], banks[bkb].t[:].bitcast(BF16)[:, 0:512],
                               [mtl, dl4_free[0]])
                    bank_release(bkb, mdL)
                    mbb = p.op("tensor", lambda e, cs=cs, bkc=bkc: e.matmul(
                        banks[bkc].t[:], onesb[0:1, :], cs["nbrow4"], start=True, stop=True),
                        waits=[mnbr] + list(banks[bkc].free), inc=pT)
                    mN = None
                    for kh in range(2):
                        kkv = kkb[ssl][kh].unsqueeze(1).broadcast_to([128, 2, 128])
                        mN = p.op("vector", lambda e, cs=cs, kkv=kkv, kh=kh: e.tensor_tensor(
                            cs["N4"][:, 256 * kh:256 * (kh + 1)].rearrange("p (a b) -> p a b", a=2),
                            kkv, cs["dS4"][:, 256 * kh:256 * (kh + 1)].rearrange("p (a b) -> p a b", a=2),
                            ALU.mult), waits=[kkms[kh], mdS, mN], inc=pV)
                    mN = p.op("vector", lambda e, cs=cs, bkc=bkc: e.tensor_tensor(
                        cs["N4"], cs["N4"], banks[bkc].t[:], ALU.mult), waits=[mbb, mN], inc=pV)
                    bank_release(bkc, mN)
                    mNL = None
                    for kh in range(2):
                        kkv = kkb[ssl][kh].unsqueeze(1).broadcast_to([128, 2, 128])
                        mNL = p.op("vector", lambda e, cs=cs, kkv=kkv, kh=kh: e.tensor_tensor(
                            cs["NL4"][:, 256 * kh:256 * (kh + 1)].rearrange("p (a b) -> p a b", a=2),
                            kkv, cs["dL4"][:, 256 * kh:256 * (kh + 1)].rearrange("p (a b) -> p a b", a=2),
                            ALU.mult), waits=[kkms[kh], mdL, mNL], inc=pV)
                    nbv = cnb[slc].unsqueeze(-1).broadcast_to([128, 4, 128])
                    dl4_free[0] = mNL
                    mNL = p.op("vector", lambda e, cs=cs, nbv=nbv: e.tensor_tensor(
                        cs["NL4"].rearrange("p (a b) -> p a b", a=4),
                        cs["NL4"].rearrange("p (a b) -> p a b", a=4), nbv, ALU.mult),
                        waits=[st["mnb"], mNL], inc=pV)
                    idv = identb_sb.unsqueeze(1).broadcast_to([128, 4, 128])
                    mMk = p.op("vector", lambda e, cs=cs, idv=idv: e.tensor_tensor(
                        cs["Mk4"].rearrange("p (a b) -> p a b", a=4),
                        cs["N4"].rearrange("p (a b) -> p a b", a=4), idv, ALU.add),
                        waits=[mN, MISC_ALL], inc=pV)
                    st["Acur"], st["Bcur"] = cs["N4"], cs["NL4"]
                    st["Anew"], st["Bnew"] = cs["A24"], cs["B24"]
                    st["mAc"] = [mN] * 4
                    st["mBc"] = [mNL] * 4
                    st["mMkj"] = [mMk] * 4

                def sc_inv(st, lev, part):
                    cs, par = st["cs"], st["par"]
                    Acur, Bcur, Anew, Bnew = st["Acur"], st["Bcur"], st["Anew"], st["Bnew"]
                    mAc, mBc, mMkj = st["mAc"], st["mBc"], st["mMkj"]
                    if part == 0:
                        mAn = [None] * 4
                        if lev < 6:
                            for half in range(2):
                                bq = 4 + 2 * par + half
                                for jj in range(2):
                                    j = 2 * jj + half
                                    wj = [mAc[j], mBc[j]] + (list(banks[bq].free) if jj == 0 else [])
                                    mq = p.op("tensor", lambda e, Acur=Acur, Bcur=Bcur, j=j, jj=jj, bq=bq: e.matmul(
                                        banks[bq].t[:, 128 * jj:128 * (jj + 1)],
                                        Bcur[:, 128 * j:128 * (j + 1)], Acur[:, 128 * j:128 * (j + 1)],
                                        start=True, stop=True), waits=wj, inc=pT)
                                engb = "vector" if half == 0 else "scalar"
                                for jj in range(2):
                                    j = 2 * jj + half
                                    mAn[j] = evac(Anew[:, 128 * j:128 * (j + 1)],
                                                  banks[bq].t[:, 128 * jj:128 * (jj + 1)], [mq], eng=engb)
                                bank_release(bq, [mAn[half], mAn[2 + half]])
                        st["mAn_t"] = mAn
                        return
                    if part == 1:
                        mBn = [None] * 4
                        for half in range(2):
                            bq = 4 + 2 * par + half
                            for jj in range(2):
                                j = 2 * jj + half
                                wj = [mAc[j], mBc[j]] + (list(banks[bq].free) if jj == 0 else [])
                                mq = p.op("tensor", lambda e, Acur=Acur, Bcur=Bcur, j=j, jj=jj, bq=bq: e.matmul(
                                    banks[bq].t[:, 128 * jj:128 * (jj + 1)],
                                    Acur[:, 128 * j:128 * (j + 1)], Bcur[:, 128 * j:128 * (j + 1)],
                                    start=True, stop=True), waits=wj, inc=pT)
                            engb = "vector" if half == 0 else "scalar"
                            for jj in range(2):
                                j = 2 * jj + half
                                mBn[j] = evac(Bnew[:, 128 * j:128 * (j + 1)],
                                              banks[bq].t[:, 128 * jj:128 * (jj + 1)], [mq], eng=engb)
                            bank_release(bq, [mBn[half], mBn[2 + half]])
                        st["mBn_t"] = mBn
                        return
                    mAn = st["mAn_t"]
                    mBn = st["mBn_t"]
                    nMk = [None] * 4
                    for half in range(2):
                        bq = 4 + 2 * par + half
                        for jj in range(2):
                            j = 2 * jj + half
                            wj = [mBn[j], mMkj[j]] + (list(banks[bq].free) if jj == 0 else [])
                            p.op("tensor", lambda e, Bnew=Bnew, cs=cs, j=j, jj=jj, bq=bq: e.matmul(
                                banks[bq].t[:, 128 * jj:128 * (jj + 1)],
                                Bnew[:, 128 * j:128 * (j + 1)], cs["Mk4"][:, 128 * j:128 * (j + 1)],
                                start=True, stop=False), waits=wj)
                            mq = p.op("tensor", lambda e, cs=cs, j=j, jj=jj, bq=bq: e.matmul(
                                banks[bq].t[:, 128 * jj:128 * (jj + 1)],
                                identb_sb, cs["Mk4"][:, 128 * j:128 * (j + 1)],
                                start=False, stop=True), inc=pT)
                        engb = "vector" if half == 0 else "scalar"
                        for jj in range(2):
                            j = 2 * jj + half
                            nMk[j] = evac(cs["Mk4"][:, 128 * j:128 * (j + 1)],
                                          banks[bq].t[:, 128 * jj:128 * (jj + 1)], [mq], eng=engb)
                        bank_release(bq, [nMk[half], nMk[2 + half]])
                    st["mMkj"] = nMk
                    if lev < 6:
                        st["Acur"], st["Anew"] = Anew, Acur
                        st["mAc"] = st["mAn_t"]
                    st["Bcur"], st["Bnew"] = Bnew, Bcur
                    st["mBc"] = st["mBn_t"]

                def sc_T6(st):
                    c, par, slc, ssl = st["c"], st["par"], st["slc"], st["ssl"]
                    cs, csw, bkc, bkb = st["cs"], st["csw"], st["bkc"], st["bkb"]
                    kcol = st["kcol"]
                    mMkj = st["mMkj"]
                    ktmA4 = cs["NL4"]
                    ktmB4 = cs["ktm4"]
                    mvt = None
                    for j in range(4):
                        wj = [convo_ready[4 + j]] + (list(banks[bkb].free) if j == 0 else [])
                        mvt = p.op("tensor", lambda e, j=j, c=c, bkb=bkb: e.transpose(
                            banks[bkb].t[:].bitcast(BF16)[:, 128 * j:128 * (j + 1)],
                            convo[4 + j][:, 128 * c:128 * (c + 1)], identb_sb), waits=wj, inc=pT)
                        convo_free[4 + j] = mvt
                    cbv = cb[slc][:, 0:4].unsqueeze(-1).broadcast_to([128, 4, 128])
                    mbv = p.op("vector", lambda e, cs=cs, cbv=cbv, bkb=bkb: e.tensor_tensor(
                        cs["vtm4"].rearrange("p (a b) -> p a b", a=4),
                        banks[bkb].t[:].bitcast(BF16)[:, 0:512].rearrange("p (a b) -> p a b", a=4),
                        cbv, ALU.mult), waits=[mvt, st["me1"], vtm4_free[par]] + csw, inc=pV)
                    bank_release(bkb, mbv)
                    st["mbv"] = mbv
                    mwt = None
                    for kh in range(2):
                        kqv = kqb[ssl][kh].unsqueeze(1).broadcast_to([128, 2, 128])
                        mwt = p.op("vector", lambda e, cs=cs, kqv=kqv, kh=kh: e.tensor_tensor(
                            cs["wt4"][:, 256 * kh:256 * (kh + 1)].rearrange("p (a b) -> p a b", a=2),
                            kqv, cs["dI4"][:, 256 * kh:256 * (kh + 1)].rearrange("p (a b) -> p a b", a=2),
                            ALU.mult), waits=[st["kqms"][kh], st["mdI"], mwt] + mMkj, inc=pV)
                    st["mwt"] = mwt
                    mbe4 = p.op("tensor", lambda e, cs=cs, bkc=bkc: e.matmul(
                        banks[bkc].t[:], onesb[0:1, :], cs["erow4"], start=True, stop=True),
                        waits=[st["mer"]] + list(banks[bkc].free), inc=pT)
                    mqe = None
                    for kh in range(2):
                        qnv = qn[kh][:, 128 * c:128 * (c + 1)].unsqueeze(1).broadcast_to([128, 2, 128])
                        mqe = p.op("vector", lambda e, cs=cs, qnv=qnv, kh=kh, bkc=bkc: e.tensor_tensor(
                            cs["qe4"][:, 256 * kh:256 * (kh + 1)].rearrange("p (a b) -> p a b", a=2),
                            qnv, banks[bkc].t[:, 256 * kh:256 * (kh + 1)].rearrange("p (a b) -> p a b", a=2),
                            ALU.mult), waits=[mbe4, qn_ms[kh], mqe] + mMkj, inc=pV)
                    bank_release(bkc, mqe)
                    st["mqe"] = mqe
                    mgc = p.op("tensor", lambda e, cs=cs, bkc=bkc: e.matmul(
                        banks[bkc].t[:, 0:4], onesf[0:1, :],
                        cs["grow4"][0:1, 127:512:128],
                        start=True, stop=True),
                        waits=[st["mgre"]] + list(banks[bkc].free), inc=pT)
                    mks1 = p.op("vector", lambda e, slc=slc, bkc=bkc, kcol=kcol: e.tensor_tensor(
                        kcol[:, 0:4], banks[bkc].t[:, 0:4], cgc[slc][:, 4:8], ALU.subtract),
                        waits=[mgc, st["me2"], kcol_free[par]], inc=pV)
                    mks1b = p.op("scalar", lambda e, kcol=kcol: e.activation(
                        kcol[:, 0:4], kcol[:, 0:4], AF.Exp), waits=[mks1], inc=pA)
                    mks2 = p.op("scalar", lambda e, bkc=bkc, kcol=kcol: e.activation(
                        kcol[:, 4:8], banks[bkc].t[:, 0:4], AF.Exp), waits=[mgc, mks1], inc=pA)
                    bank_release(bkc, mks2)
                    st["mks2"] = mks2
                    mbex = p.op("vector", lambda e, slc=slc: e.tensor_tensor(
                        cex[slc], cex[slc], cb[slc][:, 0:4], ALU.mult),
                        waits=[st["mex_"], st["me1"]], inc=pV)
                    mkt = None
                    for j in range(4):
                        kh = j // 2
                        wj = [kn_ms[kh]] + (list(banks[bkb].free) if j == 0 else [])
                        mkt = p.op("tensor", lambda e, kh=kh, c=c, j=j, bkb=bkb: e.transpose(
                            banks[bkb].t[:].bitcast(BF16)[:, 128 * j:128 * (j + 1)],
                            kn[kh][:, 128 * c:128 * (c + 1)], identb_sb), waits=wj, inc=pT)
                    bexv = cex[slc].unsqueeze(-1).broadcast_to([128, 4, 128])
                    kscv = kcol[:, 0:4].unsqueeze(-1).broadcast_to([128, 4, 128])
                    mka = p.op("vector", lambda e, bexv=bexv, bkb=bkb, ktmA4=ktmA4: e.tensor_tensor(
                        ktmA4.rearrange("p (a b) -> p a b", a=4),
                        banks[bkb].t[:].bitcast(BF16)[:, 0:512].rearrange("p (a b) -> p a b", a=4),
                        bexv, ALU.mult), waits=[mkt, mbex] + mMkj, inc=pV)
                    mkb = p.op("vector", lambda e, kscv=kscv, bkb=bkb, ktmB4=ktmB4: e.tensor_tensor(
                        ktmB4.rearrange("p (a b) -> p a b", a=4),
                        banks[bkb].t[:].bitcast(BF16)[:, 0:512].rearrange("p (a b) -> p a b", a=4),
                        kscv, ALU.mult), waits=[mkt, mks1b, ktm4_free[par]], inc=pV)
                    bank_release(bkb, mkb)
                    st["mka"], st["mkb"] = mka, mkb

                def sc_P1(st):
                    cs, par = st["cs"], st["par"]
                    P14 = cs["U4"]
                    mP1 = [None] * 4
                    for half in range(2):
                        bq = 6 - 2 * par + half
                        for jj in range(2):
                            j = 2 * half + jj
                            wj = [st["mMkj"][j], st["mbv"]] + (list(banks[bq].free) if jj == 0 else [])
                            mq = p.op("tensor", lambda e, cs=cs, j=j, jj=jj, bq=bq: e.matmul(
                                banks[bq].t[:, 128 * jj:128 * (jj + 1)],
                                cs["Mk4"][:, 128 * j:128 * (j + 1)], cs["vtm4"][:, 128 * j:128 * (j + 1)],
                                start=True, stop=True), waits=wj, inc=pT)
                        engb = "vector" if half == 0 else "scalar"
                        me_ = evac(P14[:, 256 * half:256 * (half + 1)],
                                   banks[bq].t[:, 0:256], [mq], eng=engb)
                        mP1[2 * half] = mP1[2 * half + 1] = me_
                        bank_release(bq, me_)
                    vtm4_free[par] = (pT, pT.n)
                    st["mP1"] = mP1

                def sc_W2(st):
                    cs, par = st["cs"], st["par"]
                    ktmA4 = cs["NL4"]
                    W24 = cs["rhs4"]
                    mW2 = [None] * 4
                    for half in range(2):
                        bq = 6 - 2 * par + half
                        for jj in range(2):
                            j = 2 * half + jj
                            wj = [st["mMkj"][j], st["mka"]] + (list(banks[bq].free) if jj == 0 else [])
                            mq = p.op("tensor", lambda e, cs=cs, j=j, jj=jj, bq=bq, ktmA4=ktmA4: e.matmul(
                                banks[bq].t[:, 128 * jj:128 * (jj + 1)],
                                cs["Mk4"][:, 128 * j:128 * (j + 1)], ktmA4[:, 128 * j:128 * (j + 1)],
                                start=True, stop=True), waits=wj, inc=pT)
                        engb = "vector" if half == 0 else "scalar"
                        me_ = evac(W24[:, 256 * half:256 * (half + 1)],
                                   banks[bq].t[:, 0:256], [mq], eng=engb)
                        mW2[2 * half] = mW2[2 * half + 1] = me_
                        bank_release(bq, me_)
                    st["mW2"] = mW2

                def sc_Q2(st):
                    cs, par = st["cs"], st["par"]
                    W24 = cs["rhs4"]
                    mQ2 = [None] * 4
                    for half in range(2):
                        bq = 6 - 2 * par + half
                        for jj in range(2):
                            j = 2 * half + jj
                            wj = [st["mW2"][j], st["mwt"]] + (list(banks[bq].free) if jj == 0 else [])
                            mq = p.op("tensor", lambda e, cs=cs, j=j, jj=jj, bq=bq, W24=W24: e.matmul(
                                banks[bq].t[:, 128 * jj:128 * (jj + 1)],
                                W24[:, 128 * j:128 * (j + 1)], cs["wt4"][:, 128 * j:128 * (j + 1)],
                                start=True, stop=True), waits=wj, inc=pT)
                        for jj in range(2):
                            j = 2 * half + jj
                            mQ2[j] = p.op("vector", lambda e, cs=cs, j=j, jj=jj, bq=bq: e.scalar_tensor_tensor(
                                cs["qe4"][:, 128 * j:128 * (j + 1)],
                                banks[bq].t[:, 128 * jj:128 * (jj + 1)], -1.0,
                                cs["qe4"][:, 128 * j:128 * (j + 1)], ALU.mult, ALU.add),
                                waits=[mq, st["mqe"]], inc=pV)
                        bank_release(bq, [mQ2[2 * half], mQ2[2 * half + 1]])
                    st["mQ2"] = mQ2

                def sc_M2(st):
                    cs, par = st["cs"], st["par"]
                    W24 = cs["rhs4"]
                    ktmB4 = cs["ktm4"]
                    M2T4 = cs["dS4"]
                    kcol = st["kcol"]
                    mM2 = [None] * 4
                    for half in range(2):
                        bq = 4 + 2 * par + half
                        for jj in range(2):
                            j = 2 * half + jj
                            wj = [st["mW2"][j], st["mkb"]] + (list(banks[bq].free) if jj == 0 else [])
                            mq = p.op("tensor", lambda e, j=j, jj=jj, bq=bq, W24=W24, ktmB4=ktmB4: e.matmul(
                                banks[bq].t[:, 128 * jj:128 * (jj + 1)],
                                W24[:, 128 * j:128 * (j + 1)], ktmB4[:, 128 * j:128 * (j + 1)],
                                start=True, stop=True), waits=wj, inc=pT)
                        for jj in range(2):
                            j = 2 * half + jj
                            mM2[j] = p.op("vector", lambda e, j=j, jj=jj, bq=bq, M2T4=M2T4, kcol=kcol: e.scalar_tensor_tensor(
                                M2T4[:, 128 * j:128 * (j + 1)], identb_sb,
                                kcol[:, 4 + j:5 + j], banks[bq].t[:, 128 * jj:128 * (jj + 1)],
                                ALU.mult, ALU.subtract), waits=[mq, st["mks2"]], inc=pV)
                        bank_release(bq, [mM2[2 * half], mM2[2 * half + 1]])
                    ktm4_free[par] = (pT, pT.n)
                    st["mM2"] = mM2

                def sc_chain(st):
                    c, par, slc, ssl = st["c"], st["par"], st["slc"], st["ssl"]
                    cs, csl = st["cs"], st["csl"]
                    P14 = cs["U4"]
                    ktmB4 = cs["ktm4"]
                    M2T4 = cs["dS4"]
                    mP1, mQ2, mM2 = st["mP1"], st["mQ2"], st["mM2"]
                    mOc = [None] * 4
                    mS1 = [None] * 4
                    for half in range(2):
                        for jj in range(2):
                            j = 2 * half + jj
                            ci = 4 * b + j
                            bq = 6 - 2 * par + half
                            wj = [mP1[j], st["mwt"]] + (list(banks[bq].free) if jj == 0 else [])
                            p.op("tensor", lambda e, cs=cs, j=j, jj=jj, bq=bq, P14=P14: e.matmul(
                                banks[bq].t[:, 128 * jj:128 * (jj + 1)],
                                P14[:, 128 * j:128 * (j + 1)], cs["wt4"][:, 128 * j:128 * (j + 1)],
                                start=True, stop=False), waits=wj)
                            p.op("tensor", lambda e, cs=cs, j=j, jj=jj, bq=bq, ci=ci: e.matmul(
                                banks[bq].t[:, 128 * jj:128 * (jj + 1)],
                                Sb[:, 128 * ci:128 * (ci + 1)], cs["qe4"][:, 128 * j:128 * (j + 1)],
                                start=False, stop=True), waits=[mQ2[j], st_ms[ci]], inc=pT)
                            mOc[j] = (pT, pT.n)
                            bq = 4 + 2 * par + half
                            wj = [st["mkb"], mP1[j]] + (list(banks[bq].free) if jj == 0 else [])
                            p.op("tensor", lambda e, j=j, jj=jj, bq=bq, ktmB4=ktmB4, P14=P14: e.matmul(
                                banks[bq].t[:, 128 * jj:128 * (jj + 1)],
                                ktmB4[:, 128 * j:128 * (j + 1)], P14[:, 128 * j:128 * (j + 1)],
                                start=True, stop=False), waits=wj)
                            p.op("tensor", lambda e, j=j, jj=jj, bq=bq, ci=ci, M2T4=M2T4: e.matmul(
                                banks[bq].t[:, 128 * jj:128 * (jj + 1)],
                                M2T4[:, 128 * j:128 * (j + 1)], Sb[:, 128 * ci:128 * (ci + 1)],
                                start=False, stop=True), waits=[mM2[j]], inc=pT)
                            mS1[j] = (pT, pT.n)
                    mOev = [None] * 4
                    for half in range(2):
                        bq = 6 - 2 * par + half
                        engb = "vector" if half == 0 else "scalar"
                        for jj in range(2):
                            j = 2 * half + jj
                            w = [mOc[2 * half + 1]] + ([ost_free[j]] if ost_free[j] else [])
                            mOev[j] = evac(ost[j][:, 128 * c:128 * (c + 1)],
                                           banks[bq].t[:, 128 * jj:128 * (jj + 1)], w, eng=engb)
                            ost_ms[j][c] = mOev[j]
                        bank_release(bq, [mOev[2 * half], mOev[2 * half + 1]])
                    mSf = [None] * 4
                    for half in range(2):
                        bq = 4 + 2 * par + half
                        engb = "vector" if half == 0 else "scalar"
                        for jj in range(2):
                            j = 2 * half + jj
                            ci = 4 * b + j
                            mSf[j] = evac(Sf[:, 128 * ci:128 * (ci + 1)],
                                          banks[bq].t[:, 128 * jj:128 * (jj + 1)],
                                          [mS1[2 * half + 1]], eng=engb)
                        bank_release(bq, [mSf[2 * half], mSf[2 * half + 1]])
                    mSb = p.op("vector", lambda e, b=b: e.tensor_copy(
                        Sb[:, 512 * b:512 * (b + 1)], Sf[:, 512 * b:512 * (b + 1)]),
                        waits=mSf + [mOev[0], mOev[1], mOev[2], mOev[3]], inc=pV)
                    for j in range(4):
                        st_ms[4 * b + j] = mSb
                    kcol_free[par] = mSb
                    cs["free"] = mSb
                    csl["free"] = mSb
                    kkq_free[ssl] = (pV, pV.n)

                for cp in range(2) if SCAN_ON else []:
                    sA = sc_new(2 * cp)
                    sB = sc_new(2 * cp + 1)
                    sc_early(sA)
                    deq_tick(2 if DEQIL else 0)
                    yield
                    sc_early(sB)
                    deq_tick(2 if DEQIL else 0)
                    yield
                    for lev in range(1, 7):
                        for part in range(3):
                            sc_inv(sA, lev, part)
                            sc_inv(sB, lev, part)
                        if lev % 3 == 0:
                            deq_tick(1 if DEQIL else 0)
                        yield
                    sc_T6(sA)
                    sc_T6(sB)
                    deq_tick(1 if DEQIL else 0)
                    yield
                    sc_P1(sA)
                    sc_P1(sB)
                    yield
                    sc_W2(sA)
                    sc_W2(sB)
                    yield
                    sc_Q2(sA)
                    sc_Q2(sB)
                    yield
                    sc_M2(sA)
                    sc_M2(sB)
                    yield
                    sc_chain(sA)
                    sc_chain(sB)
                    deq_tick(1 if DEQIL else 0)
                    yield
                for pr in range(2) if (GATE_ON and SCAN_ON) else []:
                    gst = [{} for _ in range(2)]
                    for u in range(2):
                        j = 2 * pr + u
                        w = [zs_ms[j]] + [ost_ms[j][c_] for c_ in range(4)]
                        if xgt_free[u]:
                            w.append(xgt_free[u])
                        gst[u]["mxg"] = p.op("vector", lambda e, j=j, u=u: e.tensor_tensor(
                            xgt2[u], ost[j], zs[j], ALU.mult), waits=w, inc=pV)
                        zs_free[j] = gst[u]["mxg"]
                        ost_free[j] = gst[u]["mxg"]
                    for u in range(2):
                        gst[u]["msq"] = p.op("vector", lambda e, u=u: e.tensor_tensor(
                            sqg2[u], xgt2[u], xgt2[u], ALU.mult),
                            waits=[l2_sq_free[u], gst[u]["mxg"]], inc=pV)
                    for u in range(2):
                        bq = 4 + u
                        gst[u]["mcs"] = p.op("tensor", lambda e, u=u, bq=bq: e.matmul(
                            banks[bq].t[0:1, :], onesb[:, 0:1], sqg2[u], start=True, stop=True),
                            waits=[gst[u]["msq"]] + list(banks[bq].free), inc=pT)
                        l2_sq_free[u] = gst[u]["mcs"]
                    for u in range(2):
                        bq = 4 + u
                        mr_ = p.op("vector", lambda e, u=u, bq=bq: e.tensor_scalar(
                            rrow2[u], banks[bq].t[0:1, :], 1.0 / 128.0, RMS_EPS, ALU.mult, ALU.add),
                            waits=[gst[u]["mcs"], l2_rr_free[u]], inc=pV)
                        bank_release(bq, mr_)
                        gst[u]["mr"] = mr_
                    for u in range(2):
                        gst[u]["mlg"] = p.op("scalar", lambda e, u=u: e.activation(
                            rrow2[u], rrow2[u], AF.Ln), waits=[gst[u]["mr"]], inc=pA)
                    for u in range(2):
                        gst[u]["mex"] = p.op("scalar", lambda e, u=u: e.activation(
                            rrow2[u], rrow2[u], AF.Exp, scale=-0.5), waits=[gst[u]["mlg"]], inc=pA)
                    for u in range(2):
                        bq = 4 + u
                        gst[u]["mbc"] = p.op("tensor", lambda e, u=u, bq=bq: e.matmul(
                            banks[bq].t[:], onesf[0:1, :], rrow2[u], start=True, stop=True),
                            waits=[gst[u]["mex"]] + list(banks[bq].free), inc=pT)
                        l2_rr_free[u] = gst[u]["mbc"]
                    for u in range(2):
                        j = 2 * pr + u
                        bq = 4 + u
                        ytb = yt[j % 2]
                        w = [gst[u]["mbc"]] + ([yt_free[j % 2]] if yt_free[j % 2] else [])
                        my1 = p.op("vector", lambda e, ytb=ytb, u=u, bq=bq: e.tensor_tensor(
                            ytb, xgt2[u], banks[bq].t[:], ALU.mult), waits=w, inc=pV)
                        xgt_free[u] = my1
                        myv = p.op("vector", lambda e, ytb=ytb: e.tensor_scalar(
                            ytb, ytb, normw_sb, None, ALU.mult), waits=[MISC_ALL, my1], inc=pV)
                        bank_release(bq, myv)
                        jd = b * 4 + tt // 2
                        col0 = (TT * tt) % 1024
                        mys = p.op("gpsimd", lambda e, jd=jd, j=j, col0=col0, ytb=ytb: e.dma_start(
                            out=dram["a2a_in"][512 * jd + 128 * j:512 * jd + 128 * (j + 1),
                                               col0:col0 + 512], in_=ytb),
                            waits=[myv], inc=d_yst[j % 2], k=16)
                        yt_free[j % 2] = (d_yst[j % 2], d_yst[j % 2].n)
                if b == 0:
                    if "convq" in dbg:
                        p.op("gpsimd", lambda e, tt=tt: e.dma_start(
                            out=dbg["convq"][:, TT * tt:TT * (tt + 1)], in_=qn[0]),
                            waits=[qn_ms[0]], inc=d_dbg, k=16)
                        qn_free[0] = (d_dbg, d_dbg.n)
                    if "convk" in dbg:
                        p.op("gpsimd", lambda e, tt=tt: e.dma_start(
                            out=dbg["convk"][:, TT * tt:TT * (tt + 1)], in_=kn[0]),
                            waits=[kn_ms[0]], inc=d_dbg, k=16)
                        kn_free[0] = (d_dbg, d_dbg.n)
                    if "convv" in dbg:
                        p.op("gpsimd", lambda e, tt=tt: e.dma_start(
                            out=dbg["convv"][:, TT * tt:TT * (tt + 1)], in_=convo[4]),
                            waits=[convo_ready[4]], inc=d_dbg, k=16)
                        convo_free[4] = (d_dbg, d_dbg.n)
                    if "gbeta" in dbg:
                        p.op("gpsimd", lambda e, tt=tt: e.dma_start(
                            out=dbg["gbeta"][0:4, TT * tt:TT * (tt + 1)], in_=bsig[0:4, :]),
                            waits=[mbs], inc=d_dbg, k=16)
                        bsig_free[0] = (d_dbg, d_dbg.n)
                        p.op("gpsimd", lambda e, tt=tt: e.dma_start(
                            out=dbg["gbeta"][4:8, TT * tt:TT * (tt + 1)], in_=gcum[4:8, :]),
                            waits=[ba_ms], inc=d_dbg, k=16)
                        gcum_free[0] = (d_dbg, d_dbg.n)
                if gti < 12 and DEQ_ON:
                    deq_tick(deq_goal[0] - deq_idx[0])
                    if deq_idx[0] in (64, 128, 192):
                        deq_B()
                        mat = "gud"[deq_idx[0] // 64 - 1]
                        p.op("gpsimd", lambda e, mat=mat: e.collective_compute(
                            "AllGather", ALU.bypass, replica_groups=[list(range(NCORES))],
                            ins=[dram[f"agw_{mat}_in"].opt()], outs=[dram[f"agw_{mat}_out"].opt()]),
                            waits=[(d_wst[0], d_wst[0].n), (d_wst[1], d_wst[1].n)],
                            inc=c_w[mat], k=None)
                        if mat == "g" and "wdeq" in dbg:
                            p.op("gpsimd", lambda e: e.dma_start(
                                out=dbg["wdeq"], in_=dram["agw_g_in"]),
                                waits=[(d_wst[0], d_wst[0].n), (d_wst[1], d_wst[1].n)],
                                inc=d_dbg, k=16)
            pend_scan[0] = do_scan()

    if pend_scan[0]:
        for _ in pend_scan[0]:
            pass
        pend_scan[0] = None
    p.op("gpsimd", lambda e: e.collective_compute(
        "AllToAll", ALU.bypass, replica_groups=[list(range(NCORES))],
        ins=[dram["a2a_in"].opt()], outs=[dram["a2a_out"].opt()]),
        waits=[(d_yst[0], d_yst[0].n), (d_yst[1], d_yst[1].n)], inc=c_a2a, k=None)
    A2A_DONE = (c_a2a, 1)
    if "yloc" in dbg:
        p.op("gpsimd", lambda e: e.dma_start(out=dbg["yloc"], in_=dram["a2a_out"][0:512, :]),
             waits=[A2A_DONE], inc=d_dbg, k=16)
    PH1END = [(pT, pT.n), (pV, pV.n), (pA, pA.n), (d_yst[0], d_yst[0].n),
              (d_yst[1], d_yst[1].n), (d_wst[0], d_wst[0].n), (d_wst[1], d_wst[1].n)]
    if STAGE < 2:
        p.op("gpsimd", lambda e: e.nop(), waits=[A2A_DONE, (d_dbg, d_dbg.n)] + PH1END)
        return

    A.reset(resmark)
    ln2fm = A.carve([128, 16 * 1024], BF16, "ln2fm")
    ph3_mark = A.mark()
    wo_sb = A.carve([128, 32 * 1024], BF16, "wo")
    ytl = A.carve([128, 32 * 128], BF16, "ytl")
    ostg = A.carve([128, 8 * 2048], F32, "ostg")
    xt2 = A.carve([128, D], F32, "xt2")
    sq2 = A.carve([128, D], F32, "sq2")
    ln2b = A.carve([128, D], BF16, "ln2b")
    st2 = A.carve([128, 8], F32, "st2")
    wo_free = None
    ytl_free = None
    first3 = [True]
    for hh in range(2):
        wload = []
        for q4 in range(4):
            w = list(PH1END) if (hh == 0 and q4 == 0) else []
            if wo_free:
                w.append(wo_free)
            wload.append(p.op("gpsimd", lambda e, q4=q4, hh=hh: e.dma_start(
                out=wo_sb[:, 8192 * q4:8192 * (q4 + 1)].rearrange("p (c h) -> p c h", h=1024),
                in_=dram["wout"].rearrange("(c p) h -> p c h", p=128)[
                    :, 8 * q4:8 * (q4 + 1), 1024 * hh:1024 * (hh + 1)]),
                waits=w, inc=d_wo, k=16))
        WO_RDY = (d_wo, d_wo.n)
        for i in range(8):
            w = [A2A_DONE, WO_RDY] + ([ytl_free] if ytl_free else [])
            if hh == 0 and i == 0:
                w += PH1END
            myl = p.op("sync", lambda e, i=i: e.dma_start(
                out=ytl[:].rearrange("p (c q) -> p c q", q=128),
                in_=dram["a2a_out"].rearrange("(c p) t -> p c t", p=128)[:, :, 128 * i:128 * (i + 1)]),
                waits=w, inc=d_yt, k=16)
            for hb in range(2):
                bk = (i * 2 + hb) % 4
                for dc in range(32):
                    w = []
                    if dc == 0:
                        w = [myl, WO_RDY] + list(banks[bk].free)
                    inc = pT if dc == 31 else None
                    mmo = p.op("tensor", lambda e, dc=dc, hb=hb, bk=bk: e.matmul(
                        banks[bk].t[:], ytl[:, 128 * dc:128 * (dc + 1)],
                        wo_sb[:, 1024 * dc + 512 * hb:1024 * dc + 512 * (hb + 1)],
                        start=(dc == 0), stop=(dc == 31)), waits=w, inc=inc)
                me_ = evac(ostg[:, 2048 * i + 1024 * hh + 512 * hb:
                               2048 * i + 1024 * hh + 512 * (hb + 1)], banks[bk].t[:], [mmo])
                bank_release(bk, me_)
            ytl_free = (pT, pT.n)
        wo_free = (pT, pT.n)
    OSTG_RDY = (pV, pV.n)
    OSTG_RDY_A = (pA, pA.n)
    ln2fm_ms = []
    xt2_free = None
    p2_m5b = None
    LNT3 = []
    for i in range(8):
        og = ostg[:, 2048 * i:2048 * (i + 1)]
        w = [OSTG_RDY, OSTG_RDY_A]
        m1 = p.op("vector", lambda e, og=og: e.tensor_reduce(st2[:, 0:1], og, AX.X, ALU.add),
                  waits=w + [p2_m5b], inc=pV)
        m2_ = p.op("scalar", lambda e, og=og: e.activation(sq2, og, AF.Square),
                   waits=w + (list(xt2_free) if xt2_free else []), inc=pA)
        mr2 = p.op("vector", lambda e: e.tensor_reduce(st2[:, 1:2], sq2, AX.X, ALU.add),
                   waits=[m2_], inc=pV)
        mm1 = p.op("vector", lambda e: e.tensor_scalar(st2[:, 2:3], st2[:, 0:1], 1.0 / 2048.0, None, ALU.mult),
                   waits=[m1], inc=pV)
        mm2 = p.op("vector", lambda e: e.tensor_scalar(st2[:, 3:4], st2[:, 1:2], 1.0 / 2048.0, None, ALU.mult),
                   waits=[mr2], inc=pV)
        mm3 = p.op("vector", lambda e: e.tensor_tensor(st2[:, 4:5], st2[:, 2:3], st2[:, 2:3], ALU.mult),
                   waits=[mm1], inc=pV)
        m4 = p.op("vector", lambda e: e.tensor_tensor(st2[:, 5:6], st2[:, 3:4], st2[:, 4:5], ALU.subtract),
                  waits=[mm2, mm3], inc=pV)
        m5 = p.op("scalar", lambda e: e.activation(st2[:, 6:7], st2[:, 5:6], AF.Ln, bias=LN_EPS),
                  waits=[m4], inc=pA)
        m6 = p.op("scalar", lambda e: e.activation(st2[:, 7:8], st2[:, 6:7], AF.Exp, scale=-0.5),
                  waits=[m5], inc=pA)
        mlx = p.op("sync", lambda e, i=i: e.dma_start(
            out=xt2, in_=dram["x_sl"][128 * i:128 * (i + 1), :]),
            waits=list(xt2_free) if xt2_free else [], inc=d_x, k=16)
        mln_ = p.op("vector", lambda e, og=og: e.tensor_scalar(
            sq2, og, st2[:, 2:3], st2[:, 7:8], ALU.subtract, ALU.mult), waits=[m6, mm1], inc=pV)
        mx2 = p.op("vector", lambda e: e.tensor_tensor(xt2, sq2, xt2, ALU.add),
                   waits=[(d_x, d_x.n), mln_], inc=pV)
        mx2st = p.op("gpsimd", lambda e, i=i: e.dma_start(
            out=dram["x2_sp"][128 * i:128 * (i + 1), :], in_=xt2), waits=[mx2], inc=d_x2st, k=16)
        if "x2" in dbg:
            p.op("gpsimd", lambda e, i=i: e.dma_start(
                out=dbg["x2"][128 * i:128 * (i + 1), :], in_=xt2), waits=[mx2], inc=d_dbg, k=16)
        m2b = p.op("scalar", lambda e: e.activation(sq2, xt2, AF.Square), waits=[mx2], inc=pA)
        mrb1 = p.op("vector", lambda e: e.tensor_reduce(st2[:, 1:2], xt2, AX.X, ALU.add),
                    waits=[m2b, mx2], inc=pV)
        mrb2 = p.op("vector", lambda e: e.tensor_reduce(st2[:, 0:1], sq2, AX.X, ALU.add),
                    waits=[m2b], inc=pV)
        mmb1 = p.op("vector", lambda e: e.tensor_scalar(st2[:, 2:3], st2[:, 1:2], 1.0 / 2048.0, None, ALU.mult),
                    waits=[mrb1], inc=pV)
        mmb2 = p.op("vector", lambda e: e.tensor_scalar(st2[:, 3:4], st2[:, 0:1], 1.0 / 2048.0, None, ALU.mult),
                    waits=[mrb2], inc=pV)
        mmb3 = p.op("vector", lambda e: e.tensor_tensor(st2[:, 4:5], st2[:, 2:3], st2[:, 2:3], ALU.mult),
                    waits=[mmb1], inc=pV)
        m4b = p.op("vector", lambda e: e.tensor_tensor(st2[:, 5:6], st2[:, 3:4], st2[:, 4:5], ALU.subtract),
                   waits=[mmb2, mmb3], inc=pV)
        m5b = p.op("scalar", lambda e: e.activation(st2[:, 6:7], st2[:, 5:6], AF.Ln, bias=LN_EPS),
                   waits=[m4b], inc=pA)
        p2_m5b = m5b
        m6b = p.op("scalar", lambda e: e.activation(st2[:, 7:8], st2[:, 6:7], AF.Exp, scale=-0.5),
                   waits=[m5b], inc=pA)
        mlb = p.op("vector", lambda e: e.tensor_scalar(
            ln2b, xt2, st2[:, 2:3], st2[:, 7:8], ALU.subtract, ALU.mult), waits=[m6b, mmb1], inc=pV)
        lnview = ln2b.rearrange("p (c a two) -> p c two a", c=8, two=2)
        for parity in range(2):
            for cblk in range(8):
                fc = parity * 8 + cblk
                bk = 2 + (fc & 1)
                w = [mlb] + list(banks[bk].free)
                mt = p.op("tensor", lambda e, cblk=cblk, parity=parity, bk=bk: e.transpose(
                    banks[bk].t[:].bitcast(BF16)[:, 0:128], lnview[:, cblk, parity, :],
                    identb_sb), waits=w, inc=pT)
                me_ = evac(ln2fm[:, 1024 * fc + 128 * i:1024 * fc + 128 * (i + 1)],
                           banks[bk].t[:].bitcast(BF16)[:, 0:128], [mt])
                bank_release(bk, me_)
                ln2fm_ms.append(me_)
        xt2_free = [(pT, pT.n), mx2st]
        if i == 3:
            LNT3.extend([(pV, pV.n), (pA, pA.n)])
    LN2FM_RDY = [(pV, pV.n), (pA, pA.n)]
    LN2_RDY = {0: LNT3, 1: LN2FM_RDY}
    PH2END = [(pT, pT.n), (pV, pV.n), (pA, pA.n), (d_x2st, d_x2st.n)]
    if STAGE < 3:
        p.op("gpsimd", lambda e: e.nop(), waits=[(d_dbg, d_dbg.n)] + PH2END)
        return

    A.reset(ph3_mark)
    inter = A.carve([128, 64 * 512], BF16, "inter")
    wgl = [A.carve([128, 16 * 256], BF16, f"wgl{i}") for i in range(2)]
    wul = [A.carve([128, 16 * 256], BF16, f"wul{i}") for i in range(2)]
    wdl = [A.carve([128, 512], BF16, f"wdl{i}") for i in range(4)]
    gtmp2 = [A.carve([128, 512], BF16, f"gtmp{i}") for i in range(2)]
    utmp2 = [A.carve([128, 512], BF16, f"utmp{i}") for i in range(2)]
    acc = A.carve([128, 4 * 2048], F32, "acc")
    xt3 = A.carve([128, D], F32, "xt3")
    sq3 = A.carve([128, D], F32, "sq3")
    outb = A.carve([128, D], F32, "outb")
    st3 = A.carve([128, 8], F32, "st3")
    wgl_free = [None] * 4
    wul_free = [None] * 4
    wdl_free = [None] * 4
    inter_free = None
    acc_free = None
    gtmp_free = [None, None]
    utmp_free = [None, None]
    first_p3 = [True]
    wrd_pair = [None]
    wru_pair = [None]
    prev_mo3 = None
    prev_mst = None
    prev_m6ep = None
    acc_seg_ms = [[None] * 4 for _ in range(4)]
    AGW_DONE = {m: (c_w[m], 1) for m in "gud"}
    for ts in range(2):
        for nh in range(2):
            inter_ms = [None] * 64
            for ntl in range(64):
                nt = 64 * nh + ntl
                rr_ = nt // 16
                ncol = 128 * (nt % 16)
                bufw = (ntl // 2) % 2
                off = 128 * (ntl % 2)
                if ntl % 2 == 0:
                    w = [AGW_DONE["g"]] + ([wgl_free[bufw]] if wgl_free[bufw] else [])
                    if first_p3[0]:
                        w += [wo_free] + LNT3
                        first_p3[0] = False
                    p.op("sync", lambda e, rr_=rr_, ncol=ncol, bufw=bufw: e.dma_start(
                        out=wgl[bufw][:].rearrange("p (c q) -> p c q", q=256),
                        in_=dram["agw_g_out"].rearrange("(r c p) n -> r p c n", r=8, p=128)[
                            rr_][:, :, ncol:ncol + 256]), waits=w, inc=d_wgl[bufw], k=16)
                    w = [AGW_DONE["u"]] + ([wul_free[bufw]] if wul_free[bufw] else [])
                    p.op("sync", lambda e, rr_=rr_, ncol=ncol, bufw=bufw: e.dma_start(
                        out=wul[bufw][:].rearrange("p (c q) -> p c q", q=256),
                        in_=dram["agw_u_out"].rearrange("(r c p) n -> r p c n", r=8, p=128)[
                            rr_][:, :, ncol:ncol + 256]), waits=w, inc=d_wul[bufw], k=16)
                    wrd_pair[0] = (d_wgl[bufw], d_wgl[bufw].n)
                    wru_pair[0] = (d_wul[bufw], d_wul[bufw].n)
                WRD = wrd_pair[0]
                WRU = wru_pair[0]
                gtmp = gtmp2[ntl % 2]
                utmp = utmp2[ntl % 2]
                bnkg = 0 + 2 * (ntl % 2)
                bnku = 1 + 2 * (ntl % 2)
                mm_ms = {}
                for (bnk, wsb, wrd) in ((bnkg, wgl[bufw], WRD), (bnku, wul[bufw], WRU)):
                    for fc in range(16):
                        w = []
                        if fc == 0:
                            w = [wrd] + LN2_RDY[ts] + \
                                list(banks[bnk].free)
                        inc = pT if fc == 15 else None
                        mm_ms[bnk] = p.op("tensor", lambda e, fc=fc, wsb=wsb, bnk=bnk, ts=ts, off=off: e.matmul(
                            banks[bnk].t[:], wsb[:, 256 * fc + off:256 * fc + off + 128],
                            ln2fm[:, 1024 * fc + 512 * ts:1024 * fc + 512 * (ts + 1)],
                            start=(fc == 0), stop=(fc == 15)), waits=w, inc=inc)
                if ntl % 2 == 1:
                    wgl_free[bufw] = mm_ms[bnkg]
                    wul_free[bufw] = mm_ms[bnku]
                mgact = p.op("scalar", lambda e, bnkg=bnkg, gtmp=gtmp: e.activation(
                    gtmp, banks[bnkg].t[:], AF.Gelu_apprx_tanh),
                    waits=[mm_ms[bnkg], gtmp_free[ntl % 2]], inc=pA)
                bank_release(bnkg, mgact)
                muact = p.op("vector", lambda e, bnku=bnku, utmp=utmp: e.tensor_copy(
                    utmp, banks[bnku].t[:]),
                    waits=[mm_ms[bnku], utmp_free[ntl % 2]], inc=pV)
                bank_release(bnku, muact)
                w = [mgact, muact] + ([inter_free] if inter_free else [])
                inter_ms[ntl] = p.op("vector", lambda e, ntl=ntl, gtmp=gtmp, utmp=utmp: e.tensor_tensor(
                    inter[:, 512 * ntl:512 * (ntl + 1)], gtmp, utmp, ALU.mult),
                    waits=w, inc=pV)
                gtmp_free[ntl % 2] = inter_ms[ntl]
                utmp_free[ntl % 2] = inter_ms[ntl]
            for h4 in range(4):
                for ntl in range(64):
                    nt = 64 * nh + ntl
                    bufw = ntl % 4
                    w = [AGW_DONE["d"]] + ([wdl_free[bufw]] if wdl_free[bufw] else [])
                    p.op("sync", lambda e, nt=nt, h4=h4, bufw=bufw: e.dma_start(
                        out=wdl[bufw], in_=dram["agw_d_out"][128 * nt:128 * (nt + 1),
                                                             512 * h4:512 * (h4 + 1)]),
                        waits=w, inc=d_wd[bufw], k=16)
                    WDRD = (d_wd[bufw], d_wd[bufw].n)
                    for tq in range(4):
                        bnk = 4 + tq
                        w = [inter_ms[ntl]]
                        if ntl == 0:
                            w += [WDRD] + list(banks[bnk].free)
                        else:
                            w += [WDRD]
                        mmd = p.op("tensor", lambda e, ntl=ntl, tq=tq, bnk=bnk, bufw=bufw: e.matmul(
                            banks[bnk].t[:], inter[:, 512 * ntl + 128 * tq:512 * ntl + 128 * (tq + 1)],
                            wdl[bufw], start=(ntl == 0), stop=(ntl == 63)),
                            waits=w, inc=(pT if tq == 3 else None))
                    wdl_free[bufw] = (pT, pT.n)
                for tq in range(4):
                    bnk = 4 + tq
                    dst = acc[:, 2048 * tq + 512 * h4:2048 * tq + 512 * (h4 + 1)]
                    w = [(pT, pT.n)] + ([acc_free] if acc_free else [])
                    if nh == 0:
                        me_ = evac(dst, banks[bnk].t[:], w)
                        acc_seg_ms[tq][h4] = me_
                    else:
                        me_ = p.op("vector", lambda e, dst=dst, bnk=bnk: e.tensor_tensor(
                            dst, dst, banks[bnk].t[:], ALU.add),
                            waits=w + [acc_seg_ms[tq][h4]], inc=pV)
                        acc_seg_ms[tq][h4] = me_
                    bank_release(bnk, me_)
            inter_free = (pT, pT.n)
        ACC_RDY = [(pV, pV.n), (pA, pA.n)]
        ep_ms = []
        for tq in range(4):
            i = 4 * ts + tq
            ac = acc[:, 2048 * tq:2048 * (tq + 1)]
            w = list(ACC_RDY)
            m1 = p.op("vector", lambda e, ac=ac: e.tensor_reduce(st3[:, 0:1], ac, AX.X, ALU.add),
                      waits=w + [prev_m6ep], inc=pV)
            m2_ = p.op("scalar", lambda e, ac=ac: e.activation(sq3, ac, AF.Square),
                       waits=w + ([prev_mo3] if prev_mo3 else []), inc=pA)
            mr2 = p.op("vector", lambda e: e.tensor_reduce(st3[:, 1:2], sq3, AX.X, ALU.add),
                       waits=[m2_], inc=pV)
            mm1 = p.op("vector", lambda e: e.tensor_scalar(st3[:, 2:3], st3[:, 0:1], 1.0 / 2048.0, None, ALU.mult),
                       waits=[m1], inc=pV)
            mm2 = p.op("vector", lambda e: e.tensor_scalar(st3[:, 3:4], st3[:, 1:2], 1.0 / 2048.0, None, ALU.mult),
                       waits=[mr2], inc=pV)
            mm3 = p.op("vector", lambda e: e.tensor_tensor(st3[:, 4:5], st3[:, 2:3], st3[:, 2:3], ALU.mult),
                       waits=[mm1], inc=pV)
            m4 = p.op("vector", lambda e: e.tensor_tensor(st3[:, 5:6], st3[:, 3:4], st3[:, 4:5], ALU.subtract),
                      waits=[mm2, mm3], inc=pV)
            m5 = p.op("scalar", lambda e: e.activation(st3[:, 6:7], st3[:, 5:6], AF.Ln, bias=LN_EPS),
                      waits=[m4], inc=pA)
            m6 = p.op("scalar", lambda e: e.activation(st3[:, 7:8], st3[:, 6:7], AF.Exp, scale=-0.5),
                      waits=[m5], inc=pA)
            prev_m6ep = m6
            mx3 = p.op("sync", lambda e, i=i: e.dma_start(
                out=xt3, in_=dram["x2_sp"][128 * i:128 * (i + 1), :]),
                waits=[(d_x2st, d_x2st.n)] + LN2FM_RDY + ([prev_mo3] if prev_mo3 else []), inc=d_x2l, k=16)
            mln3 = p.op("vector", lambda e, ac=ac: e.tensor_scalar(
                sq3, ac, st3[:, 2:3], st3[:, 7:8], ALU.subtract, ALU.mult), waits=[m6, mm1], inc=pV)
            mo3 = p.op("vector", lambda e: e.tensor_tensor(outb, sq3, xt3, ALU.add),
                       waits=[(d_x2l, d_x2l.n), mln3] + ([prev_mst] if prev_mst else []), inc=pV)
            prev_mo3 = mo3
            mst = p.op("gpsimd", lambda e, i=i: e.dma_start(
                out=dram["out"][128 * i:128 * (i + 1), :], in_=outb), waits=[mo3], inc=d_outst, k=16)
            prev_mst = mst
            ep_ms.append(mst)
        acc_free = (pV, pV.n)
    p.op("gpsimd", lambda e: e.nop(),
         waits=[(d_outst, d_outst.n), (d_dbg, d_dbg.n), (d_x2st, d_x2st.n)])


def _host_prep(inputs):
    import ml_dtypes
    x = np.ascontiguousarray(inputs["L0_attn_hidden_states"]).reshape(B * S, D)
    wqkvz = inputs["L0_attn_w_qkvz"]
    wba = inputs["L0_attn_w_ba"]
    convw = np.ascontiguousarray(inputs["L0_attn_conv_w"][:, 0, :])
    dtb = inputs["L0_attn_dt_bias"]
    alog = inputs["L0_attn_A_log"]
    normw = inputs["L0_attn_norm_w"]
    wout = inputs["L0_attn_w_out"]

    def bytesT(pk):
        bv = np.ascontiguousarray(pk.view(np.uint8)[:, ::4])
        return np.ascontiguousarray(bv.T)

    gT = bytesT(inputs["L0_ffn_gate_W_packed"])
    uT = bytesT(inputs["L0_ffn_up_W_packed"])
    dT = bytesT(inputs["L0_ffn_down_W_packed"])
    sgT = np.ascontiguousarray(inputs["L0_ffn_gate_scales"].T)
    suT = np.ascontiguousarray(inputs["L0_ffn_up_scales"].T)
    sdT = np.ascontiguousarray(inputs["L0_ffn_down_scales"].T)

    idx = np.concatenate([np.concatenate([256 * k + 2 * np.arange(128),
                                          256 * k + 2 * np.arange(128) + 1])
                          for k in range(8)])
    ident = np.eye(128, dtype=np.float32).astype(ml_dtypes.bfloat16)
    sidx = np.arange(128)
    maskS = np.where(sidx[:, None] < sidx[None, :], 0.0, MASK_NEG).astype(np.float32)
    maskI = np.where(sidx[:, None] <= sidx[None, :], 0.0, MASK_NEG).astype(np.float32)

    in_maps = []
    for c in range(NCORES):
        bac = [8 * c, 8 * c + 1, 8 * c + 4, 8 * c + 5,
               8 * c + 2, 8 * c + 3, 8 * c + 6, 8 * c + 7]
        cwrows = np.concatenate([
            np.arange(128 * (2 * c), 128 * (2 * c + 2)),
            2048 + np.arange(128 * (2 * c), 128 * (2 * c + 2)),
            4096 + np.arange(128 * (4 * c), 128 * (4 * c + 4))])
        im = {
            "x_sl": np.ascontiguousarray(x[TPC * c:TPC * (c + 1)]),
            "wq": np.ascontiguousarray(wqkvz[:, 1536 * c:1536 * (c + 1)]),
            "wba": np.ascontiguousarray(wba[:, bac]),
            "convw": np.ascontiguousarray(convw[cwrows]),
            "dtba": np.ascontiguousarray(np.stack([dtb[4 * c:4 * c + 4],
                                                   alog[4 * c:4 * c + 4]])),
            "normw": np.ascontiguousarray(normw),
            "wout": np.ascontiguousarray(wout),
            "pkg": np.ascontiguousarray(gT[:, NSL * c + idx]),
            "pku": np.ascontiguousarray(uT[:, NSL * c + idx]),
            "pkd": np.ascontiguousarray(dT[1024 * c:1024 * (c + 1)]),
            "scg": np.ascontiguousarray(sgT[:, NSL * c + idx]),
            "scu": np.ascontiguousarray(suT[:, NSL * c + idx]),
            "scd": np.ascontiguousarray(sdT[64 * c:64 * (c + 1)]),
            "identb": ident,
            "maskS": maskS,
            "maskI": maskI,
        }
        in_maps.append(im)
    return in_maps


_NC_CACHE = {}


def kernel(**inputs):
    debug = inputs.pop("_debug", ())
    key = tuple(sorted(debug))
    if key not in _NC_CACHE:
        _NC_CACHE[key] = build_program(debug=key)
    nc = _NC_CACHE[key]
    in_maps = _host_prep(inputs)
    res = run_bass_kernel_spmd(nc, in_maps, core_ids=list(range(NCORES)))
    out = np.concatenate([res.results[c]["out"] for c in range(NCORES)], axis=0)
    full = out.reshape(B, S, D).astype(np.float32)
    if debug:
        return full, res
    return full


# version 31 (final)
# speedup vs baseline: 1.1344x; 1.1344x over previous
import contextlib
import numpy as np
import concourse.bass as bass
import concourse.mybir as mybir
from concourse.bass_utils import run_bass_kernel_spmd
from concourse.aot_env import allow_ici_all_to_all

F32 = mybir.dt.float32
BF16 = mybir.dt.bfloat16
U8 = mybir.dt.uint8
AF = mybir.ActivationFunctionType
ALU = mybir.AluOpType
AX = mybir.AxisListType

NCORES = 8
B, S, D = 2, 4096, 2048
TPC = 1024
TT = 512
CH = 128
NTT = S // TT
DK = DV = 128
MLP = 16384
NSL = MLP // NCORES
LN_EPS = 1e-5
RMS_EPS = 1e-6
L2_EPS = 1e-6
MASK_NEG = -60000.0

ENGS = ("sync", "tensor", "vector", "scalar", "gpsimd")


class Sem:
    __slots__ = ("name", "n", "h")

    def __init__(self, name):
        self.name, self.n, self.h = name, 0, None


class Plan:
    def __init__(self):
        self.steps = {e: [] for e in ENGS}
        self.sems = []

    def sem(self, name):
        s = Sem(name)
        self.sems.append(s)
        return s

    def op(self, eng, fn, waits=(), inc=None, k=1):
        for w_ in waits:
            if w_ is not None and (not isinstance(w_, tuple) or len(w_) != 2):
                raise ValueError(f"bad wait item: {w_!r}")
        ws = tuple((s, v) for w_ in waits if w_ is not None for (s, v) in [w_])
        if inc is not None:
            inc.n += (1 if k is None else k)
        self.steps[eng].append((fn, ws, (inc, k) if inc is not None else None))
        return (inc, inc.n) if inc is not None else None


class Bank:
    __slots__ = ("t", "free")

    def __init__(self, t):
        self.t = t
        self.free = []


def emit(nc, plan, stack):
    for s in plan.sems:
        s.h = stack.enter_context(nc.semaphore(s.name))
    block = stack.enter_context(nc.Block())
    for eng in ENGS:
        steps = plan.steps[eng]

        def body(e, steps=steps):
            for fn, ws, inc in steps:
                for (sm, val) in ws:
                    e.wait_ge(sm.h, val)
                inst = fn(e)
                if inc is not None:
                    if inc[1] is None:
                        inst.then_inc(inc[0].h)
                    else:
                        inst.then_inc(inc[0].h, inc[1])
        getattr(block, eng)(body)


class Arena:

    def __init__(self, t, nbytes):
        self.t = t
        self.nbytes = nbytes
        self.off = 0

    def carve(self, shape, dtype, name=""):
        p, f = shape
        esz = {F32: 4, BF16: 2, U8: 1, mybir.dt.int32: 4, mybir.dt.uint16: 2}[dtype]
        nb = f * esz
        nb_al = (nb + 31) // 32 * 32
        assert self.off + nb_al <= self.nbytes, \
            f"arena overflow at {name}: {self.off}+{nb_al} > {self.nbytes}"
        ap = self.t[:, self.off:self.off + nb].bitcast(dtype)
        self.off += nb_al
        if p < 128:
            ap = ap[0:p, :]
        return ap

    def mark(self):
        return self.off

    def reset(self, mark):
        self.off = mark


def build_program(debug=()):
    import os
    dbgflag = os.environ.get("KDBG", "0") == "1"
    nc = bass.Bass("TRN2", target_bir_lowering=False, debug=dbgflag)
    dram = {}

    def din(name, shape, dtype):
        dram[name] = nc.dram_tensor(name, shape, dtype, kind="ExternalInput").ap()
        return dram[name]

    def dout(name, shape, dtype):
        dram[name] = nc.dram_tensor(name, shape, dtype, kind="ExternalOutput").ap()
        return dram[name]

    def dint(name, shape, dtype):
        dram[name] = nc.dram_tensor(name, shape, dtype).ap()
        return dram[name]

    x_sl = din("x_sl", [TPC, D], F32)
    wq = din("wq", [D, 1536], F32)
    wba = din("wba", [D, 8], F32)
    convw = din("convw", [1024, 4], F32)
    dtba = din("dtba", [2, 4], F32)
    normw = din("normw", [DV], F32)
    wout = din("wout", [4096, D], F32)
    pkg = din("pkg", [1024, NSL], U8)
    pku = din("pku", [1024, NSL], U8)
    pkd = din("pkd", [1024, D], U8)
    scg = din("scg", [64, NSL], F32)
    scu = din("scu", [64, NSL], F32)
    scd = din("scd", [64, D], F32)
    identb = din("identb", [128, 128], BF16)
    maskS = din("maskS", [128, 128], F32)
    maskI = din("maskI", [128, 128], F32)
    out = dout("out", [TPC, D], F32)

    ag0_in = dint("ag0_in", [D, TPC], BF16)
    ag0_out = dint("ag0_out", [NCORES * D, TPC], BF16)
    agw_in = {m: dint(f"agw_{m}_in", [NSL, D], BF16) for m in "gud"}
    agw_out = {m: dint(f"agw_{m}_out", [MLP, D], BF16) for m in "gud"}
    a2a_in = dint("a2a_in", [4096, TPC], BF16)
    a2a_out = dint("a2a_out", [4096, TPC], BF16)
    x2_sp = dint("x2_sp", [TPC, D], F32)
    dbg = {}
    for name in debug:
        dbg[name] = dout("dbg_" + name, DEBUG_SHAPES[name][0], DEBUG_SHAPES[name][1])

    p = Plan()
    stack = contextlib.ExitStack()
    with stack, allow_ici_all_to_all():
        build_body(nc, p, stack, dram, dbg)
        emit(nc, p, stack)
    return nc


DEBUG_SHAPES = {
    "lnx": ([NCORES * D, TPC], BF16),
    "lnxin": ([D, TPC], BF16),
    "convq": ([128, S], F32),
    "convk": ([128, S], F32),
    "convv": ([128, S], F32),
    "gbeta": ([8, S], F32),
    "yloc": ([512, TPC], F32),
    "x2": ([TPC, D], F32),
    "wdeq": ([NSL, D], F32),
    "inter": ([128, 512], F32),
}


def build_body(nc, p, stack, dram, dbg):
    import os
    STAGE = int(os.environ.get("KSTAGE", "3"))
    DEQ_ON = os.environ.get("KDEQ", "1") == "1"
    SCAN_ON = os.environ.get("KSCAN", "1") == "1"
    GATE_ON = os.environ.get("KGATE", "1") == "1"
    DEQN = int(os.environ.get("KDEQN", "192"))
    SCPART = int(os.environ.get("KSCPART", "3"))
    SC2 = int(os.environ.get("KSC2", "9"))
    DEQIL = os.environ.get("KDEQIL", "1") == "1"
    SC3 = os.environ.get("KSC3", "abc")
    NTTL = int(os.environ.get("KNTT", "99"))
    sb = stack.enter_context(nc.sbuf_tensor("arena", [128, 210944], U8))
    A = Arena(sb[:], 210944)
    banks = [Bank(stack.enter_context(nc.psum_tensor(f"pb{i}", [128, 512], F32)))
             for i in range(8)]

    pT, pV, pA = p.sem("pT"), p.sem("pV"), p.sem("pA")
    d_x = p.sem("d_x")
    d_misc = p.sem("d_misc")
    d_wq = p.sem("d_wq")
    d_lnx = [p.sem("d_lnx0"), p.sem("d_lnx1")]
    d_pk = p.sem("d_pk")
    d_sc = [p.sem("d_sc0"), p.sem("d_sc1")]
    d_ag0st = p.sem("d_ag0st")
    d_wst = [p.sem("d_wst0"), p.sem("d_wst1")]
    d_yst = [p.sem("d_yst0"), p.sem("d_yst1")]
    d_wo = p.sem("d_wo")
    d_yt = p.sem("d_yt")
    d_x2st = p.sem("d_x2st")
    d_wgl = [p.sem(f"d_wgl{i}") for i in range(4)]
    d_wul = [p.sem(f"d_wul{i}") for i in range(4)]
    d_wd = [p.sem(f"d_wd{i}") for i in range(4)]
    d_x2l = p.sem("d_x2l")
    d_outst = p.sem("d_outst")
    d_dbg = p.sem("d_dbg")
    c_ag0 = p.sem("c_ag0")
    c_w = {m: p.sem(f"c_{m}") for m in "gud"}
    c_a2a = p.sem("c_a2a")

    def bank_acquire(i):
        return banks[i].free

    def bank_release(i, milestone):
        banks[i].free = milestone if isinstance(milestone, list) else [milestone]

    ev_tog = [0]

    def evac(dst, src, waits, eng=None):
        if eng is None:
            ev_tog[0] ^= 1
            eng = "vector" if ev_tog[0] else "scalar"
        if eng == "vector":
            return p.op("vector", lambda e, d=dst, s=src: e.tensor_copy(d, s),
                        waits=waits, inc=pV)
        return p.op("scalar", lambda e, d=dst, s=src: e.copy(d, s),
                    waits=waits, inc=pA)

    CONSTS = [-2.0, 1.0, -6.0, 0.5, -0.5, -1.0, LN_EPS, RMS_EPS, L2_EPS,
              1.0 / 128.0, 1.0 / 2048.0, DK ** -0.5, 2048.0]
    cst = A.carve([128, len(CONSTS)], F32, "cst")
    for i, v in enumerate(CONSTS):
        nc.const_aps.register(F32, float(v), cst[:, i:i + 1])
    onesb = A.carve([128, 128], BF16, "onesb")
    onesf = A.carve([128, 128], F32, "onesf")
    identb_sb = A.carve([128, 128], BF16, "identb")
    identf = A.carve([8, 8], F32, "identf")
    maskS_sb = A.carve([128, 128], F32, "maskS")
    maskI_sb = A.carve([128, 128], F32, "maskI")
    normw_sb = A.carve([128, 1], F32, "normw")
    dtcol = A.carve([8, 2], F32, "dtcol")
    convw_sb = A.carve([128, 32], F32, "convw")

    for i, v in enumerate(CONSTS):
        p.op("vector", lambda e, i=i, v=v: e.memset(cst[:, i:i + 1], float(v)))
    p.op("vector", lambda e: e.memset(onesb[:], 1.0))
    cini = p.op("vector", lambda e: e.memset(onesf[:], 1.0), inc=pV)
    p.op("sync", lambda e: e.dma_start(out=identb_sb, in_=dram["identb"]), inc=d_misc, k=16)
    IDF_MS = None
    p.op("sync", lambda e: e.dma_start(out=maskS_sb, in_=dram["maskS"]), inc=d_misc, k=16)
    p.op("sync", lambda e: e.dma_start(out=maskI_sb, in_=dram["maskI"]), inc=d_misc, k=16)
    p.op("sync", lambda e: e.dma_start(out=normw_sb, in_=dram["normw"].unsqueeze(-1)),
         inc=d_misc, k=16)
    p.op("sync", lambda e: e.dma_start(
         out=convw_sb[:].rearrange("p (g t) -> p g t", t=4),
         in_=dram["convw"].rearrange("(g p) t -> p g t", p=128)), inc=d_misc, k=16)
    mz = p.op("vector", lambda e: e.memset(dtcol[0:4, :], 0.0), inc=pV)
    p.op("sync", lambda e: e.dma_start(out=dtcol[4:8, 0:1],
         in_=dram["dtba"][0:1, :].rearrange("a b -> b a")), inc=d_misc, k=16)
    p.op("sync", lambda e: e.dma_start(out=dtcol[4:8, 1:2],
         in_=dram["dtba"][1:2, :].rearrange("a b -> b a")), inc=d_misc, k=16)
    MISC_ALL = (d_misc, d_misc.n)
    IDF_MS = p.op("vector", lambda e: e.tensor_copy(identf, identb_sb[0:8, 0:8]),
                  waits=[MISC_ALL], inc=pV)
    m = p.op("scalar", lambda e: e.activation(dtcol[:, 1:2], dtcol[:, 1:2], AF.Exp),
             waits=[MISC_ALL, mz], inc=pA)
    m = p.op("vector", lambda e: e.tensor_scalar(dtcol[:, 1:2], dtcol[:, 1:2],
             -1.0, None, ALU.mult), waits=[m], inc=pV)
    NEGA_READY = m

    resmark = A.mark()
    wq_sb = A.carve([128, 16 * 1536], BF16, "wq")
    wba_sb = A.carve([128, 16 * 8], BF16, "wba")
    for fc in range(16):
        p.op("gpsimd", lambda e, fc=fc: e.dma_start(
            out=wq_sb[:, 1536 * fc:1536 * (fc + 1)],
            in_=dram["wq"][128 * fc:128 * (fc + 1), :]), inc=d_wq, k=16)
        p.op("gpsimd", lambda e, fc=fc: e.dma_start(
            out=wba_sb[:, 8 * fc:8 * (fc + 1)],
            in_=dram["wba"][128 * fc:128 * (fc + 1), :]), inc=d_wq, k=16)
    WQ_ALL = (d_wq, d_wq.n)

    ph0_mark = A.mark()
    xt = A.carve([128, D], F32, "xt")
    sq0 = A.carve([128, D], F32, "sq0")
    st0 = A.carve([128, 8], F32, "st0")
    lnb = A.carve([128, D], BF16, "lnb")
    tstage = A.carve([128, 16 * 128], BF16, "tstage")
    xt_free = None
    lnb_free = None
    tstage_free = None
    p0_m3 = None
    p0_m5 = None
    for i in range(8):
        ld = p.op("sync", lambda e, i=i: e.dma_start(
            out=xt, in_=dram["x_sl"][128 * i:128 * (i + 1), :]),
            waits=[xt_free] if xt_free else [], inc=d_x, k=16)
        w = [(d_x, d_x.n)]
        m1 = p.op("vector", lambda e: e.tensor_reduce(st0[:, 0:1], xt, AX.X, ALU.add),
                  waits=w + [p0_m5], inc=pV)
        m2_ = p.op("scalar", lambda e: e.activation(sq0, xt, AF.Square),
                   waits=w + [p0_m3], inc=pA)
        m3 = p.op("vector", lambda e: e.tensor_reduce(st0[:, 1:2], sq0, AX.X, ALU.add),
                  waits=[m2_], inc=pV)
        p0_m3 = m3
        mm1 = p.op("vector", lambda e: e.tensor_scalar(st0[:, 2:3], st0[:, 0:1], 1.0 / 2048.0, None, ALU.mult),
                   waits=[m1], inc=pV)
        mm2 = p.op("vector", lambda e: e.tensor_scalar(st0[:, 3:4], st0[:, 1:2], 1.0 / 2048.0, None, ALU.mult),
                   waits=[m3], inc=pV)
        mm3 = p.op("vector", lambda e: e.tensor_tensor(st0[:, 4:5], st0[:, 2:3], st0[:, 2:3], ALU.mult),
                   waits=[mm1], inc=pV)
        m4 = p.op("vector", lambda e: e.tensor_tensor(st0[:, 5:6], st0[:, 3:4], st0[:, 4:5], ALU.subtract),
                  waits=[mm2, mm3], inc=pV)
        m5 = p.op("scalar", lambda e: e.activation(st0[:, 6:7], st0[:, 5:6], AF.Ln, bias=LN_EPS),
                  waits=[m4], inc=pA)
        p0_m5 = m5
        m6 = p.op("scalar", lambda e: e.activation(st0[:, 7:8], st0[:, 6:7], AF.Exp, scale=-0.5),
                  waits=[m5], inc=pA)
        xt_free = p.op("vector", lambda e: e.tensor_scalar(
            lnb, xt, st0[:, 2:3], st0[:, 7:8], ALU.subtract, ALU.mult),
            waits=[m6, mm1] + ([lnb_free] if lnb_free else []), inc=pV)
        evms = []
        for c in range(16):
            bk = 2 + (c & 1)
            pw = [xt_free, MISC_ALL]
            pw.extend(banks[bk].free)
            if tstage_free:
                pw.append(tstage_free)
            mt = p.op("tensor", lambda e, c=c, bk=bk: e.transpose(
                banks[bk].t[:].bitcast(BF16)[:, 0:128], lnb[:, 128 * c:128 * (c + 1)], identb_sb),
                waits=pw, inc=pT)
            me = evac(tstage[:, 128 * c:128 * (c + 1)], banks[bk].t[:].bitcast(BF16)[:, 0:128], [mt])
            bank_release(bk, me)
            evms.append(me)
        lnb_free = (pT, pT.n)
        stm = p.op("gpsimd", lambda e, i=i: e.dma_start(
            out=dram["ag0_in"].rearrange("(c p) t -> p c t", p=128)[:, :, 128 * i:128 * (i + 1)],
            in_=tstage[:].rearrange("p (c q) -> p c q", q=128)),
            waits=evms, inc=d_ag0st, k=16)
        tstage_free = (d_ag0st, d_ag0st.n)
    if "lnxin" in dbg:
        p.op("gpsimd", lambda e: e.dma_start(out=dbg["lnxin"], in_=dram["ag0_in"]),
             waits=[(d_ag0st, d_ag0st.n)], inc=d_dbg, k=16)
    AG0STORES = (d_ag0st, d_ag0st.n)
    p.op("gpsimd", lambda e: e.collective_compute(
        "AllGather", ALU.bypass, replica_groups=[list(range(NCORES))],
        ins=[dram["ag0_in"].opt()], outs=[dram["ag0_out"].opt()]),
        waits=[AG0STORES], inc=c_ag0, k=None)
    AG0_DONE = (c_ag0, 1)
    if "lnx" in dbg:
        p.op("gpsimd", lambda e: e.dma_start(out=dbg["lnx"], in_=dram["ag0_out"]),
             waits=[AG0_DONE], inc=d_dbg, k=16)
    if STAGE < 1:
        p.op("gpsimd", lambda e: e.nop(), waits=[AG0_DONE, (d_dbg, d_dbg.n)])
        return

    A.reset(ph0_mark)
    KH = 512
    dq_pk = A.carve([128, 2048], U8, "dq_pk")
    dq_sc = [A.carve([128, KH], F32, f"dq_sc{i}") for i in range(2)]
    dq_m8 = A.carve([128, KH], U8, "dq_m8")
    dq_s8 = dq_m8
    dq_mf = [A.carve([128, KH], BF16, f"dq_mf{i}") for i in range(2)]
    dq_sg = [A.carve([128, KH], BF16, f"dq_sg{i}") for i in range(2)]
    dq_r1 = [A.carve([128, KH], BF16, f"dq_r1{i}") for i in range(2)]
    dq_r2 = [A.carve([128, KH], BF16, f"dq_r2{i}") for i in range(2)]
    dq_wf = [A.carve([128, KH], BF16, "dq_wf") for _ in range(2)]
    dq_state = {"pk_free": None, "sc_free": [None, None],
                "sc_ready": [None, None], "mf_free": [None, None],
                "sg_free": [None, None], "r1_free": [None, None],
                "r2_free": [None, None], "wf_free": [None, None],
                "wtog": 0, "utog": 0, "tile_loaded": None,
                "pk_last_read": None, "pend": None}

    PKMAP = {"g": "pkg", "u": "pku", "d": "pkd"}
    SCMAP = {"g": "scg", "u": "scu", "d": "scd"}

    def deq_A(m, ti, half, colh):
        ds = dq_state
        tog = ds["utog"]
        if half == 0 and colh == 0:
            w = [AG0STORES] + ([ds["pk_last_read"]] if ds["pk_last_read"] else [])
            ds["tile_loaded"] = p.op("sync", lambda e, m=m, ti=ti: e.dma_start(
                out=dq_pk, in_=dram[PKMAP[m]][128 * ti:128 * (ti + 1), :]),
                waits=w, inc=d_pk, k=16)
        if half == 0:
            ctog = colh % 2
            w = [ds["sc_free"][ctog]] if ds["sc_free"][ctog] else []
            for rr_ in range(8):
                p.op("sync", lambda e, rr_=rr_, m=m, ti=ti, colh=colh, ctog=ctog: e.dma_start(
                    out=dq_sc[ctog][16 * rr_:16 * (rr_ + 1), :],
                    in_=dram[SCMAP[m]][8 * ti + rr_:8 * ti + rr_ + 1,
                                       KH * colh:KH * (colh + 1)].broadcast_to([16, KH])),
                    waits=w if rr_ == 0 else [], inc=d_sc[ctog], k=16)
            ds["sc_ready"][ctog] = (d_sc[ctog], d_sc[ctog].n)
        pk = dq_pk[:, KH * colh:KH * (colh + 1)]
        w0 = [ds["tile_loaded"]]
        if ds["mf_free"][tog]:
            w0.append(ds["mf_free"][tog])
        if half == 0:
            mu8 = p.op("vector", lambda e, pk=pk: e.tensor_single_scalar(
                dq_m8, pk, 7, ALU.bitwise_and), waits=w0, inc=pV)
        else:
            mu8 = p.op("vector", lambda e, pk=pk: e.tensor_scalar(
                dq_m8, pk, 4, 7, ALU.logical_shift_right, ALU.bitwise_and),
                waits=w0, inc=pV)
        mfm = p.op("vector", lambda e, tog=tog: e.tensor_copy(dq_mf[tog], dq_m8),
                   waits=[mu8], inc=pV)
        if half == 0:
            ms8 = p.op("vector", lambda e, pk=pk: e.tensor_scalar(
                dq_s8, pk, 3, 1, ALU.logical_shift_right, ALU.bitwise_and),
                waits=[mfm], inc=pV)
        else:
            ms8 = p.op("vector", lambda e, pk=pk: e.tensor_scalar(
                dq_s8, pk, 7, 1, ALU.logical_shift_right, ALU.bitwise_and),
                waits=[mfm], inc=pV)
        ds["pk_last_read"] = ms8
        w = [ms8] + ([ds["sg_free"][tog]] if ds["sg_free"][tog] else [])
        mcs8 = p.op("vector", lambda e, tog=tog: e.tensor_copy(dq_sg[tog], dq_s8),
                    waits=w, inc=pV)
        msg2 = p.op("vector", lambda e, tog=tog: e.tensor_scalar(
            dq_sg[tog], dq_sg[tog], -2.0, 1.0, ALU.mult, ALU.add),
            waits=[mcs8], inc=pV)
        w = [mfm] + ([ds["r1_free"][tog]] if ds["r1_free"][tog] else [])
        r1m = p.op("scalar", lambda e, tog=tog: e.activation(
            dq_r1[tog], dq_mf[tog], AF.Relu, scale=0.5, bias=-2.0), waits=w, inc=pA)
        w = [ds["r2_free"][tog]] if ds["r2_free"][tog] else []
        r2m = p.op("scalar", lambda e, tog=tog: e.activation(
            dq_r2[tog], dq_mf[tog], AF.Relu, bias=-6.0), waits=w, inc=pA)
        ds["pend"] = (m, ti, half, colh, tog, r2m, msg2)
        ds["utog"] ^= 1

    def deq_B():
        ds = dq_state
        if ds["pend"] is None:
            return
        m, ti, half, colh, tog, r2m, msg2 = ds["pend"]
        ds["pend"] = None
        ctog = colh % 2
        mq1 = p.op("vector", lambda e, tog=tog: e.tensor_tensor(
            dq_r1[tog], dq_r1[tog], dq_r2[tog], ALU.add), waits=[r2m], inc=pV)
        ds["r2_free"][tog] = mq1
        mq2 = p.op("vector", lambda e, tog=tog: e.scalar_tensor_tensor(
            dq_r1[tog], dq_mf[tog], 0.5, dq_r1[tog], ALU.mult, ALU.add),
            waits=[mq1], inc=pV)
        ds["mf_free"][tog] = mq2
        mq3 = p.op("vector", lambda e, tog=tog: e.tensor_tensor(
            dq_r1[tog], dq_r1[tog], dq_sg[tog], ALU.mult),
            waits=[mq2, msg2], inc=pV)
        ds["sg_free"][tog] = mq3
        wt_ = ds["wtog"]
        wf = dq_wf[wt_]
        w = [mq3, ds["sc_ready"][ctog]]
        if ds["wf_free"][wt_]:
            w.append(ds["wf_free"][wt_])
        wfm = p.op("vector", lambda e, wf=wf, tog=tog, ctog=ctog: e.tensor_tensor(
            wf, dq_r1[tog], dq_sc[ctog], ALU.mult), waits=w, inc=pV)
        ds["r1_free"][tog] = wfm
        if half == 1:
            ds["sc_free"][ctog] = wfm
        row0 = 128 * (8 * half + ti) if m in "gu" else 256 * ti + 128 * half
        stm = p.op("gpsimd", lambda e, m=m, row0=row0, colh=colh, wf=wf: e.dma_start(
            out=dram[f"agw_{m}_in"][row0:row0 + 128, KH * colh:KH * (colh + 1)],
            in_=wf), waits=[wfm], inc=d_wst[wt_], k=16)
        ds["wf_free"][wt_] = (d_wst[wt_], d_wst[wt_].n)
        ds["wtog"] ^= 1

    def emit_deq_half(m, ti, half, colh):
        prev = dq_state["pend"]
        dq_state["pend"] = None
        deq_A(m, ti, half, colh)
        new = dq_state["pend"]
        dq_state["pend"] = prev
        deq_B()
        dq_state["pend"] = new

    DEQ_UNITS = [(m, ti, half, colh) for m in "gud" for ti in range(8)
                 for colh in range(4) for half in range(2)]
    AGW_AFTER = {}

    lnxb = [A.carve([128, 16 * 512], BF16, "lnx0"), A.carve([128, 16 * 512], BF16, "lnx1")]
    raw = [[A.carve([128, 515], BF16, f"raw{ct}_{bu}") for bu in range(1 if ct % 6 >= 4 else 2)]
           for ct in range(12)]
    ba_f = [A.carve([8, 512], F32, f"baf{bu}") for bu in range(2)]
    convo = [A.carve([128, 512], BF16, f"convo{cg}") for cg in range(8)]
    qn = [A.carve([128, 512], BF16, f"qn{kh}") for kh in range(2)]
    kn = [A.carve([128, 512], BF16, f"kn{kh}") for kh in range(2)]
    sqg2 = [A.carve([128, 512], BF16, f"sqg{i}") for i in range(2)]
    sqg = sqg2[0]
    rrow2 = [A.carve([1, 512], F32, f"rrow{i}") for i in range(2)]
    rrow = rrow2[0]
    bsig = A.carve([8, 512], F32, "bsig")
    spt = A.carve([8, 512], F32, "spt")
    g8 = A.carve([8, 512], F32, "g8")
    gcum = A.carve([8, 512], F32, "gcum")
    kkb = [[A.carve([128, 128], BF16, f"kkb{s}{kh}") for kh in range(2)] for s in range(2)]
    kqb = [[A.carve([128, 128], BF16, f"kqb{s}{kh}") for kh in range(2)] for s in range(2)]
    cb = [A.carve([128, 8], F32, f"cb{s}") for s in range(2)]
    cgc = [A.carve([128, 8], F32, f"cgc{s}") for s in range(2)]
    cex = [A.carve([128, 4], F32, f"cex{s}") for s in range(2)]
    cnb = [A.carve([128, 4], F32, f"cnb{s}") for s in range(2)]
    kcol4 = A.carve([128, 16], F32, "kcol4")
    dsc_sh = A.carve([128, 512], F32, "dsc_sh")
    d4_sh = A.carve([128, 512], F32, "d4_sh")
    dl4_sh = A.carve([128, 512], BF16, "dl4_sh")
    dsc_free = [None]
    vtm4_free = [None, None]
    d4_free = [None]
    dl4_free = [None]
    ktm4_free = [None, None]
    CS = []
    for s in range(2):
        d = dict(free=None)
        d["grow4"] = A.carve([1, 512], F32, f"grow4{s}")
        d["nbrow4"] = A.carve([1, 512], BF16, f"nbrow4{s}")
        d["erow4"] = A.carve([1, 512], BF16, f"erow4{s}")
        d["D4"] = d4_sh
        d["dsc"] = dsc_sh
        d["dL4"] = dl4_sh
        d["ktm4"] = A.carve([128, 512], BF16, f"ktm4_{s}")
        d["vtm4"] = A.carve([128, 512], BF16, f"vtm4_{s}")
        for nm in ("dS4", "dI4", "N4", "NL4", "Mk4", "A24", "B24",
                   "rhs4", "U4"):
            d[nm] = A.carve([128, 512], BF16, f"{nm}{s}")
        d["wt4"] = d["B24"]
        d["qe4"] = d["A24"]
        CS.append(d)
    ost = [A.carve([128, 512], BF16, f"ost{j}") for j in range(4)]
    xgt2 = [A.carve([128, 512], BF16, f"xgt{i}") for i in range(2)]
    zs = [A.carve([128, 512], BF16, f"zs{j}") for j in range(4)]
    yt = [A.carve([128, 512], BF16, f"yt{i}") for i in range(2)]
    Sf = A.carve([128, 1024], F32, "Sf")
    Sb = A.carve([128, 1024], BF16, "Sb")
    ph1_high = A.off

    mzs = p.op("vector", lambda e: e.memset(Sf, 0.0), inc=pV)
    mzb = p.op("vector", lambda e: e.memset(Sb, 0.0), inc=pV)
    st_ms = [mzb] * 8
    stf_ms = [mzs] * 8

    CONV_MAP = [(0, 0), (1, 2), (2, 4), (3, 5), (6, 1), (7, 3), (8, 6), (9, 7)]
    lnx_free = [None, None]
    lnx_ready = [None, None]
    raw_ready = [[None, None] for _ in range(12)]
    raw_free = [[None, None] for _ in range(12)]
    baf_free = [None, None]
    convo_ready = [None] * 8
    convo_free = [None] * 8
    qn_ms = [None, None]
    kn_ms = [None, None]
    qn_free = [None, None]
    kn_free = [None, None]
    yt_free = [None, None]
    ba_ms = None
    bsig_free = [None]
    gcum_free = [None]
    l2_sq_free = [None, None]
    l2_rr_free = [None, None]
    CSLOT = [{"free": None}, {"free": None}]
    kcol_free = [None, None]
    kkq_free = [None, None]
    ba_act_ms = None
    mg8_w = [None]
    ost_ms = [[None] * 4 for _ in range(4)]
    ost_free = [None] * 4
    zs_free = [None] * 4
    xgt_free = [None, None]
    chd = [0]
    deq_idx = [0]

    def tiny_bank():
        chd[0] += 1
        return 2 + (chd[0] & 1)

    deq_goal = [0]
    pend_scan = [None]

    def deq_tick(n):
        for _ in range(max(n, 0)):
            if deq_idx[0] < min(DEQN, deq_goal[0]):
                m_, ti_, half_, colh_ = DEQ_UNITS[deq_idx[0]]
                emit_deq_half(m_, ti_, half_, colh_)
                deq_idx[0] += 1

    for b in range(B):
        for tt in range(NTT):
            gti = NTT * b + tt
            if gti >= NTTL:
                continue
            bu = gti % 2
            r = gti // 2
            cb_ = (gti % 2) * 512
            def emit_lnx(g2):
                bu2 = g2 % 2
                r2 = g2 // 2
                cb2 = (g2 % 2) * 512
                w = [AG0_DONE] + ([lnx_free[bu2]] if lnx_free[bu2] else [])
                lnx_ready[bu2] = p.op("sync", lambda e, r2=r2, cb2=cb2, bu2=bu2: e.dma_start(
                    out=lnxb[bu2][:].rearrange("p (c q) -> p c q", q=512),
                    in_=dram["ag0_out"].rearrange("(r c p) t -> r p c t", r=8, p=128)[r2][:, :, cb2:cb2 + 512]),
                    waits=w, inc=d_lnx[bu2], k=16)
            if gti == 0:
                emit_lnx(0)
            for ct in range(13):
                bk = ct % 2
                if ct < 12:
                    kh, sub = ct // 6, ct % 6
                    coff = 768 * kh + 128 * sub
                    cw = 128
                    pv = banks[bk].t[:]
                else:
                    cw = 8
                    pv = banks[bk].t[0:8, :]
                for fc in range(16):
                    w = []
                    if fc == 0:
                        w = [lnx_ready[bu], WQ_ALL]
                        w.extend(banks[bk].free)
                    if ct < 12:
                        lh = wq_sb[:, 1536 * fc + coff:1536 * fc + coff + cw]
                    else:
                        lh = wba_sb[:, 8 * fc:8 * fc + 8]
                    inc = pT if fc == 15 else None
                    mm = p.op("tensor", lambda e, lh=lh, pv=pv, fc=fc, bu=bu: e.matmul(
                        pv, lh, lnxb[bu][:, 512 * fc:512 * (fc + 1)],
                        start=(fc == 0), stop=(fc == 15)), waits=w, inc=inc)
                if ct == 12:
                    lnx_free[bu] = mm
                if ct < 12:
                    rbu = 0 if sub >= 4 else bu
                    w = [mm]
                    if raw_free[ct][rbu]:
                        w.append(raw_free[ct][rbu])
                    if sub < 4:
                        if tt == 0:
                            p.op("vector", lambda e, ct=ct, bu=bu: e.memset(raw[ct][bu][:, 0:3], 0.0),
                                 waits=[raw_free[ct][bu]] if raw_free[ct][bu] else [])
                        else:
                            mh = p.op("vector", lambda e, ct=ct, bu=bu: e.tensor_copy(
                                raw[ct][bu][:, 0:3], raw[ct][1 - bu][:, 512:515]),
                                waits=[raw_ready[ct][1 - bu]] + ([raw_free[ct][bu]] if raw_free[ct][bu] else []),
                                inc=pV)
                            raw_free[ct][1 - bu] = mh
                    raw_ready[ct][rbu] = evac(raw[ct][rbu][:, 3:515], pv, w)
                    bank_release(bk, raw_ready[ct][rbu])
                else:
                    w = [mm]
                    if baf_free[bu]:
                        w.append(baf_free[bu])
                    bam = evac(ba_f[bu], pv, w)
                    bank_release(bk, bam)
            if gti + 1 < min(NTTL, 2 * NTT):
                emit_lnx(gti + 1)
            if pend_scan[0]:
                for _ in pend_scan[0]:
                    pass
                pend_scan[0] = None
            for (ct, cg) in CONV_MAP:
                src = raw[ct][bu]
                tmp = convo[cg]
                w = [raw_ready[ct][bu], MISC_ALL]
                if convo_free[cg]:
                    w.append(convo_free[cg])
                mt = p.op("vector", lambda e, src=src, tmp=tmp, cg=cg: e.tensor_scalar(
                    tmp, src[:, 0:512], convw_sb[:, 4 * cg:4 * cg + 1], None, ALU.mult),
                    waits=w, inc=pV)
                for tap in (1, 2, 3):
                    mt = p.op("vector", lambda e, src=src, tmp=tmp, cg=cg, tap=tap: e.scalar_tensor_tensor(
                        tmp, src[:, tap:tap + 512], convw_sb[:, 4 * cg + tap:4 * cg + tap + 1],
                        tmp, ALU.mult, ALU.add), waits=[mt], inc=pV)
                raw_free[ct][bu] = mt
                convo_ready[cg] = p.op("scalar", lambda e, tmp=tmp, cg=cg: e.activation(
                    convo[cg], tmp, AF.Silu), waits=[mt], inc=pA)
            zs_ms = [None] * 4
            for j in range(4):
                kh = j // 2
                ct_z = 6 * kh + 4 + (j % 2)
                w = [raw_ready[ct_z][0]] + ([zs_free[j]] if zs_free[j] else [])
                zs_ms[j] = p.op("scalar", lambda e, ct_z=ct_z, j=j: e.activation(
                    zs[j], raw[ct_z][0][:, 3:515], AF.Silu), waits=w, inc=pA)
                raw_free[ct_z][0] = zs_ms[j]
            L2C = [(0, qn, qn_ms, qn_free, 0), (2, kn, kn_ms, kn_free, 0),
                   (1, qn, qn_ms, qn_free, 1), (3, kn, kn_ms, kn_free, 1)]
            for pr in range(2):
                stl = [{} for _ in range(2)]
                for u in range(2):
                    cg, dstl, msl, freel, kh = L2C[2 * pr + u]
                    tg = u
                    stl[u]["msq"] = p.op("vector", lambda e, cg=cg, tg=tg: e.tensor_tensor(
                        sqg2[tg], convo[cg], convo[cg], ALU.mult),
                        waits=[convo_ready[cg], l2_sq_free[tg]], inc=pV)
                for u in range(2):
                    tg = u
                    bq = 6 + tg
                    stl[u]["mcs"] = p.op("tensor", lambda e, tg=tg, bq=bq: e.matmul(
                        banks[bq].t[0:1, :], onesb[:, 0:1], sqg2[tg], start=True, stop=True),
                        waits=[stl[u]["msq"], cini] + list(banks[bq].free), inc=pT)
                    l2_sq_free[tg] = stl[u]["mcs"]
                for u in range(2):
                    tg = u
                    bq = 6 + tg
                    mln = p.op("scalar", lambda e, tg=tg, bq=bq: e.activation(
                        rrow2[tg], banks[bq].t[0:1, :], AF.Ln, bias=L2_EPS),
                        waits=[stl[u]["mcs"], l2_rr_free[tg]], inc=pA)
                    bank_release(bq, mln)
                    stl[u]["mln"] = mln
                for u in range(2):
                    cg = L2C[2 * pr + u][0]
                    tg = u
                    mex = p.op("scalar", lambda e, tg=tg: e.activation(
                        rrow2[tg], rrow2[tg], AF.Exp, scale=-0.5),
                        waits=[stl[u]["mln"]], inc=pA)
                    if cg < 2:
                        mex = p.op("vector", lambda e, tg=tg: e.tensor_scalar(
                            rrow2[tg], rrow2[tg], DK ** -0.5, None, ALU.mult),
                            waits=[mex], inc=pV)
                    stl[u]["mex"] = mex
                for u in range(2):
                    tg = u
                    bq = 6 + tg
                    stl[u]["mbc"] = p.op("tensor", lambda e, tg=tg, bq=bq: e.matmul(
                        banks[bq].t[:], onesf[0:1, :], rrow2[tg], start=True, stop=True),
                        waits=[stl[u]["mex"]] + list(banks[bq].free), inc=pT)
                    l2_rr_free[tg] = stl[u]["mbc"]
                for u in range(2):
                    cg, dstl, msl, freel, kh = L2C[2 * pr + u]
                    tg = u
                    bq = 6 + tg
                    w2 = [stl[u]["mbc"]]
                    if freel[kh]:
                        w2.append(freel[kh])
                    msl[kh] = p.op("vector", lambda e, cg=cg, dstl=dstl, kh=kh, bq=bq: e.tensor_tensor(
                        dstl[kh], convo[cg], banks[bq].t[:], ALU.mult), waits=w2, inc=pV)
                    bank_release(bq, msl[kh])
                    convo_free[cg] = msl[kh]
            mbs = p.op("scalar", lambda e, bu=bu: e.activation(bsig, ba_f[bu], AF.Sigmoid),
                       waits=[bam] + ([bsig_free[0]] if bsig_free[0] else []), inc=pA)
            msp = p.op("vector", lambda e, bu=bu: e.tensor_scalar(
                spt, ba_f[bu], dtcol[:, 0:1], None, ALU.add),
                waits=[bam, MISC_ALL, ba_act_ms], inc=pV)
            baf_free[bu] = msp
            mb1 = p.op("scalar", lambda e: e.activation(g8, spt, AF.Abs),
                       waits=[msp, mg8_w[0]], inc=pA)
            mrt = p.op("scalar", lambda e: e.activation(spt, spt, AF.Relu), waits=[mb1], inc=pA)
            mb2 = p.op("scalar", lambda e: e.activation(g8, g8, AF.Exp, scale=-1.0),
                       waits=[mb1], inc=pA)
            mab = p.op("scalar", lambda e: e.activation(g8, g8, AF.Ln, bias=1.0),
                       waits=[mb2], inc=pA)
            ba_act_ms = mab
            msp2 = p.op("vector", lambda e: e.tensor_tensor(spt, spt, g8, ALU.add),
                        waits=[mab, mrt], inc=pV)
            mg8 = p.op("vector", lambda e: e.tensor_scalar(
                g8, spt, dtcol[:, 1:2], None, ALU.mult),
                waits=[NEGA_READY, mab, msp2], inc=pV)
            for c in range(4):
                ba_ms = p.op("vector", lambda e, c=c: e.tensor_tensor_scan(
                    gcum[:, 128 * c:128 * (c + 1)], onesf[0:8, 0:128],
                    g8[:, 128 * c:128 * (c + 1)], 0.0, ALU.mult, ALU.add),
                    waits=([gcum_free[0], mg8] if c == 0 else [mg8]), inc=pV)
            mg8_w[0] = ba_ms

            def do_scan(gti=gti, b=b, tt=tt, bu=bu, mbs=mbs,
                        ba_ms=ba_ms, zs_ms=zs_ms):
                if gti < 12 and DEQ_ON:
                    deq_goal[0] = 16 * (gti + 1)
                def sc_new(c):
                    st = {"c": c, "par": c & 1}
                    st["slc"] = c % 2
                    st["ssl"] = (gti * 4 + c) % 2
                    st["csl"] = CSLOT[st["slc"]]
                    st["cs"] = CS[c % 2]
                    st["csw"] = [st["cs"]["free"]] if st["cs"]["free"] else []
                    st["bkc"] = 2 + (c & 1)
                    st["bkb"] = 2 + ((c + 1) & 1)
                    st["kcol"] = kcol4[:, 8 * st["par"]:8 * st["par"] + 8]
                    return st

                def sc_early(st):
                    c, par, slc, ssl = st["c"], st["par"], st["slc"], st["ssl"]
                    cs, csl, csw, bkc, bkb = st["cs"], st["csl"], st["csw"], st["bkc"], st["bkb"]
                    bkt = par
                    w = [mbs, ba_ms] + ([csl["free"]] if csl["free"] else []) \
                        + list(banks[bkt].free)
                    mt1 = p.op("tensor", lambda e, c=c, bkt=bkt: e.transpose(
                        banks[bkt].t[:, 0:8], bsig[:, 128 * c:128 * (c + 1)], identf),
                        waits=w + [IDF_MS], inc=pT)
                    bsig_free[0] = mt1
                    mt2 = p.op("tensor", lambda e, c=c, bkt=bkt: e.transpose(
                        banks[bkt].t[:, 8:16], gcum[:, 128 * c:128 * (c + 1)], identf),
                        waits=[ba_ms], inc=pT)
                    gcum_free[0] = mt2
                    me1 = evac(cb[slc], banks[bkt].t[:, 0:8], [mt2], eng="scalar")
                    me2 = evac(cgc[slc], banks[bkt].t[:, 8:16], [mt2], eng="scalar")
                    bank_release(bkt, [me1, me2])
                    st["me1"], st["me2"] = me1, me2
                    st["mex_"] = p.op("scalar", lambda e, slc=slc: e.activation(
                        cex[slc], cgc[slc][:, 4:8], AF.Exp), waits=[me2], inc=pA)
                    mnb = p.op("vector", lambda e, slc=slc: e.tensor_scalar(
                        cnb[slc], cb[slc][:, 0:4], -1.0, None, ALU.mult), waits=[me1], inc=pV)
                    st["mnb"] = mnb
                    mge = None
                    for j in range(4):
                        wj = [ba_ms] + csw + (list(banks[bkc].free) if j == 0 else [])
                        mge = p.op("tensor", lambda e, j=j, c=c, bkc=bkc: e.matmul(
                            banks[bkc].t[0:1, 128 * j:128 * (j + 1)],
                            identf[:, 4 + j:5 + j], gcum[0:8, 128 * c:128 * (c + 1)],
                            start=True, stop=True), waits=wj, inc=pT)
                        gcum_free[0] = mge
                    mgre = evac(cs["grow4"], banks[bkc].t[0:1, 0:512], [mge] + csw)
                    bank_release(bkc, mgre)
                    st["mgre"] = mgre
                    mbe = None
                    for j in range(4):
                        wj = [mbs] + (list(banks[bkb].free) if j == 0 else [])
                        mbe = p.op("tensor", lambda e, j=j, c=c, bkb=bkb: e.matmul(
                            banks[bkb].t[0:1, 128 * j:128 * (j + 1)],
                            identf[:, j:j + 1], bsig[0:8, 128 * c:128 * (c + 1)],
                            start=True, stop=True), waits=wj, inc=pT)
                        bsig_free[0] = mbe
                    mnbr = p.op("vector", lambda e, cs=cs, bkb=bkb: e.tensor_scalar(
                        cs["nbrow4"], banks[bkb].t[0:1, 0:512], -1.0, None, ALU.mult),
                        waits=[mbe] + csw, inc=pV)
                    bank_release(bkb, mnbr)
                    st["mnbr"] = mnbr
                    st["mer"] = p.op("scalar", lambda e, cs=cs: e.activation(
                        cs["erow4"], cs["grow4"], AF.Exp), waits=[mgre] + csw, inc=pA)
                    kkms, kqms = [None, None], [None, None]
                    for kh in range(2):
                        bkk = 4 + 2 * par + kh
                        w = [kn_ms[kh], kkq_free[ssl]] + list(banks[bkk].free)
                        mk1 = p.op("tensor", lambda e, kh=kh, c=c, bkk=bkk: e.matmul(
                            banks[bkk].t[:, 0:128], kn[kh][:, 128 * c:128 * (c + 1)],
                            kn[kh][:, 128 * c:128 * (c + 1)], start=True, stop=True),
                            waits=w, inc=pT)
                        mk2 = p.op("tensor", lambda e, kh=kh, c=c, bkk=bkk: e.matmul(
                            banks[bkk].t[:, 128:256], kn[kh][:, 128 * c:128 * (c + 1)],
                            qn[kh][:, 128 * c:128 * (c + 1)], start=True, stop=True),
                            waits=[qn_ms[kh]], inc=pT)
                        engk = "vector" if kh == 0 else "scalar"
                        kkms[kh] = evac(kkb[ssl][kh], banks[bkk].t[:, 0:128], [mk2], eng=engk)
                        kqms[kh] = evac(kqb[ssl][kh], banks[bkk].t[:, 128:256], [mk2], eng=engk)
                        bank_release(bkk, [kkms[kh], kqms[kh]])
                    st["kkms"], st["kqms"] = kkms, kqms
                    mbc1 = p.op("tensor", lambda e, cs=cs, bkc=bkc: e.matmul(
                        banks[bkc].t[:], onesf[0:1, :], cs["grow4"], start=True, stop=True),
                        waits=[mgre] + list(banks[bkc].free), inc=pT)
                    gview = cgc[slc][:, 4:8].unsqueeze(-1).broadcast_to([128, 4, 128])
                    mD = p.op("vector", lambda e, cs=cs, gview=gview, bkc=bkc: e.tensor_tensor(
                        cs["D4"].rearrange("p (a b) -> p a b", a=4),
                        banks[bkc].t[:].rearrange("p (a b) -> p a b", a=4), gview, ALU.subtract),
                        waits=[mbc1, me2, d4_free[0]] + csw, inc=pV)
                    bank_release(bkc, mD)
                    mskSv = maskS_sb.unsqueeze(1).broadcast_to([128, 4, 128])
                    mskIv = maskI_sb.unsqueeze(1).broadcast_to([128, 4, 128])
                    mds_ = p.op("vector", lambda e, cs=cs, mskSv=mskSv: e.tensor_tensor(
                        cs["dsc"].rearrange("p (a b) -> p a b", a=4),
                        cs["D4"].rearrange("p (a b) -> p a b", a=4), mskSv, ALU.add),
                        waits=[MISC_ALL, mD, dsc_free[0]], inc=pV)
                    mdS = p.op("scalar", lambda e, cs=cs: e.activation(
                        cs["dS4"], cs["dsc"], AF.Exp), waits=[mds_], inc=pA)
                    mdi_ = p.op("vector", lambda e, cs=cs, mskIv=mskIv: e.tensor_tensor(
                        cs["dsc"].rearrange("p (a b) -> p a b", a=4),
                        cs["D4"].rearrange("p (a b) -> p a b", a=4), mskIv, ALU.add),
                        waits=[mdS], inc=pV)
                    mdI = p.op("scalar", lambda e, cs=cs: e.activation(
                        cs["dI4"], cs["dsc"], AF.Exp), waits=[mdi_], inc=pA)
                    dsc_free[0] = mdI
                    d4_free[0] = mdi_
                    st["mdS"], st["mdI"] = mdS, mdI
                    mtl = None
                    for j in range(4):
                        wj = [mdS] + (list(banks[bkb].free) if j == 0 else [])
                        mtl = p.op("tensor", lambda e, cs=cs, j=j, bkb=bkb: e.transpose(
                            banks[bkb].t[:].bitcast(BF16)[:, 128 * j:128 * (j + 1)],
                            cs["dS4"][:, 128 * j:128 * (j + 1)], identb_sb), waits=wj, inc=pT)
                    mdL = evac(cs["dL4"], banks[bkb].t[:].bitcast(BF16)[:, 0:512],
                               [mtl, dl4_free[0]])
                    bank_release(bkb, mdL)
                    mbb = p.op("tensor", lambda e, cs=cs, bkc=bkc: e.matmul(
                        banks[bkc].t[:], onesb[0:1, :], cs["nbrow4"], start=True, stop=True),
                        waits=[mnbr] + list(banks[bkc].free), inc=pT)
                    mN = None
                    for kh in range(2):
                        kkv = kkb[ssl][kh].unsqueeze(1).broadcast_to([128, 2, 128])
                        mN = p.op("vector", lambda e, cs=cs, kkv=kkv, kh=kh: e.tensor_tensor(
                            cs["N4"][:, 256 * kh:256 * (kh + 1)].rearrange("p (a b) -> p a b", a=2),
                            kkv, cs["dS4"][:, 256 * kh:256 * (kh + 1)].rearrange("p (a b) -> p a b", a=2),
                            ALU.mult), waits=[kkms[kh], mdS, mN], inc=pV)
                    mN = p.op("vector", lambda e, cs=cs, bkc=bkc: e.tensor_tensor(
                        cs["N4"], cs["N4"], banks[bkc].t[:], ALU.mult), waits=[mbb, mN], inc=pV)
                    bank_release(bkc, mN)
                    mNL = None
                    for kh in range(2):
                        kkv = kkb[ssl][kh].unsqueeze(1).broadcast_to([128, 2, 128])
                        mNL = p.op("vector", lambda e, cs=cs, kkv=kkv, kh=kh: e.tensor_tensor(
                            cs["NL4"][:, 256 * kh:256 * (kh + 1)].rearrange("p (a b) -> p a b", a=2),
                            kkv, cs["dL4"][:, 256 * kh:256 * (kh + 1)].rearrange("p (a b) -> p a b", a=2),
                            ALU.mult), waits=[kkms[kh], mdL, mNL], inc=pV)
                    nbv = cnb[slc].unsqueeze(-1).broadcast_to([128, 4, 128])
                    dl4_free[0] = mNL
                    mNL = p.op("vector", lambda e, cs=cs, nbv=nbv: e.tensor_tensor(
                        cs["NL4"].rearrange("p (a b) -> p a b", a=4),
                        cs["NL4"].rearrange("p (a b) -> p a b", a=4), nbv, ALU.mult),
                        waits=[st["mnb"], mNL], inc=pV)
                    idv = identb_sb.unsqueeze(1).broadcast_to([128, 4, 128])
                    mMk = p.op("vector", lambda e, cs=cs, idv=idv: e.tensor_tensor(
                        cs["Mk4"].rearrange("p (a b) -> p a b", a=4),
                        cs["N4"].rearrange("p (a b) -> p a b", a=4), idv, ALU.add),
                        waits=[mN, MISC_ALL], inc=pV)
                    st["Acur"], st["Bcur"] = cs["N4"], cs["NL4"]
                    st["Anew"], st["Bnew"] = cs["A24"], cs["B24"]
                    st["mAc"] = [mN] * 4
                    st["mBc"] = [mNL] * 4
                    st["mMkj"] = [mMk] * 4

                def sc_inv(st, lev, part):
                    cs, par = st["cs"], st["par"]
                    Acur, Bcur, Anew, Bnew = st["Acur"], st["Bcur"], st["Anew"], st["Bnew"]
                    mAc, mBc, mMkj = st["mAc"], st["mBc"], st["mMkj"]
                    if part == 0:
                        mAn = [None] * 4
                        if lev < 6:
                            for half in range(2):
                                bq = 4 + 2 * par + half
                                for jj in range(2):
                                    j = 2 * jj + half
                                    wj = [mAc[j], mBc[j]] + (list(banks[bq].free) if jj == 0 else [])
                                    mq = p.op("tensor", lambda e, Acur=Acur, Bcur=Bcur, j=j, jj=jj, bq=bq: e.matmul(
                                        banks[bq].t[:, 128 * jj:128 * (jj + 1)],
                                        Bcur[:, 128 * j:128 * (j + 1)], Acur[:, 128 * j:128 * (j + 1)],
                                        start=True, stop=True), waits=wj, inc=pT)
                                engb = "vector" if half == 0 else "scalar"
                                for jj in range(2):
                                    j = 2 * jj + half
                                    mAn[j] = evac(Anew[:, 128 * j:128 * (j + 1)],
                                                  banks[bq].t[:, 128 * jj:128 * (jj + 1)], [mq], eng=engb)
                                bank_release(bq, [mAn[half], mAn[2 + half]])
                        st["mAn_t"] = mAn
                        return
                    if part == 1:
                        mBn = [None] * 4
                        for half in range(2):
                            bq = 4 + 2 * par + half
                            for jj in range(2):
                                j = 2 * jj + half
                                wj = [mAc[j], mBc[j]] + (list(banks[bq].free) if jj == 0 else [])
                                mq = p.op("tensor", lambda e, Acur=Acur, Bcur=Bcur, j=j, jj=jj, bq=bq: e.matmul(
                                    banks[bq].t[:, 128 * jj:128 * (jj + 1)],
                                    Acur[:, 128 * j:128 * (j + 1)], Bcur[:, 128 * j:128 * (j + 1)],
                                    start=True, stop=True), waits=wj, inc=pT)
                            engb = "vector" if half == 0 else "scalar"
                            for jj in range(2):
                                j = 2 * jj + half
                                mBn[j] = evac(Bnew[:, 128 * j:128 * (j + 1)],
                                              banks[bq].t[:, 128 * jj:128 * (jj + 1)], [mq], eng=engb)
                            bank_release(bq, [mBn[half], mBn[2 + half]])
                        st["mBn_t"] = mBn
                        return
                    mAn = st["mAn_t"]
                    mBn = st["mBn_t"]
                    nMk = [None] * 4
                    for half in range(2):
                        bq = 4 + 2 * par + half
                        for jj in range(2):
                            j = 2 * jj + half
                            wj = [mBn[j], mMkj[j]] + (list(banks[bq].free) if jj == 0 else [])
                            p.op("tensor", lambda e, Bnew=Bnew, cs=cs, j=j, jj=jj, bq=bq: e.matmul(
                                banks[bq].t[:, 128 * jj:128 * (jj + 1)],
                                Bnew[:, 128 * j:128 * (j + 1)], cs["Mk4"][:, 128 * j:128 * (j + 1)],
                                start=True, stop=False), waits=wj)
                            mq = p.op("tensor", lambda e, cs=cs, j=j, jj=jj, bq=bq: e.matmul(
                                banks[bq].t[:, 128 * jj:128 * (jj + 1)],
                                identb_sb, cs["Mk4"][:, 128 * j:128 * (j + 1)],
                                start=False, stop=True), inc=pT)
                        engb = "vector" if half == 0 else "scalar"
                        for jj in range(2):
                            j = 2 * jj + half
                            nMk[j] = evac(cs["Mk4"][:, 128 * j:128 * (j + 1)],
                                          banks[bq].t[:, 128 * jj:128 * (jj + 1)], [mq], eng=engb)
                        bank_release(bq, [nMk[half], nMk[2 + half]])
                    st["mMkj"] = nMk
                    if lev < 6:
                        st["Acur"], st["Anew"] = Anew, Acur
                        st["mAc"] = st["mAn_t"]
                    st["Bcur"], st["Bnew"] = Bnew, Bcur
                    st["mBc"] = st["mBn_t"]

                def sc_T6(st):
                    c, par, slc, ssl = st["c"], st["par"], st["slc"], st["ssl"]
                    cs, csw, bkc, bkb = st["cs"], st["csw"], st["bkc"], st["bkb"]
                    kcol = st["kcol"]
                    mMkj = st["mMkj"]
                    ktmA4 = cs["NL4"]
                    ktmB4 = cs["ktm4"]
                    mvt = None
                    for j in range(4):
                        wj = [convo_ready[4 + j]] + (list(banks[bkb].free) if j == 0 else [])
                        mvt = p.op("tensor", lambda e, j=j, c=c, bkb=bkb: e.transpose(
                            banks[bkb].t[:].bitcast(BF16)[:, 128 * j:128 * (j + 1)],
                            convo[4 + j][:, 128 * c:128 * (c + 1)], identb_sb), waits=wj, inc=pT)
                        convo_free[4 + j] = mvt
                    cbv = cb[slc][:, 0:4].unsqueeze(-1).broadcast_to([128, 4, 128])
                    mbv = p.op("vector", lambda e, cs=cs, cbv=cbv, bkb=bkb: e.tensor_tensor(
                        cs["vtm4"].rearrange("p (a b) -> p a b", a=4),
                        banks[bkb].t[:].bitcast(BF16)[:, 0:512].rearrange("p (a b) -> p a b", a=4),
                        cbv, ALU.mult), waits=[mvt, st["me1"], vtm4_free[par]] + csw, inc=pV)
                    bank_release(bkb, mbv)
                    st["mbv"] = mbv
                    mwt = None
                    for kh in range(2):
                        kqv = kqb[ssl][kh].unsqueeze(1).broadcast_to([128, 2, 128])
                        mwt = p.op("vector", lambda e, cs=cs, kqv=kqv, kh=kh: e.tensor_tensor(
                            cs["wt4"][:, 256 * kh:256 * (kh + 1)].rearrange("p (a b) -> p a b", a=2),
                            kqv, cs["dI4"][:, 256 * kh:256 * (kh + 1)].rearrange("p (a b) -> p a b", a=2),
                            ALU.mult), waits=[st["kqms"][kh], st["mdI"], mwt] + mMkj, inc=pV)
                    st["mwt"] = mwt
                    mbe4 = p.op("tensor", lambda e, cs=cs, bkc=bkc: e.matmul(
                        banks[bkc].t[:], onesb[0:1, :], cs["erow4"], start=True, stop=True),
                        waits=[st["mer"]] + list(banks[bkc].free), inc=pT)
                    mqe = None
                    for kh in range(2):
                        qnv = qn[kh][:, 128 * c:128 * (c + 1)].unsqueeze(1).broadcast_to([128, 2, 128])
                        mqe = p.op("vector", lambda e, cs=cs, qnv=qnv, kh=kh, bkc=bkc: e.tensor_tensor(
                            cs["qe4"][:, 256 * kh:256 * (kh + 1)].rearrange("p (a b) -> p a b", a=2),
                            qnv, banks[bkc].t[:, 256 * kh:256 * (kh + 1)].rearrange("p (a b) -> p a b", a=2),
                            ALU.mult), waits=[mbe4, qn_ms[kh], mqe] + mMkj, inc=pV)
                    bank_release(bkc, mqe)
                    st["mqe"] = mqe
                    mgc = p.op("tensor", lambda e, cs=cs, bkc=bkc: e.matmul(
                        banks[bkc].t[:, 0:4], onesf[0:1, :],
                        cs["grow4"][0:1, 127:512:128],
                        start=True, stop=True),
                        waits=[st["mgre"]] + list(banks[bkc].free), inc=pT)
                    mks1 = p.op("vector", lambda e, slc=slc, bkc=bkc, kcol=kcol: e.tensor_tensor(
                        kcol[:, 0:4], banks[bkc].t[:, 0:4], cgc[slc][:, 4:8], ALU.subtract),
                        waits=[mgc, st["me2"], kcol_free[par]], inc=pV)
                    mks1b = p.op("scalar", lambda e, kcol=kcol: e.activation(
                        kcol[:, 0:4], kcol[:, 0:4], AF.Exp), waits=[mks1], inc=pA)
                    mks2 = p.op("scalar", lambda e, bkc=bkc, kcol=kcol: e.activation(
                        kcol[:, 4:8], banks[bkc].t[:, 0:4], AF.Exp), waits=[mgc, mks1], inc=pA)
                    bank_release(bkc, mks2)
                    st["mks2"] = mks2
                    mbex = p.op("vector", lambda e, slc=slc: e.tensor_tensor(
                        cex[slc], cex[slc], cb[slc][:, 0:4], ALU.mult),
                        waits=[st["mex_"], st["me1"]], inc=pV)
                    mkt = None
                    for j in range(4):
                        kh = j // 2
                        wj = [kn_ms[kh]] + (list(banks[bkb].free) if j == 0 else [])
                        mkt = p.op("tensor", lambda e, kh=kh, c=c, j=j, bkb=bkb: e.transpose(
                            banks[bkb].t[:].bitcast(BF16)[:, 128 * j:128 * (j + 1)],
                            kn[kh][:, 128 * c:128 * (c + 1)], identb_sb), waits=wj, inc=pT)
                    bexv = cex[slc].unsqueeze(-1).broadcast_to([128, 4, 128])
                    kscv = kcol[:, 0:4].unsqueeze(-1).broadcast_to([128, 4, 128])
                    mka = p.op("vector", lambda e, bexv=bexv, bkb=bkb, ktmA4=ktmA4: e.tensor_tensor(
                        ktmA4.rearrange("p (a b) -> p a b", a=4),
                        banks[bkb].t[:].bitcast(BF16)[:, 0:512].rearrange("p (a b) -> p a b", a=4),
                        bexv, ALU.mult), waits=[mkt, mbex] + mMkj, inc=pV)
                    mkb = p.op("vector", lambda e, kscv=kscv, bkb=bkb, ktmB4=ktmB4: e.tensor_tensor(
                        ktmB4.rearrange("p (a b) -> p a b", a=4),
                        banks[bkb].t[:].bitcast(BF16)[:, 0:512].rearrange("p (a b) -> p a b", a=4),
                        kscv, ALU.mult), waits=[mkt, mks1b, ktm4_free[par]], inc=pV)
                    bank_release(bkb, mkb)
                    st["mka"], st["mkb"] = mka, mkb

                def sc_P1(st):
                    cs, par = st["cs"], st["par"]
                    P14 = cs["U4"]
                    mP1 = [None] * 4
                    for half in range(2):
                        bq = 6 - 2 * par + half
                        for jj in range(2):
                            j = 2 * half + jj
                            wj = [st["mMkj"][j], st["mbv"]] + (list(banks[bq].free) if jj == 0 else [])
                            mq = p.op("tensor", lambda e, cs=cs, j=j, jj=jj, bq=bq: e.matmul(
                                banks[bq].t[:, 128 * jj:128 * (jj + 1)],
                                cs["Mk4"][:, 128 * j:128 * (j + 1)], cs["vtm4"][:, 128 * j:128 * (j + 1)],
                                start=True, stop=True), waits=wj, inc=pT)
                        engb = "vector" if half == 0 else "scalar"
                        me_ = evac(P14[:, 256 * half:256 * (half + 1)],
                                   banks[bq].t[:, 0:256], [mq], eng=engb)
                        mP1[2 * half] = mP1[2 * half + 1] = me_
                        bank_release(bq, me_)
                    vtm4_free[par] = (pT, pT.n)
                    st["mP1"] = mP1

                def sc_W2(st):
                    cs, par = st["cs"], st["par"]
                    ktmA4 = cs["NL4"]
                    W24 = cs["rhs4"]
                    mW2 = [None] * 4
                    for half in range(2):
                        bq = 6 - 2 * par + half
                        for jj in range(2):
                            j = 2 * half + jj
                            wj = [st["mMkj"][j], st["mka"]] + (list(banks[bq].free) if jj == 0 else [])
                            mq = p.op("tensor", lambda e, cs=cs, j=j, jj=jj, bq=bq, ktmA4=ktmA4: e.matmul(
                                banks[bq].t[:, 128 * jj:128 * (jj + 1)],
                                cs["Mk4"][:, 128 * j:128 * (j + 1)], ktmA4[:, 128 * j:128 * (j + 1)],
                                start=True, stop=True), waits=wj, inc=pT)
                        engb = "vector" if half == 0 else "scalar"
                        me_ = evac(W24[:, 256 * half:256 * (half + 1)],
                                   banks[bq].t[:, 0:256], [mq], eng=engb)
                        mW2[2 * half] = mW2[2 * half + 1] = me_
                        bank_release(bq, me_)
                    st["mW2"] = mW2

                def sc_Q2(st):
                    cs, par = st["cs"], st["par"]
                    W24 = cs["rhs4"]
                    mQ2 = [None] * 4
                    for half in range(2):
                        bq = 6 - 2 * par + half
                        for jj in range(2):
                            j = 2 * half + jj
                            wj = [st["mW2"][j], st["mwt"]] + (list(banks[bq].free) if jj == 0 else [])
                            mq = p.op("tensor", lambda e, cs=cs, j=j, jj=jj, bq=bq, W24=W24: e.matmul(
                                banks[bq].t[:, 128 * jj:128 * (jj + 1)],
                                W24[:, 128 * j:128 * (j + 1)], cs["wt4"][:, 128 * j:128 * (j + 1)],
                                start=True, stop=True), waits=wj, inc=pT)
                        for jj in range(2):
                            j = 2 * half + jj
                            mQ2[j] = p.op("vector", lambda e, cs=cs, j=j, jj=jj, bq=bq: e.scalar_tensor_tensor(
                                cs["qe4"][:, 128 * j:128 * (j + 1)],
                                banks[bq].t[:, 128 * jj:128 * (jj + 1)], -1.0,
                                cs["qe4"][:, 128 * j:128 * (j + 1)], ALU.mult, ALU.add),
                                waits=[mq, st["mqe"]], inc=pV)
                        bank_release(bq, [mQ2[2 * half], mQ2[2 * half + 1]])
                    st["mQ2"] = mQ2

                def sc_M2(st):
                    cs, par = st["cs"], st["par"]
                    W24 = cs["rhs4"]
                    ktmB4 = cs["ktm4"]
                    M2T4 = cs["dS4"]
                    kcol = st["kcol"]
                    mM2 = [None] * 4
                    for half in range(2):
                        bq = 4 + 2 * par + half
                        for jj in range(2):
                            j = 2 * half + jj
                            wj = [st["mW2"][j], st["mkb"]] + (list(banks[bq].free) if jj == 0 else [])
                            mq = p.op("tensor", lambda e, j=j, jj=jj, bq=bq, W24=W24, ktmB4=ktmB4: e.matmul(
                                banks[bq].t[:, 128 * jj:128 * (jj + 1)],
                                W24[:, 128 * j:128 * (j + 1)], ktmB4[:, 128 * j:128 * (j + 1)],
                                start=True, stop=True), waits=wj, inc=pT)
                        for jj in range(2):
                            j = 2 * half + jj
                            mM2[j] = p.op("vector", lambda e, j=j, jj=jj, bq=bq, M2T4=M2T4, kcol=kcol: e.scalar_tensor_tensor(
                                M2T4[:, 128 * j:128 * (j + 1)], identb_sb,
                                kcol[:, 4 + j:5 + j], banks[bq].t[:, 128 * jj:128 * (jj + 1)],
                                ALU.mult, ALU.subtract), waits=[mq, st["mks2"]], inc=pV)
                        bank_release(bq, [mM2[2 * half], mM2[2 * half + 1]])
                    ktm4_free[par] = (pT, pT.n)
                    st["mM2"] = mM2

                def sc_chain(st):
                    c, par, slc, ssl = st["c"], st["par"], st["slc"], st["ssl"]
                    cs, csl = st["cs"], st["csl"]
                    P14 = cs["U4"]
                    ktmB4 = cs["ktm4"]
                    M2T4 = cs["dS4"]
                    mP1, mQ2, mM2 = st["mP1"], st["mQ2"], st["mM2"]
                    mOc = [None] * 4
                    mS1 = [None] * 4
                    for half in range(2):
                        for jj in range(2):
                            j = 2 * half + jj
                            ci = 4 * b + j
                            bq = 6 - 2 * par + half
                            wj = [mP1[j], st["mwt"]] + (list(banks[bq].free) if jj == 0 else [])
                            p.op("tensor", lambda e, cs=cs, j=j, jj=jj, bq=bq, P14=P14: e.matmul(
                                banks[bq].t[:, 128 * jj:128 * (jj + 1)],
                                P14[:, 128 * j:128 * (j + 1)], cs["wt4"][:, 128 * j:128 * (j + 1)],
                                start=True, stop=False), waits=wj)
                            p.op("tensor", lambda e, cs=cs, j=j, jj=jj, bq=bq, ci=ci: e.matmul(
                                banks[bq].t[:, 128 * jj:128 * (jj + 1)],
                                Sb[:, 128 * ci:128 * (ci + 1)], cs["qe4"][:, 128 * j:128 * (j + 1)],
                                start=False, stop=True), waits=[mQ2[j], st_ms[ci]], inc=pT)
                            mOc[j] = (pT, pT.n)
                            bq = 4 + 2 * par + half
                            wj = [st["mkb"], mP1[j]] + (list(banks[bq].free) if jj == 0 else [])
                            p.op("tensor", lambda e, j=j, jj=jj, bq=bq, ktmB4=ktmB4, P14=P14: e.matmul(
                                banks[bq].t[:, 128 * jj:128 * (jj + 1)],
                                ktmB4[:, 128 * j:128 * (j + 1)], P14[:, 128 * j:128 * (j + 1)],
                                start=True, stop=False), waits=wj)
                            p.op("tensor", lambda e, j=j, jj=jj, bq=bq, ci=ci, M2T4=M2T4: e.matmul(
                                banks[bq].t[:, 128 * jj:128 * (jj + 1)],
                                M2T4[:, 128 * j:128 * (j + 1)], Sb[:, 128 * ci:128 * (ci + 1)],
                                start=False, stop=True), waits=[mM2[j]], inc=pT)
                            mS1[j] = (pT, pT.n)
                    mOev = [None] * 4
                    for half in range(2):
                        bq = 6 - 2 * par + half
                        engb = "vector" if half == 0 else "scalar"
                        for jj in range(2):
                            j = 2 * half + jj
                            w = [mOc[2 * half + 1]] + ([ost_free[j]] if ost_free[j] else [])
                            mOev[j] = evac(ost[j][:, 128 * c:128 * (c + 1)],
                                           banks[bq].t[:, 128 * jj:128 * (jj + 1)], w, eng=engb)
                            ost_ms[j][c] = mOev[j]
                        bank_release(bq, [mOev[2 * half], mOev[2 * half + 1]])
                    mSf = [None] * 4
                    for half in range(2):
                        bq = 4 + 2 * par + half
                        engb = "vector" if half == 0 else "scalar"
                        for jj in range(2):
                            j = 2 * half + jj
                            ci = 4 * b + j
                            mSf[j] = evac(Sf[:, 128 * ci:128 * (ci + 1)],
                                          banks[bq].t[:, 128 * jj:128 * (jj + 1)],
                                          [mS1[2 * half + 1]], eng=engb)
                        bank_release(bq, [mSf[2 * half], mSf[2 * half + 1]])
                    mSb = p.op("vector", lambda e, b=b: e.tensor_copy(
                        Sb[:, 512 * b:512 * (b + 1)], Sf[:, 512 * b:512 * (b + 1)]),
                        waits=mSf + [mOev[0], mOev[1], mOev[2], mOev[3]], inc=pV)
                    for j in range(4):
                        st_ms[4 * b + j] = mSb
                    kcol_free[par] = mSb
                    cs["free"] = mSb
                    csl["free"] = mSb
                    kkq_free[ssl] = (pV, pV.n)

                for cp in range(2) if SCAN_ON else []:
                    sA = sc_new(2 * cp)
                    sB = sc_new(2 * cp + 1)
                    sc_early(sA)
                    deq_tick(2 if DEQIL else 0)
                    yield
                    sc_early(sB)
                    deq_tick(2 if DEQIL else 0)
                    yield
                    for lev in range(1, 7):
                        for part in range(3):
                            sc_inv(sA, lev, part)
                            sc_inv(sB, lev, part)
                        if lev % 3 == 0:
                            deq_tick(1 if DEQIL else 0)
                        yield
                    sc_T6(sA)
                    sc_T6(sB)
                    deq_tick(1 if DEQIL else 0)
                    yield
                    sc_P1(sA)
                    sc_P1(sB)
                    yield
                    sc_W2(sA)
                    sc_W2(sB)
                    yield
                    sc_Q2(sA)
                    sc_Q2(sB)
                    yield
                    sc_M2(sA)
                    sc_M2(sB)
                    yield
                    sc_chain(sA)
                    sc_chain(sB)
                    deq_tick(1 if DEQIL else 0)
                    yield
                for pr in range(2) if (GATE_ON and SCAN_ON) else []:
                    gst = [{} for _ in range(2)]
                    for u in range(2):
                        j = 2 * pr + u
                        w = [zs_ms[j]] + [ost_ms[j][c_] for c_ in range(4)]
                        if xgt_free[u]:
                            w.append(xgt_free[u])
                        gst[u]["mxg"] = p.op("vector", lambda e, j=j, u=u: e.tensor_tensor(
                            xgt2[u], ost[j], zs[j], ALU.mult), waits=w, inc=pV)
                        zs_free[j] = gst[u]["mxg"]
                        ost_free[j] = gst[u]["mxg"]
                    for u in range(2):
                        gst[u]["msq"] = p.op("vector", lambda e, u=u: e.tensor_tensor(
                            sqg2[u], xgt2[u], xgt2[u], ALU.mult),
                            waits=[l2_sq_free[u], gst[u]["mxg"]], inc=pV)
                    for u in range(2):
                        bq = 4 + u
                        gst[u]["mcs"] = p.op("tensor", lambda e, u=u, bq=bq: e.matmul(
                            banks[bq].t[0:1, :], onesb[:, 0:1], sqg2[u], start=True, stop=True),
                            waits=[gst[u]["msq"]] + list(banks[bq].free), inc=pT)
                        l2_sq_free[u] = gst[u]["mcs"]
                    for u in range(2):
                        bq = 4 + u
                        mr_ = p.op("vector", lambda e, u=u, bq=bq: e.tensor_scalar(
                            rrow2[u], banks[bq].t[0:1, :], 1.0 / 128.0, RMS_EPS, ALU.mult, ALU.add),
                            waits=[gst[u]["mcs"], l2_rr_free[u]], inc=pV)
                        bank_release(bq, mr_)
                        gst[u]["mr"] = mr_
                    for u in range(2):
                        gst[u]["mlg"] = p.op("scalar", lambda e, u=u: e.activation(
                            rrow2[u], rrow2[u], AF.Ln), waits=[gst[u]["mr"]], inc=pA)
                    for u in range(2):
                        gst[u]["mex"] = p.op("scalar", lambda e, u=u: e.activation(
                            rrow2[u], rrow2[u], AF.Exp, scale=-0.5), waits=[gst[u]["mlg"]], inc=pA)
                    for u in range(2):
                        bq = 4 + u
                        gst[u]["mbc"] = p.op("tensor", lambda e, u=u, bq=bq: e.matmul(
                            banks[bq].t[:], onesf[0:1, :], rrow2[u], start=True, stop=True),
                            waits=[gst[u]["mex"]] + list(banks[bq].free), inc=pT)
                        l2_rr_free[u] = gst[u]["mbc"]
                    for u in range(2):
                        j = 2 * pr + u
                        bq = 4 + u
                        ytb = yt[j % 2]
                        w = [gst[u]["mbc"]] + ([yt_free[j % 2]] if yt_free[j % 2] else [])
                        my1 = p.op("vector", lambda e, ytb=ytb, u=u, bq=bq: e.tensor_tensor(
                            ytb, xgt2[u], banks[bq].t[:], ALU.mult), waits=w, inc=pV)
                        xgt_free[u] = my1
                        myv = p.op("vector", lambda e, ytb=ytb: e.tensor_scalar(
                            ytb, ytb, normw_sb, None, ALU.mult), waits=[MISC_ALL, my1], inc=pV)
                        bank_release(bq, myv)
                        jd = b * 4 + tt // 2
                        col0 = (TT * tt) % 1024
                        mys = p.op("gpsimd", lambda e, jd=jd, j=j, col0=col0, ytb=ytb: e.dma_start(
                            out=dram["a2a_in"][512 * jd + 128 * j:512 * jd + 128 * (j + 1),
                                               col0:col0 + 512], in_=ytb),
                            waits=[myv], inc=d_yst[j % 2], k=16)
                        yt_free[j % 2] = (d_yst[j % 2], d_yst[j % 2].n)
                if b == 0:
                    if "convq" in dbg:
                        p.op("gpsimd", lambda e, tt=tt: e.dma_start(
                            out=dbg["convq"][:, TT * tt:TT * (tt + 1)], in_=qn[0]),
                            waits=[qn_ms[0]], inc=d_dbg, k=16)
                        qn_free[0] = (d_dbg, d_dbg.n)
                    if "convk" in dbg:
                        p.op("gpsimd", lambda e, tt=tt: e.dma_start(
                            out=dbg["convk"][:, TT * tt:TT * (tt + 1)], in_=kn[0]),
                            waits=[kn_ms[0]], inc=d_dbg, k=16)
                        kn_free[0] = (d_dbg, d_dbg.n)
                    if "convv" in dbg:
                        p.op("gpsimd", lambda e, tt=tt: e.dma_start(
                            out=dbg["convv"][:, TT * tt:TT * (tt + 1)], in_=convo[4]),
                            waits=[convo_ready[4]], inc=d_dbg, k=16)
                        convo_free[4] = (d_dbg, d_dbg.n)
                    if "gbeta" in dbg:
                        p.op("gpsimd", lambda e, tt=tt: e.dma_start(
                            out=dbg["gbeta"][0:4, TT * tt:TT * (tt + 1)], in_=bsig[0:4, :]),
                            waits=[mbs], inc=d_dbg, k=16)
                        bsig_free[0] = (d_dbg, d_dbg.n)
                        p.op("gpsimd", lambda e, tt=tt: e.dma_start(
                            out=dbg["gbeta"][4:8, TT * tt:TT * (tt + 1)], in_=gcum[4:8, :]),
                            waits=[ba_ms], inc=d_dbg, k=16)
                        gcum_free[0] = (d_dbg, d_dbg.n)
                if gti < 12 and DEQ_ON:
                    deq_tick(deq_goal[0] - deq_idx[0])
                    if deq_idx[0] in (64, 128, 192):
                        deq_B()
                        mat = "gud"[deq_idx[0] // 64 - 1]
                        p.op("gpsimd", lambda e, mat=mat: e.collective_compute(
                            "AllGather", ALU.bypass, replica_groups=[list(range(NCORES))],
                            ins=[dram[f"agw_{mat}_in"].opt()], outs=[dram[f"agw_{mat}_out"].opt()]),
                            waits=[(d_wst[0], d_wst[0].n), (d_wst[1], d_wst[1].n)],
                            inc=c_w[mat], k=None)
                        if mat == "g" and "wdeq" in dbg:
                            p.op("gpsimd", lambda e: e.dma_start(
                                out=dbg["wdeq"], in_=dram["agw_g_in"]),
                                waits=[(d_wst[0], d_wst[0].n), (d_wst[1], d_wst[1].n)],
                                inc=d_dbg, k=16)
            pend_scan[0] = do_scan()

    if pend_scan[0]:
        for _ in pend_scan[0]:
            pass
        pend_scan[0] = None
    p.op("gpsimd", lambda e: e.collective_compute(
        "AllToAll", ALU.bypass, replica_groups=[list(range(NCORES))],
        ins=[dram["a2a_in"].opt()], outs=[dram["a2a_out"].opt()]),
        waits=[(d_yst[0], d_yst[0].n), (d_yst[1], d_yst[1].n)], inc=c_a2a, k=None)
    A2A_DONE = (c_a2a, 1)
    if "yloc" in dbg:
        p.op("gpsimd", lambda e: e.dma_start(out=dbg["yloc"], in_=dram["a2a_out"][0:512, :]),
             waits=[A2A_DONE], inc=d_dbg, k=16)
    PH1END = [(pT, pT.n), (pV, pV.n), (pA, pA.n), (d_yst[0], d_yst[0].n),
              (d_yst[1], d_yst[1].n), (d_wst[0], d_wst[0].n), (d_wst[1], d_wst[1].n)]
    if STAGE < 2:
        p.op("gpsimd", lambda e: e.nop(), waits=[A2A_DONE, (d_dbg, d_dbg.n)] + PH1END)
        return

    A.reset(resmark)
    ln2fm = A.carve([128, 16 * 1024], BF16, "ln2fm")
    ph3_mark = A.mark()
    wo_sb = A.carve([128, 32 * 1024], BF16, "wo")
    ytl = [A.carve([128, 32 * 128], BF16, f"ytl{i}") for i in range(2)]
    ostg = A.carve([128, 8 * 2048], F32, "ostg")
    xt2 = A.carve([128, D], F32, "xt2")
    sq2 = A.carve([128, D], F32, "sq2")
    ln2b = A.carve([128, D], BF16, "ln2b")
    st2 = A.carve([128, 8], F32, "st2")
    wo_free = None
    ytl_free = [None, None]
    first3 = [True]
    for hh in range(2):
        wload = []
        for q4 in range(4):
            w = list(PH1END) if (hh == 0 and q4 == 0) else []
            if wo_free:
                w.append(wo_free)
            wload.append(p.op("gpsimd", lambda e, q4=q4, hh=hh: e.dma_start(
                out=wo_sb[:, 8192 * q4:8192 * (q4 + 1)].rearrange("p (c h) -> p c h", h=1024),
                in_=dram["wout"].rearrange("(c p) h -> p c h", p=128)[
                    :, 8 * q4:8 * (q4 + 1), 1024 * hh:1024 * (hh + 1)]),
                waits=w, inc=d_wo, k=16))
        WO_RDY = (d_wo, d_wo.n)
        for i in range(8):
            yb = i % 2
            w = [A2A_DONE, WO_RDY] + ([ytl_free[yb]] if ytl_free[yb] else [])
            if hh == 0 and i == 0:
                w += PH1END
            myl = p.op("sync", lambda e, i=i, yb=yb: e.dma_start(
                out=ytl[yb][:].rearrange("p (c q) -> p c q", q=128),
                in_=dram["a2a_out"].rearrange("(c p) t -> p c t", p=128)[:, :, 128 * i:128 * (i + 1)]),
                waits=w, inc=d_yt, k=16)
            for hb in range(2):
                bk = (i * 2 + hb) % 4
                for dc in range(32):
                    w = []
                    if dc == 0:
                        w = [myl, WO_RDY] + list(banks[bk].free)
                    inc = pT if dc == 31 else None
                    mmo = p.op("tensor", lambda e, dc=dc, hb=hb, bk=bk, yb=yb: e.matmul(
                        banks[bk].t[:], ytl[yb][:, 128 * dc:128 * (dc + 1)],
                        wo_sb[:, 1024 * dc + 512 * hb:1024 * dc + 512 * (hb + 1)],
                        start=(dc == 0), stop=(dc == 31)), waits=w, inc=inc)
                me_ = evac(ostg[:, 2048 * i + 1024 * hh + 512 * hb:
                               2048 * i + 1024 * hh + 512 * (hb + 1)], banks[bk].t[:], [mmo])
                bank_release(bk, me_)
            ytl_free[yb] = (pT, pT.n)
        wo_free = (pT, pT.n)
    OSTG_RDY = (pV, pV.n)
    OSTG_RDY_A = (pA, pA.n)
    ln2fm_ms = []
    xt2_free = None
    p2_m5b = None
    LNT3 = []
    for i in range(8):
        og = ostg[:, 2048 * i:2048 * (i + 1)]
        w = [OSTG_RDY, OSTG_RDY_A]
        m1 = p.op("vector", lambda e, og=og: e.tensor_reduce(st2[:, 0:1], og, AX.X, ALU.add),
                  waits=w + [p2_m5b], inc=pV)
        m2_ = p.op("scalar", lambda e, og=og: e.activation(sq2, og, AF.Square),
                   waits=w + (list(xt2_free) if xt2_free else []), inc=pA)
        mr2 = p.op("vector", lambda e: e.tensor_reduce(st2[:, 1:2], sq2, AX.X, ALU.add),
                   waits=[m2_], inc=pV)
        mm1 = p.op("vector", lambda e: e.tensor_scalar(st2[:, 2:3], st2[:, 0:1], 1.0 / 2048.0, None, ALU.mult),
                   waits=[m1], inc=pV)
        mm2 = p.op("vector", lambda e: e.tensor_scalar(st2[:, 3:4], st2[:, 1:2], 1.0 / 2048.0, None, ALU.mult),
                   waits=[mr2], inc=pV)
        mm3 = p.op("vector", lambda e: e.tensor_tensor(st2[:, 4:5], st2[:, 2:3], st2[:, 2:3], ALU.mult),
                   waits=[mm1], inc=pV)
        m4 = p.op("vector", lambda e: e.tensor_tensor(st2[:, 5:6], st2[:, 3:4], st2[:, 4:5], ALU.subtract),
                  waits=[mm2, mm3], inc=pV)
        m5 = p.op("scalar", lambda e: e.activation(st2[:, 6:7], st2[:, 5:6], AF.Ln, bias=LN_EPS),
                  waits=[m4], inc=pA)
        m6 = p.op("scalar", lambda e: e.activation(st2[:, 7:8], st2[:, 6:7], AF.Exp, scale=-0.5),
                  waits=[m5], inc=pA)
        mlx = p.op("sync", lambda e, i=i: e.dma_start(
            out=xt2, in_=dram["x_sl"][128 * i:128 * (i + 1), :]),
            waits=list(xt2_free) if xt2_free else [], inc=d_x, k=16)
        mln_ = p.op("vector", lambda e, og=og: e.tensor_scalar(
            sq2, og, st2[:, 2:3], st2[:, 7:8], ALU.subtract, ALU.mult), waits=[m6, mm1], inc=pV)
        mx2 = p.op("vector", lambda e: e.tensor_tensor(xt2, sq2, xt2, ALU.add),
                   waits=[(d_x, d_x.n), mln_], inc=pV)
        mx2st = p.op("gpsimd", lambda e, i=i: e.dma_start(
            out=dram["x2_sp"][128 * i:128 * (i + 1), :], in_=xt2), waits=[mx2], inc=d_x2st, k=16)
        if "x2" in dbg:
            p.op("gpsimd", lambda e, i=i: e.dma_start(
                out=dbg["x2"][128 * i:128 * (i + 1), :], in_=xt2), waits=[mx2], inc=d_dbg, k=16)
        m2b = p.op("scalar", lambda e: e.activation(sq2, xt2, AF.Square), waits=[mx2], inc=pA)
        mrb1 = p.op("vector", lambda e: e.tensor_reduce(st2[:, 1:2], xt2, AX.X, ALU.add),
                    waits=[m2b, mx2], inc=pV)
        mrb2 = p.op("vector", lambda e: e.tensor_reduce(st2[:, 0:1], sq2, AX.X, ALU.add),
                    waits=[m2b], inc=pV)
        mmb1 = p.op("vector", lambda e: e.tensor_scalar(st2[:, 2:3], st2[:, 1:2], 1.0 / 2048.0, None, ALU.mult),
                    waits=[mrb1], inc=pV)
        mmb2 = p.op("vector", lambda e: e.tensor_scalar(st2[:, 3:4], st2[:, 0:1], 1.0 / 2048.0, None, ALU.mult),
                    waits=[mrb2], inc=pV)
        mmb3 = p.op("vector", lambda e: e.tensor_tensor(st2[:, 4:5], st2[:, 2:3], st2[:, 2:3], ALU.mult),
                    waits=[mmb1], inc=pV)
        m4b = p.op("vector", lambda e: e.tensor_tensor(st2[:, 5:6], st2[:, 3:4], st2[:, 4:5], ALU.subtract),
                   waits=[mmb2, mmb3], inc=pV)
        m5b = p.op("scalar", lambda e: e.activation(st2[:, 6:7], st2[:, 5:6], AF.Ln, bias=LN_EPS),
                   waits=[m4b], inc=pA)
        p2_m5b = m5b
        m6b = p.op("scalar", lambda e: e.activation(st2[:, 7:8], st2[:, 6:7], AF.Exp, scale=-0.5),
                   waits=[m5b], inc=pA)
        mlb = p.op("vector", lambda e: e.tensor_scalar(
            ln2b, xt2, st2[:, 2:3], st2[:, 7:8], ALU.subtract, ALU.mult), waits=[m6b, mmb1], inc=pV)
        lnview = ln2b.rearrange("p (c a two) -> p c two a", c=8, two=2)
        for parity in range(2):
            for cblk in range(8):
                fc = parity * 8 + cblk
                bk = 2 + (fc & 1)
                w = [mlb] + list(banks[bk].free)
                mt = p.op("tensor", lambda e, cblk=cblk, parity=parity, bk=bk: e.transpose(
                    banks[bk].t[:].bitcast(BF16)[:, 0:128], lnview[:, cblk, parity, :],
                    identb_sb), waits=w, inc=pT)
                me_ = evac(ln2fm[:, 1024 * fc + 128 * i:1024 * fc + 128 * (i + 1)],
                           banks[bk].t[:].bitcast(BF16)[:, 0:128], [mt])
                bank_release(bk, me_)
                ln2fm_ms.append(me_)
        xt2_free = [(pT, pT.n), mx2st]
        if i == 3:
            LNT3.extend([(pV, pV.n), (pA, pA.n)])
    LN2FM_RDY = [(pV, pV.n), (pA, pA.n)]
    LN2_RDY = {0: LNT3, 1: LN2FM_RDY}
    PH2END = [(pT, pT.n), (pV, pV.n), (pA, pA.n), (d_x2st, d_x2st.n)]
    if STAGE < 3:
        p.op("gpsimd", lambda e: e.nop(), waits=[(d_dbg, d_dbg.n)] + PH2END)
        return

    A.reset(ph3_mark)
    inter = A.carve([128, 64 * 512], BF16, "inter")
    wgl = [A.carve([128, 16 * 256], BF16, f"wgl{i}") for i in range(2)]
    wul = [A.carve([128, 16 * 256], BF16, f"wul{i}") for i in range(2)]
    wdl = [A.carve([128, 512], BF16, f"wdl{i}") for i in range(4)]
    gtmp2 = [A.carve([128, 512], BF16, f"gtmp{i}") for i in range(2)]
    utmp2 = [A.carve([128, 512], BF16, f"utmp{i}") for i in range(2)]
    acc = A.carve([128, 4 * 2048], F32, "acc")
    xt3 = A.carve([128, D], F32, "xt3")
    sq3 = A.carve([128, D], F32, "sq3")
    outb = A.carve([128, D], F32, "outb")
    st3 = A.carve([128, 8], F32, "st3")
    wgl_free = [None] * 4
    wul_free = [None] * 4
    wdl_free = [None] * 4
    inter_free = None
    acc_free = None
    gtmp_free = [None, None]
    utmp_free = [None, None]
    first_p3 = [True]
    wrd_pair = [None]
    wru_pair = [None]
    prev_mo3 = None
    prev_mst = None
    prev_m6ep = None
    acc_seg_ms = [[None] * 4 for _ in range(4)]
    AGW_DONE = {m: (c_w[m], 1) for m in "gud"}
    for ts in range(2):
        for nh in range(2):
            inter_ms = [None] * 64
            for ntl in range(64):
                nt = 64 * nh + ntl
                rr_ = nt // 16
                ncol = 128 * (nt % 16)
                bufw = (ntl // 2) % 2
                off = 128 * (ntl % 2)
                if ntl % 2 == 0:
                    w = [AGW_DONE["g"]] + ([wgl_free[bufw]] if wgl_free[bufw] else [])
                    if first_p3[0]:
                        w += [wo_free] + LNT3
                        first_p3[0] = False
                    p.op("sync", lambda e, rr_=rr_, ncol=ncol, bufw=bufw: e.dma_start(
                        out=wgl[bufw][:].rearrange("p (c q) -> p c q", q=256),
                        in_=dram["agw_g_out"].rearrange("(r c p) n -> r p c n", r=8, p=128)[
                            rr_][:, :, ncol:ncol + 256]), waits=w, inc=d_wgl[bufw], k=16)
                    w = [AGW_DONE["u"]] + ([wul_free[bufw]] if wul_free[bufw] else [])
                    p.op("sync", lambda e, rr_=rr_, ncol=ncol, bufw=bufw: e.dma_start(
                        out=wul[bufw][:].rearrange("p (c q) -> p c q", q=256),
                        in_=dram["agw_u_out"].rearrange("(r c p) n -> r p c n", r=8, p=128)[
                            rr_][:, :, ncol:ncol + 256]), waits=w, inc=d_wul[bufw], k=16)
                    wrd_pair[0] = (d_wgl[bufw], d_wgl[bufw].n)
                    wru_pair[0] = (d_wul[bufw], d_wul[bufw].n)
                WRD = wrd_pair[0]
                WRU = wru_pair[0]
                gtmp = gtmp2[ntl % 2]
                utmp = utmp2[ntl % 2]
                bnkg = 0 + 2 * (ntl % 2)
                bnku = 1 + 2 * (ntl % 2)
                mm_ms = {}
                for (bnk, wsb, wrd) in ((bnkg, wgl[bufw], WRD), (bnku, wul[bufw], WRU)):
                    for fc in range(16):
                        w = []
                        if fc == 0:
                            w = [wrd] + LN2_RDY[ts] + \
                                list(banks[bnk].free)
                        inc = pT if fc == 15 else None
                        mm_ms[bnk] = p.op("tensor", lambda e, fc=fc, wsb=wsb, bnk=bnk, ts=ts, off=off: e.matmul(
                            banks[bnk].t[:], wsb[:, 256 * fc + off:256 * fc + off + 128],
                            ln2fm[:, 1024 * fc + 512 * ts:1024 * fc + 512 * (ts + 1)],
                            start=(fc == 0), stop=(fc == 15)), waits=w, inc=inc)
                if ntl % 2 == 1:
                    wgl_free[bufw] = mm_ms[bnkg]
                    wul_free[bufw] = mm_ms[bnku]
                mgact = p.op("scalar", lambda e, bnkg=bnkg, gtmp=gtmp: e.activation(
                    gtmp, banks[bnkg].t[:], AF.Gelu_apprx_tanh),
                    waits=[mm_ms[bnkg], gtmp_free[ntl % 2]], inc=pA)
                bank_release(bnkg, mgact)
                muact = p.op("vector", lambda e, bnku=bnku, utmp=utmp: e.tensor_copy(
                    utmp, banks[bnku].t[:]),
                    waits=[mm_ms[bnku], utmp_free[ntl % 2]], inc=pV)
                bank_release(bnku, muact)
                w = [mgact, muact] + ([inter_free] if inter_free else [])
                inter_ms[ntl] = p.op("vector", lambda e, ntl=ntl, gtmp=gtmp, utmp=utmp: e.tensor_tensor(
                    inter[:, 512 * ntl:512 * (ntl + 1)], gtmp, utmp, ALU.mult),
                    waits=w, inc=pV)
                gtmp_free[ntl % 2] = inter_ms[ntl]
                utmp_free[ntl % 2] = inter_ms[ntl]
            for h4 in range(4):
                for ntl in range(64):
                    nt = 64 * nh + ntl
                    bufw = ntl % 4
                    w = [AGW_DONE["d"]] + ([wdl_free[bufw]] if wdl_free[bufw] else [])
                    p.op("sync", lambda e, nt=nt, h4=h4, bufw=bufw: e.dma_start(
                        out=wdl[bufw], in_=dram["agw_d_out"][128 * nt:128 * (nt + 1),
                                                             512 * h4:512 * (h4 + 1)]),
                        waits=w, inc=d_wd[bufw], k=16)
                    WDRD = (d_wd[bufw], d_wd[bufw].n)
                    for tq in range(4):
                        bnk = 4 + tq
                        w = [inter_ms[ntl]]
                        if ntl == 0:
                            w += [WDRD] + list(banks[bnk].free)
                        else:
                            w += [WDRD]
                        mmd = p.op("tensor", lambda e, ntl=ntl, tq=tq, bnk=bnk, bufw=bufw: e.matmul(
                            banks[bnk].t[:], inter[:, 512 * ntl + 128 * tq:512 * ntl + 128 * (tq + 1)],
                            wdl[bufw], start=(ntl == 0), stop=(ntl == 63)),
                            waits=w, inc=(pT if tq == 3 else None))
                    wdl_free[bufw] = (pT, pT.n)
                for tq in range(4):
                    bnk = 4 + tq
                    dst = acc[:, 2048 * tq + 512 * h4:2048 * tq + 512 * (h4 + 1)]
                    w = [(pT, pT.n)] + ([acc_free] if acc_free else [])
                    if nh == 0:
                        me_ = evac(dst, banks[bnk].t[:], w)
                        acc_seg_ms[tq][h4] = me_
                    else:
                        me_ = p.op("vector", lambda e, dst=dst, bnk=bnk: e.tensor_tensor(
                            dst, dst, banks[bnk].t[:], ALU.add),
                            waits=w + [acc_seg_ms[tq][h4]], inc=pV)
                        acc_seg_ms[tq][h4] = me_
                    bank_release(bnk, me_)
            inter_free = (pT, pT.n)
        ACC_RDY = [(pV, pV.n), (pA, pA.n)]
        ep_ms = []
        for tq in range(4):
            i = 4 * ts + tq
            ac = acc[:, 2048 * tq:2048 * (tq + 1)]
            w = list(ACC_RDY)
            m1 = p.op("vector", lambda e, ac=ac: e.tensor_reduce(st3[:, 0:1], ac, AX.X, ALU.add),
                      waits=w + [prev_m6ep], inc=pV)
            m2_ = p.op("scalar", lambda e, ac=ac: e.activation(sq3, ac, AF.Square),
                       waits=w + ([prev_mo3] if prev_mo3 else []), inc=pA)
            mr2 = p.op("vector", lambda e: e.tensor_reduce(st3[:, 1:2], sq3, AX.X, ALU.add),
                       waits=[m2_], inc=pV)
            mm1 = p.op("vector", lambda e: e.tensor_scalar(st3[:, 2:3], st3[:, 0:1], 1.0 / 2048.0, None, ALU.mult),
                       waits=[m1], inc=pV)
            mm2 = p.op("vector", lambda e: e.tensor_scalar(st3[:, 3:4], st3[:, 1:2], 1.0 / 2048.0, None, ALU.mult),
                       waits=[mr2], inc=pV)
            mm3 = p.op("vector", lambda e: e.tensor_tensor(st3[:, 4:5], st3[:, 2:3], st3[:, 2:3], ALU.mult),
                       waits=[mm1], inc=pV)
            m4 = p.op("vector", lambda e: e.tensor_tensor(st3[:, 5:6], st3[:, 3:4], st3[:, 4:5], ALU.subtract),
                      waits=[mm2, mm3], inc=pV)
            m5 = p.op("scalar", lambda e: e.activation(st3[:, 6:7], st3[:, 5:6], AF.Ln, bias=LN_EPS),
                      waits=[m4], inc=pA)
            m6 = p.op("scalar", lambda e: e.activation(st3[:, 7:8], st3[:, 6:7], AF.Exp, scale=-0.5),
                      waits=[m5], inc=pA)
            prev_m6ep = m6
            mx3 = p.op("sync", lambda e, i=i: e.dma_start(
                out=xt3, in_=dram["x2_sp"][128 * i:128 * (i + 1), :]),
                waits=[(d_x2st, d_x2st.n)] + LN2FM_RDY + ([prev_mo3] if prev_mo3 else []), inc=d_x2l, k=16)
            mln3 = p.op("vector", lambda e, ac=ac: e.tensor_scalar(
                sq3, ac, st3[:, 2:3], st3[:, 7:8], ALU.subtract, ALU.mult), waits=[m6, mm1], inc=pV)
            mo3 = p.op("vector", lambda e: e.tensor_tensor(outb, sq3, xt3, ALU.add),
                       waits=[(d_x2l, d_x2l.n), mln3] + ([prev_mst] if prev_mst else []), inc=pV)
            prev_mo3 = mo3
            mst = p.op("gpsimd", lambda e, i=i: e.dma_start(
                out=dram["out"][128 * i:128 * (i + 1), :], in_=outb), waits=[mo3], inc=d_outst, k=16)
            prev_mst = mst
            ep_ms.append(mst)
        acc_free = (pV, pV.n)
    p.op("gpsimd", lambda e: e.nop(),
         waits=[(d_outst, d_outst.n), (d_dbg, d_dbg.n), (d_x2st, d_x2st.n)])


def _host_prep(inputs):
    import ml_dtypes
    x = np.ascontiguousarray(inputs["L0_attn_hidden_states"]).reshape(B * S, D)
    wqkvz = inputs["L0_attn_w_qkvz"]
    wba = inputs["L0_attn_w_ba"]
    convw = np.ascontiguousarray(inputs["L0_attn_conv_w"][:, 0, :])
    dtb = inputs["L0_attn_dt_bias"]
    alog = inputs["L0_attn_A_log"]
    normw = inputs["L0_attn_norm_w"]
    wout = inputs["L0_attn_w_out"]

    def bytesT(pk):
        bv = np.ascontiguousarray(pk.view(np.uint8)[:, ::4])
        return np.ascontiguousarray(bv.T)

    gT = bytesT(inputs["L0_ffn_gate_W_packed"])
    uT = bytesT(inputs["L0_ffn_up_W_packed"])
    dT = bytesT(inputs["L0_ffn_down_W_packed"])
    sgT = np.ascontiguousarray(inputs["L0_ffn_gate_scales"].T)
    suT = np.ascontiguousarray(inputs["L0_ffn_up_scales"].T)
    sdT = np.ascontiguousarray(inputs["L0_ffn_down_scales"].T)

    idx = np.concatenate([np.concatenate([256 * k + 2 * np.arange(128),
                                          256 * k + 2 * np.arange(128) + 1])
                          for k in range(8)])
    ident = np.eye(128, dtype=np.float32).astype(ml_dtypes.bfloat16)
    sidx = np.arange(128)
    maskS = np.where(sidx[:, None] < sidx[None, :], 0.0, MASK_NEG).astype(np.float32)
    maskI = np.where(sidx[:, None] <= sidx[None, :], 0.0, MASK_NEG).astype(np.float32)

    in_maps = []
    for c in range(NCORES):
        bac = [8 * c, 8 * c + 1, 8 * c + 4, 8 * c + 5,
               8 * c + 2, 8 * c + 3, 8 * c + 6, 8 * c + 7]
        cwrows = np.concatenate([
            np.arange(128 * (2 * c), 128 * (2 * c + 2)),
            2048 + np.arange(128 * (2 * c), 128 * (2 * c + 2)),
            4096 + np.arange(128 * (4 * c), 128 * (4 * c + 4))])
        im = {
            "x_sl": np.ascontiguousarray(x[TPC * c:TPC * (c + 1)]),
            "wq": np.ascontiguousarray(wqkvz[:, 1536 * c:1536 * (c + 1)]),
            "wba": np.ascontiguousarray(wba[:, bac]),
            "convw": np.ascontiguousarray(convw[cwrows]),
            "dtba": np.ascontiguousarray(np.stack([dtb[4 * c:4 * c + 4],
                                                   alog[4 * c:4 * c + 4]])),
            "normw": np.ascontiguousarray(normw),
            "wout": np.ascontiguousarray(wout),
            "pkg": np.ascontiguousarray(gT[:, NSL * c + idx]),
            "pku": np.ascontiguousarray(uT[:, NSL * c + idx]),
            "pkd": np.ascontiguousarray(dT[1024 * c:1024 * (c + 1)]),
            "scg": np.ascontiguousarray(sgT[:, NSL * c + idx]),
            "scu": np.ascontiguousarray(suT[:, NSL * c + idx]),
            "scd": np.ascontiguousarray(sdT[64 * c:64 * (c + 1)]),
            "identb": ident,
            "maskS": maskS,
            "maskI": maskI,
        }
        in_maps.append(im)
    return in_maps


_NC_CACHE = {}


def kernel(**inputs):
    debug = inputs.pop("_debug", ())
    key = tuple(sorted(debug))
    if key not in _NC_CACHE:
        _NC_CACHE[key] = build_program(debug=key)
    nc = _NC_CACHE[key]
    in_maps = _host_prep(inputs)
    res = run_bass_kernel_spmd(nc, in_maps, core_ids=list(range(NCORES)))
    out = np.concatenate([res.results[c]["out"] for c in range(NCORES)], axis=0)
    full = out.reshape(B, S, D).astype(np.float32)
    if debug:
        return full, res
    return full
